# version 06
# baseline (speedup 1.0000x reference)
import numpy as np
import concourse.bass as bass
import concourse.mybir as mybir
from concourse.bass_utils import run_bass_kernel_spmd

F32 = mybir.dt.float32
BF16 = mybir.dt.bfloat16
AF = mybir.ActivationFunctionType
ALU = mybir.AluOpType
AX = mybir.AxisListType

B = 4
T = 2048
C = 4096
HC = 16
NT = T // 128
NCC = C // 128
LN_EPS = 1e-5
RMS_EPS = 1.1920929e-07
SINK_ITERS = 50
SC_MAIN = float(np.float32(1.0 / np.sqrt(np.float32(128.0))))
SC_ADPT = 1.0 / 64.0
CB = 98304


def _consts():
    theta = 1.0 / (10000.0 ** (np.arange(0, 128, 2, dtype=np.float64) / 128.0))
    idx = np.outer(np.arange(T, dtype=np.float64), theta)
    cos2 = np.cos(idx).astype(np.float32)
    sin2 = np.sin(idx).astype(np.float32)
    cosT = np.ascontiguousarray(np.concatenate([cos2, cos2], 1).T)
    sinT = np.concatenate([sin2, sin2], 1).T.copy()
    import ml_dtypes
    kl = np.arange(128)[:, None]
    ql = np.arange(256)[None, :]
    trapA = (kl <= ql).astype(ml_dtypes.bfloat16)
    trapB = ((kl + 128) <= ql).astype(ml_dtypes.bfloat16)
    return (cosT, np.ascontiguousarray(sinT), trapA, trapB,
            np.eye(128, dtype=ml_dtypes.bfloat16),
            np.eye(128, dtype=np.float32))


class Prog:

    def __init__(self, nphase=99, dump=None):
        self.nphase = nphase
        self.ph3_stop = 99
        self.dbg_ybr = False
        self.dbg_hfin = False
        self.dump = dump
        nc = bass.Bass("TRN2", target_bir_lowering=False,
                       detect_race_conditions=False)
        self.nc = nc
        d = nc.dram_tensor
        self.x_full = d("x_full", [T, C], F32, kind="ExternalInput")
        self.xc = d("xc", [T, 2048], F32, kind="ExternalInput")
        self.wqkvT_t = d("wqkvT_t", [48, 128, NCC, 128], F32, kind="ExternalInput")
        self.wprojT_t = d("wprojT_t", [2, 128, NCC, 1024], F32, kind="ExternalInput")
        self.wbranch_t = d("wbranch_t", [128, 8, 1024], F32, kind="ExternalInput")
        self.wffnT_t = d("wffnT_t", [128, NCC, 24], F32, kind="ExternalInput")
        self.wteT_t = d("wteT_t", [128, NCC, 10], F32, kind="ExternalInput")
        self.hbase = d("hbase", [1, 24], F32, kind="ExternalInput")
        self.eyeP = d("eyeP", [1, 16], F32, kind="ExternalInput")
        self.gating = d("gating", [1, 1], F32, kind="ExternalInput")
        self.gate = d("gate", [1, 1], F32, kind="ExternalInput")
        self.out = d("out", [T, 2048], F32, kind="ExternalOutput")
        self.dbg = None
        self.xn_dram = d("xn_dram", [T, C], BF16)
        self.qkT_dram = d("qkT_dram", [48, 128, T], BF16)
        self.xpre_dram = d("xpre_dram", [T, 1024], BF16)
        self.y_dram = d("y_dram", [T, 2048], BF16)
        self.sh_y = d("sh_y", [2, T, 2048], BF16, addr_space="Shared")
        self.sh_f = d("sh_f", [2, 64], mybir.dt.int32, addr_space="Shared")
        self.nonce = d("nonce", [1, 64], mybir.dt.int32, kind="ExternalInput")
        self.g_dram = d("g_dram", [T, 2048], F32)
        self.akav_dram = d("akav_dram", [2, HC, 128, 10], BF16)
        cosT, sinNT, trapA, trapB, eye_bf, eye_f32 = _consts()
        self.cosT_d = nc.inline_tensor(cosT, "cosT_d")
        self.sinP_d = nc.inline_tensor(sinNT, "sinP_d")
        self.trapA_d = nc.inline_tensor(trapA, "trapA_d")
        self.trapB_d = nc.inline_tensor(trapB, "trapB_d")
        self.eyebf_d = nc.inline_tensor(eye_bf, "eyebf_d")
        self.eyef32_d = nc.inline_tensor(eye_f32, "eyef32_d")

        A = nc.alloc_sbuf_tensor
        self.trapA = A("trapA", [128, 256], BF16)
        self.trapB = A("trapB", [128, 256], BF16)
        self.eye_bf = A("eye_bf", [128, 128], BF16)
        self.eye_f32 = A("eye_f32", [128, 128], F32)
        self.hb_b = A("hb_b", [128, 24], F32)
        self.hb10 = A("hb10", [128, 16], F32)
        self.eyeP_b = A("eyeP_b", [128, 16], F32)
        self.gating_b = A("gating_b", [128, 1], F32)
        self.gate_b = A("gate_b", [128, 1], F32)
        self.gs = A("gs", [128, 1], F32)
        self.rinv_all = A("rinv_all", [128, NT], F32)
        self.rinv10 = A("rinv10", [128, NT], F32)
        self.logits_tm = A("logits_tm", [128, NT, 24], F32)
        self.akT = A("akT", [128, HC, 10], BF16)
        self.avT = A("avT", [128, HC, 10], BF16)
        self.av_sb = A("av_sb", [16, HC, 129], BF16)
        self.wffn_sb = A("wffn_sb", [128, NCC, 24], BF16)
        self.wte_sb = A("wte_sb", [128, NCC, 10], BF16)
        self.s_tiny = A("s_tiny", [128, 16], F32)
        self.ntl = A("ntl", [1, 64], mybir.dt.int32)
        self.ptl = A("ptl", [1, 64], mybir.dt.int32)

        self.arena = A("arena", [128, 159744], mybir.dt.uint8)
        base = nc.lookup_mloc(self.arena).addr
        self._b = base
        P = self._place
        self.sqscr = P("sqscr", 0, [128, 4096], BF16)
        self.xnT = P("xnT", 0, [128, NCC, 1024], BF16)
        self.y_br = P("y_br", 0, [128, NT, 1024], F32)
        self.wp_sb = P("wp_sb", 0, [128, NCC, 1024], BF16)
        self.w_col = P("w_col", 65536, [128, 4, NCC, 128], BF16)
        self.xpreT = P("xpreT", 65536, [128, 8, T], BF16)
        self.yT_sb = P("yT_sb", 65536, [128, 2, NCC, 256], BF16)
        self.gx_sb = P("gx_sb", CB, [128, 2, 2, 2, 512], F32)
        self.t6scr = P("t6scr", CB + 16384, [128, 512], F32)
        self.x_sb = P("x_sb", CB, [128, 2, C], F32)
        self.xn1_sb = P("xn1_sb", CB + 32768, [128, 2, C], BF16)
        self.lgT = P("lgT", CB, [24, T], F32)
        self.cosT = P("cosT", CB + 8192, [128, T], F32)
        self.sinP = P("sinP", CB + 16384, [128, T], F32)
        self.cscr = P("cscr", CB + 24576, [128, 512], F32)
        self.s1scr = P("s1scr", CB + 26624, [128, 512], F32)
        self.ropeE = P("ropeE", CB + 28672, [128, 1024], BF16)
        self.ropeO = P("ropeO", CB + 30720, [128, 1024], BF16)
        self.akmix = P("akmix", CB + 32768 - 128, [128, 2, 10], BF16)
        self.Z = P("Z", CB, [128, NT * 16], F32)
        self.T0 = P("T0", CB + 1024, [128, NT * 16], F32)
        self.E = P("E", CB + 2048, [128, NT * 16], F32)
        self.Su = P("Su", CB + 3072, [128, NT * 4], F32)
        self.lnSu = P("lnSu", CB + 3328, [128, NT * 4], F32)
        self.Sv = P("Sv", CB + 3584, [128, NT * 4], F32)
        self.lnSv = P("lnSv", CB + 3840, [128, NT * 4], F32)
        self.LG = P("LG", CB + 4096, [128, NT * 8], F32)
        self.E2 = P("E2", CB + 4608, [128, NT * 8], F32)
        self.S2 = P("S2", CB + 5120, [128, NT * 2], F32)
        self.R2 = P("R2", CB + 5248, [128, NT * 2], F32)
        self.HP = P("HP", CB + 5376, [128, NT * 8], F32)
        self.Hfin = P("Hfin", CB + 5888, [128, NT * 16], F32)
        self.Mu = P("Mu", CB + 6912, [128, NT * 4], F32)
        self.Mv = P("Mv", CB + 7168, [128, NT * 4], F32)
        self.lnR = P("lnR", CB + 7424, [128, NT * 4], F32)
        self.M2 = P("M2", CB + 7680, [128, NT * 2], F32)
        self.xpre_sb = P("xpre_sb", CB + 8192, [128, 2, 1024], BF16)
        self.wb_sb = P("wb_sb", CB + 12288, [128, 8, 1024], BF16)
        self.xn3_sb = P("xn3_sb", CB + 28672, [128, 2, C], BF16)
        self.acc_sb = P("acc_sb", CB + 45056, [128, 2048], F32)
        self.g_sb = P("g_sb", CB + 53248, [128, 2048], F32)
        self.kT_sb = P("kT_sb", 65536, [128, 2, T], BF16)
        self.qT_sb = P("qT_sb", 73728, [128, 2, T], BF16)
        self.v_sb = P("v_sb", 81920, [128, 2, 16, 144], BF16)
        self.pT_sb = P("pT_sb", 91136, [128, 2, 256], BF16)
        self.ea_sb = P("ea_sb", 92160, [128, 160], F32)
        self.asum_sb = P("asum_sb", 92800, [128, 16], F32)
        self.gra_sb = P("gra_sb", 92864, [128, 16], F32)
        self.ap_sb = P("ap_sb", 92928, [128, 20], BF16)
        self.apT_sb = P("apT_sb", 92992, [16, 2, 256], BF16)
        self.ysb = P("ysb", 94016, [128, 2, 2, 128], BF16)
        self.ascr = P("ascr", 95040, [128, 8], F32)

        self.PS = [nc.alloc_psum_tensor(f"PS{i}", [128, 512], F32) for i in range(7)]
        self.PS7 = nc.alloc_psum_tensor("PS7", [128, 1024], BF16)

        self.sems = {}
        for name in ["sLD", "sSP", "sPE", "sAE", "sVE", "sOUT", "sCC", "sXG"]:
            self.sems[name] = nc.alloc_semaphore(name)
        self.sem_poll = nc.alloc_semaphore("sPOLL")
        self.sem_vc = nc.alloc_semaphore("sVC")
        self._vc = 0
        self.sem_ac = nc.alloc_semaphore("sAC")
        self._ac = 0
        self.sem_bar = nc.alloc_semaphore("sBAR")
        self._nbar = 0
        self.M = {}
        self.cnt = {k: 0 for k in self.sems}
        self.mode = "plan"

    def _place(self, name, off, shape, dtype):
        nbytes = int(np.prod(shape[1:])) * mybir.dt.size(dtype)
        assert off + nbytes <= 159744, (name, off, nbytes)
        return self.nc.alloc_sbuf_tensor_at(name, shape, dtype,
                                            offset=self._b + off)

    def inc(self, inst, sem, key, n=1):
        if self.mode == "plan":
            self.cnt[sem] += n
            assert key not in self.M, key
            self.M[key] = self.cnt[sem]
        else:
            assert inst is not None, f"milestone {key} has no instruction"
            inst.then_inc(self.sems[sem], n)
        return inst

    def V(self, eng, inst):
        if self.mode == "vector":
            self._vc += 1
            inst.then_inc(self.sem_vc, 1)
            eng.wait_ge(self.sem_vc, self._vc)
        return inst

    def SA(self, eng, inst):
        if self.mode == "scalar":
            self._ac += 1
            inst.then_inc(self.sem_ac, 1)
            eng.wait_ge(self.sem_ac, self._ac)
        return inst

    def VM(self, eng, key):
        if self.mode == "vector":
            eng.wait_ge(self.sems["sVE"], self.M[key])

    def wait(self, eng, sem, key):
        if self.mode == "plan":
            return
        eng.wait_ge(self.sems[sem], self.M[key])

    def on(self, which):
        return self.mode == which

    def build(self):
        with self.nc.Block() as block:
            self.mode = "plan"
            self._walk(None)
            for mode in ["sync", "tensor", "scalar", "vector", "gpsimd"]:
                self.mode = mode
                getattr(block, mode)(lambda eng: self._walk(eng))
        return self.nc

    def _walk(self, eng):
        phases = [self.ph0_consts, self.ph1_ln, self.ph2_qkv, self.ph3_ffn,
                  self.ph4_attn, self.ph5_gather, self.ph6_proj]
        for i, fn in enumerate(phases):
            if i > self.nphase:
                break
            fn(eng)
            if i < len(phases) - 1:
                self.barrier(eng, i)
        if self.dump is not None:
            self.ph_dump(eng)

    def ph_dump(self, eng):
        src = getattr(self, self.dump)
        n = int(np.prod(src.shape))
        if self.dbg is None:
            self.dbg = self.nc.dram_tensor("dbg", [1, n], src.dtype,
                                           kind="ExternalOutput")
        if self.on("gpsimd") or self.on("plan"):
            inst = None
            if self.on("gpsimd"):
                inst = eng.dma_start(out=self.dbg[0, :], in_=src.ap().flatten())
            self.inc(inst, "sSP", "dumpd", 16)
            self.wait(eng, "sSP", "dumpd")

    def barrier(self, eng, i):
        marks = {"sync": "sLD", "tensor": "sPE", "scalar": "sAE",
                 "vector": "sVE", "gpsimd": "sSP"}
        if self.mode == "plan":
            self._nbar += 1
            for e, sm in marks.items():
                self.M[f"bar{i}_{e}"] = self.cnt[sm]
            self.M[f"bar{i}"] = 5 * self._nbar
            return
        eng.wait_ge(self.sems[marks[self.mode]], self.M[f"bar{i}_{self.mode}"])
        eng.nop().then_inc(self.sem_bar, 1)
        eng.wait_ge(self.sem_bar, self.M[f"bar{i}"])

    def ph0_consts(self, eng):
        if self.on("sync") or self.on("plan"):
            loads = [
                (self.trapA[:, :], self.trapA_d[:, :]),
                (self.trapB[:, :], self.trapB_d[:, :]),
                (self.eye_bf[:, :], self.eyebf_d[:, :]),
                (self.eye_f32[:, :], self.eyef32_d[:, :]),
                (self.hb_b[:, :], bass.AP(self.hbase, 0, [[0, 128], [1, 24]])),
                (self.eyeP_b[:, :], bass.AP(self.eyeP, 0, [[0, 128], [1, 16]])),
                (self.gating_b[:, :], bass.AP(self.gating, 0, [[0, 128], [1, 1]])),
                (self.gate_b[:, :], bass.AP(self.gate, 0, [[0, 128], [1, 1]])),
            ]
            for i, (dst, src) in enumerate(loads):
                inst = eng.dma_start(out=dst, in_=src) if self.on("sync") else None
                self.inc(inst, "sLD", f"c{i}", 16)
        if self.on("gpsimd") or self.on("plan"):
            casts = [(self.wffn_sb[:, :, :], self.wffnT_t[:, :, :]),
                     (self.wte_sb[:, :, :], self.wteT_t[:, :, :]),
                     (self.ntl[:, :], self.nonce[:, :])]
            for i, (dst, src) in enumerate(casts):
                inst = eng.dma_start(out=dst, in_=src) if self.on("gpsimd") else None
                self.inc(inst, "sSP", f"cw{i}", 16)
        if self.on("scalar") or self.on("plan"):
            self.wait(eng, "sLD", "c7")
            inst = None
            if self.on("scalar"):
                inst = eng.activation(self.s_tiny[:, 0:1], self.gate_b[:, :],
                                      AF.Exp, scale=np.float32(-1.0))
            self.inc(inst, "sAE", "gsx")
        if self.on("vector") or self.on("plan"):
            self.wait(eng, "sAE", "gsx")
            self.wait(eng, "sLD", "c4")
            inst = None
            if self.on("vector"):
                self.V(eng, eng.tensor_scalar_add(self.s_tiny[:, 1:2],
                                                  self.s_tiny[:, 0:1], 1.0))
                eng.reciprocal(self.gs[:, :], self.s_tiny[:, 1:2])
                inst = eng.tensor_scalar_mul(self.hb10[:, :], self.hb_b[:, 0:16], 10.0)
            self.inc(inst, "sVE", "hb10")

    def ph1_ln(self, eng):
        inv_c = 1.0 / C
        for tp in range(NT // 2):
            uu = (2 * tp, 2 * tp + 1)
            if self.on("sync") or self.on("plan"):
                for tt in uu:
                    bi = tt % 2
                    if tt >= 2:
                        self.wait(eng, "sVE", f"xn{tt-2}")
                        self.wait(eng, "sAE", f"rsq{tt-2}")
                    inst = eng.dma_start(out=self.x_sb[:, bi, :],
                                         in_=self.x_full[tt * 128:(tt + 1) * 128, :]) if self.on("sync") else None
                    self.inc(inst, "sLD", f"x{tt}", 16)
            if self.on("vector") or self.on("plan"):
                st = self.s_tiny
                for tt in uu:
                    bi = tt % 2; o = 8 * bi
                    self.wait(eng, "sLD", f"x{tt}")
                    if self.on("vector"):
                        self.V(eng, eng.reduce_sum(st[:, o:o+1],
                                                   self.x_sb[:, bi, :], AX.X))
                for tt in uu:
                    bi = tt % 2; o = 8 * bi
                    self.wait(eng, "sAE", f"sq{tt}")
                    inst = None
                    if self.on("vector"):
                        self.V(eng, eng.tensor_scalar_mul(st[:, o+2:o+3],
                                                          st[:, o:o+1], inv_c))
                        self.V(eng, eng.tensor_mul(st[:, o+3:o+4], st[:, o+2:o+3],
                                                   st[:, o+2:o+3]))
                        self.V(eng, eng.tensor_scalar_mul(st[:, o+4:o+5],
                                                          st[:, o+1:o+2], inv_c))
                        inst = eng.tensor_sub(st[:, o+5:o+6], st[:, o+4:o+5],
                                              st[:, o+3:o+4])
                    self.inc(inst, "sVE", f"var{tt}")
                for tt in uu:
                    bi = tt % 2; o = 8 * bi
                    self.wait(eng, "sAE", f"rstd{tt}")
                    if tt >= 2:
                        self.wait(eng, "sSP", f"xnsp{tt-2}")
                    inst = None
                    if self.on("vector"):
                        inst = eng.tensor_scalar(self.xn1_sb[:, bi, :],
                                                 self.x_sb[:, bi, :],
                                                 st[:, o+2:o+3], st[:, o+7:o+8],
                                                 ALU.subtract, ALU.mult)
                    self.inc(inst, "sVE", f"xn{tt}")
            if self.on("scalar") or self.on("plan"):
                st = self.s_tiny
                for tt in uu:
                    bi = tt % 2; o = 8 * bi
                    self.wait(eng, "sLD", f"x{tt}")
                    inst = None
                    if self.on("scalar"):
                        inst = eng.activation(self.sqscr[:, :], self.x_sb[:, bi, :],
                                              AF.Square, accum_out=st[:, o+1:o+2])
                    self.inc(inst, "sAE", f"sq{tt}")
                for tt in uu:
                    bi = tt % 2; o = 8 * bi
                    self.wait(eng, "sVE", f"var{tt}")
                    inst = None
                    if self.on("scalar"):
                        self.SA(eng, eng.activation(st[:, o+6:o+7], st[:, o+5:o+6],
                                                    AF.Ln, bias=np.float32(LN_EPS)))
                        inst = eng.activation(st[:, o+7:o+8], st[:, o+6:o+7],
                                              AF.Exp, scale=np.float32(-0.5))
                    self.inc(inst, "sAE", f"rstd{tt}")
                for tt in uu:
                    bi = tt % 2; o = 8 * bi
                    self.wait(eng, "sVE", f"xn{tt}")
                    inst = None
                    if self.on("scalar"):
                        inst = eng.activation(self.sqscr[:, :], self.xn1_sb[:, bi, :],
                                              AF.Square, accum_out=st[:, o+1:o+2])
                    self.inc(inst, "sAE", f"rsq{tt}")
                for tt in uu:
                    bi = tt % 2; o = 8 * bi
                    if self.on("scalar"):
                        eng.wait_ge(self.sems["sAE"], self.M[f"rsq{tt}"])
                    inst = None
                    if self.on("scalar"):
                        self.SA(eng, eng.activation(st[:, o+6:o+7], st[:, o+1:o+2],
                                                    AF.Ln, scale=np.float32(inv_c),
                                                    bias=np.float32(RMS_EPS)))
                        inst = eng.activation(self.rinv_all[:, tt:tt + 1],
                                              st[:, o+6:o+7], AF.Exp,
                                              scale=np.float32(-0.5))
                    self.inc(inst, "sAE", f"rinv{tt}")
            if self.on("gpsimd") or self.on("plan"):
                for tt in uu:
                    bi = tt % 2
                    self.wait(eng, "sVE", f"xn{tt}")
                    inst = eng.dma_start(out=self.xn_dram[tt * 128:(tt + 1) * 128, :],
                                         in_=self.xn1_sb[:, bi, :]) if self.on("gpsimd") else None
                    self.inc(inst, "sSP", f"xnsp{tt}", 16)
        if self.on("vector") or self.on("plan"):
            self.wait(eng, "sAE", f"rinv{NT-1}")
            inst = None
            if self.on("vector"):
                inst = eng.tensor_scalar_mul(self.rinv10[:, :], self.rinv_all[:, :],
                                             10.0)
            self.inc(inst, "sVE", "rinvten")

    def ph2_qkv(self, eng):
        if self.on("sync") or self.on("plan"):
            for i, (dst, src) in enumerate([(self.cosT, self.cosT_d),
                                            (self.sinP, self.sinP_d)]):
                inst = eng.dma_start(out=dst[:, :], in_=src[:, :]) if self.on("sync") else None
                self.inc(inst, "sLD", f"rt{i}", 16)
        bprev = {}
        rprev = {}
        akprev = None
        for tp in range(2):
            if self.on("sync") or self.on("plan"):
                if tp == 1:
                    self.wait(eng, "sPE", "lg0")
                for cc in range(NCC):
                    src = self.xn_dram[tp * 1024:(tp + 1) * 1024,
                                       cc * 128:(cc + 1) * 128]
                    inst = None
                    if self.on("sync"):
                        inst = eng.dma_start(out=self.xnT[:, cc, :], in_=src,
                                             transpose=True)
                    self.inc(inst, "sLD", f"xnT{tp}_{cc}", 16)
            for p in range(16):
                is_k = p >= 8
                sa, sb = (0, 1) if p % 2 == 0 else (2, 3)
                if self.on("gpsimd") or self.on("plan"):
                    if p >= 2:
                        self.wait(eng, "sPE", f"qk{tp}_{p-2}_1")
                    elif tp == 1:
                        self.wait(eng, "sPE", "lg0")
                    for z, slot in ((0, sa), (1, sb)):
                        inst = None
                        if self.on("gpsimd"):
                            inst = eng.dma_start(out=self.w_col[:, slot, :, :],
                                                 in_=self.wqkvT_t[2 * p + z, :, :, :])
                        self.inc(inst, "sSP", f"wc{tp}_{p}_{z}", 16)
                for win in range(2):
                    pse = self.PS[2 * win]
                    pso = self.PS[2 * win + 1]
                    key = f"qk{tp}_{p}_{win}"
                    rider = is_k and tp == 0 and win == 0
                    if self.on("tensor") or self.on("plan"):
                        self.wait(eng, "sSP", f"wc{tp}_{p}_{1}")
                        self.wait(eng, "sLD", f"xnT{tp}_{NCC-1}")
                        tag = f"b{win}"
                        if tag in bprev:
                            self.wait(eng, bprev[tag][0], bprev[tag][1])
                        bprev[tag] = ("sVE", f"rp{key}")
                        if rider:
                            for tg in ("r4", "r5"):
                                if tg in bprev:
                                    self.wait(eng, bprev[tg][0], bprev[tg][1])
                                bprev[tg] = ("sAE", f"akm{tp}_{p}")
                        inst = None
                        for c in range(NCC):
                            if self.on("tensor"):
                                eng.matmul(pse[:, :], self.w_col[:, sa, c, :],
                                           self.xnT[:, c, win * 512:(win + 1) * 512],
                                           start=(c == 0), stop=(c == NCC - 1))
                                inst = eng.matmul(
                                    pso[:, :], self.w_col[:, sb, c, :],
                                    self.xnT[:, c, win * 512:(win + 1) * 512],
                                    start=(c == 0), stop=(c == NCC - 1))
                                if rider:
                                    eng.matmul(self.PS[4][:, 0:10],
                                               self.w_col[:, sa, c, :],
                                               self.wte_sb[:, c, :],
                                               start=(c == 0), stop=(c == NCC - 1))
                                    inst = eng.matmul(self.PS[5][:, 0:10],
                                                      self.w_col[:, sb, c, :],
                                                      self.wte_sb[:, c, :],
                                                      start=(c == 0),
                                                      stop=(c == NCC - 1))
                        self.inc(inst, "sPE", key)
                    if self.on("vector") or self.on("plan"):
                        self.wait(eng, "sPE", key)
                        if win == 0 and tp == 0 and p == 0:
                            self.wait(eng, "sLD", "rt1")
                        for tag in ("rE", "rO"):
                            if win == 0 and tag in rprev:
                                self.wait(eng, "sLD", rprev[tag])
                        half = slice(tp * 1024 + win * 512,
                                     tp * 1024 + (win + 1) * 512)
                        wsl = slice(win * 512, (win + 1) * 512)
                        inst = None
                        if self.on("vector"):
                            self.V(eng, eng.tensor_mul(self.cscr[:, :], pse[:, :],
                                                       self.cosT[:, half]))
                            self.V(eng, eng.tensor_mul(self.s1scr[:, :], pso[:, :],
                                                       self.sinP[:, half]))
                            self.V(eng, eng.tensor_sub(self.ropeE[:, wsl],
                                                       self.cscr[:, :],
                                                       self.s1scr[:, :]))
                            self.V(eng, eng.tensor_mul(self.cscr[:, :], pso[:, :],
                                                       self.cosT[:, half]))
                            self.V(eng, eng.tensor_mul(self.s1scr[:, :], pse[:, :],
                                                       self.sinP[:, half]))
                            inst = eng.tensor_add(self.ropeO[:, wsl],
                                                  self.cscr[:, :], self.s1scr[:, :])
                        self.inc(inst, "sVE", f"rp{key}")
                        self.VM(eng, f"rp{key}")
                if is_k and tp == 0:
                    if self.on("scalar") or self.on("plan"):
                        self.wait(eng, "sPE", f"qk{tp}_{p}_0")
                        if akprev is not None:
                            self.wait(eng, "sLD", akprev)
                        inst = None
                        if self.on("scalar"):
                            eng.activation(self.akmix[:, 0, :], self.PS[4][:, 0:10],
                                           AF.Copy)
                            inst = eng.activation(self.akmix[:, 1, :],
                                                  self.PS[5][:, 0:10], AF.Copy)
                        self.inc(inst, "sAE", f"akm{tp}_{p}")
                    if self.on("sync") or self.on("plan"):
                        self.wait(eng, "sAE", f"akm{tp}_{p}")
                        hA = 2 * (p - 8)
                        for z in range(2):
                            dst = bass.AP(
                                self.akav_dram,
                                (0 * HC + hA) * 1280 + z * 640,
                                [[1280, 2], [10, 64], [1, 10]])
                            inst = None
                            if self.on("sync"):
                                inst = eng.dma_start(out=dst,
                                                     in_=self.akmix[:, z, :])
                            self.inc(inst, "sLD", f"aks{tp}_{p}_{z}", 16)
                        akprev = f"aks{tp}_{p}_1"
                if self.on("sync") or self.on("plan"):
                    self.wait(eng, "sVE", f"rpqk{tp}_{p}_1")
                    hA = 2 * (p % 8) + (16 if is_k else 0)
                    half = slice(tp * 1024, (tp + 1) * 1024)
                    outs = [
                        (self.qkT_dram[hA, 0:64, half], self.ropeE[0:64, :]),
                        (self.qkT_dram[hA + 1, 0:64, half], self.ropeE[64:128, :]),
                        (self.qkT_dram[hA, 64:128, half], self.ropeO[0:64, :]),
                        (self.qkT_dram[hA + 1, 64:128, half], self.ropeO[64:128, :]),
                    ]
                    for z, (dst, srcb) in enumerate(outs):
                        inst = None
                        if self.on("sync"):
                            inst = eng.dma_start(out=dst, in_=srcb)
                        self.inc(inst, "sLD", f"spl{tp}_{p}_{z}", 16)
                    rprev["rE"] = f"spl{tp}_{p}_1"
                    rprev["rO"] = f"spl{tp}_{p}_3"
            for oc in range(32, 48):
                h = oc - 32
                sa, sb = (0, 1) if oc % 2 == 0 else (2, 3)
                rider = tp == 0
                key = f"v{tp}_{oc}"
                if self.on("gpsimd") or self.on("plan"):
                    pk = (f"v{tp}_{oc-2}" if oc >= 34 else f"qk{tp}_{14+(oc-32)}_1")
                    self.wait(eng, "sPE", pk)
                    inst = None
                    if self.on("gpsimd"):
                        inst = eng.dma_start(out=self.w_col[:, sa, :, :],
                                             in_=self.wqkvT_t[oc, :, :, :])
                    self.inc(inst, "sSP", f"wcv{tp}_{oc}", 16)
                if self.on("tensor") or self.on("plan"):
                    self.wait(eng, "sSP", f"wcv{tp}_{oc}")
                    psa, psb = self.PS[sa], self.PS[sb]
                    tag = "b0" if sa == 0 else "b1"
                    if tag in bprev:
                        self.wait(eng, bprev[tag][0], bprev[tag][1])
                    bprev[tag] = ("sAE", f"ev{key}")
                    if rider:
                        tg = "r4" if oc % 2 == 0 else "r5"
                        if tg in bprev:
                            self.wait(eng, bprev[tg][0], bprev[tg][1])
                        bprev[tg] = ("sAE", f"akmv{tp}_{oc}")
                    inst = None
                    for c in range(NCC):
                        if self.on("tensor"):
                            eng.matmul(psa[:, :], self.w_col[:, sa, c, :],
                                       self.xnT[:, c, 0:512],
                                       start=(c == 0), stop=(c == NCC - 1))
                            inst = eng.matmul(psb[:, :], self.w_col[:, sa, c, :],
                                              self.xnT[:, c, 512:1024],
                                              start=(c == 0), stop=(c == NCC - 1))
                            if rider:
                                inst = eng.matmul(
                                    self.PS[4 + (oc % 2)][:, 0:10],
                                    self.w_col[:, sa, c, :], self.wte_sb[:, c, :],
                                    start=(c == 0), stop=(c == NCC - 1))
                    self.inc(inst, "sPE", key)
                if self.on("scalar") or self.on("plan"):
                    self.wait(eng, "sPE", key)
                    if "rE" in rprev:
                        self.wait(eng, "sLD", rprev["rE"])
                    inst = None
                    if self.on("scalar"):
                        eng.activation(self.ropeE[:, 0:512], self.PS[sa][:, :],
                                       AF.Copy)
                        inst = eng.activation(self.ropeE[:, 512:1024],
                                              self.PS[sb][:, :], AF.Copy)
                    self.inc(inst, "sAE", f"ev{key}")
                    if rider:
                        if akprev is not None:
                            self.wait(eng, "sLD", akprev)
                        inst = None
                        if self.on("scalar"):
                            inst = eng.activation(self.akmix[:, 0, :],
                                                  self.PS[4 + (oc % 2)][:, 0:10],
                                                  AF.Copy)
                        self.inc(inst, "sAE", f"akmv{tp}_{oc}")
                if self.on("sync") or self.on("plan"):
                    self.wait(eng, "sAE", f"ev{key}")
                    inst = None
                    if self.on("sync"):
                        inst = eng.dma_start(
                            out=self.qkT_dram[oc, :, tp * 1024:(tp + 1) * 1024],
                            in_=self.ropeE[:, :])
                    self.inc(inst, "sLD", f"vsp{key}", 16)
                    rprev["rE"] = f"vsp{key}"
                    if rider:
                        self.wait(eng, "sAE", f"akmv{tp}_{oc}")
                        inst = None
                        if self.on("sync"):
                            inst = eng.dma_start(
                                out=self.akav_dram[1, h, :, :],
                                in_=self.akmix[:, 0, :])
                        self.inc(inst, "sLD", f"avs{tp}_{oc}", 16)
                        akprev = f"avs{tp}_{oc}"
            if self.on("tensor") or self.on("plan"):
                self.wait(eng, "sLD", f"xnT{tp}_{NCC-1}")
                for tg in ("r4", "r5"):
                    if tg in bprev:
                        self.wait(eng, bprev[tg][0], bprev[tg][1])
                    bprev[tg] = ("sAE", f"evlg{tp}")
                inst = None
                for c in range(NCC):
                    if self.on("tensor"):
                        lhsT = self.wffn_sb[:, c, :]
                        eng.matmul(self.PS[4][0:24, :], lhsT,
                                   self.xnT[:, c, 0:512],
                                   start=(c == 0), stop=(c == NCC - 1))
                        inst = eng.matmul(self.PS[5][0:24, :], lhsT,
                                          self.xnT[:, c, 512:1024],
                                          start=(c == 0), stop=(c == NCC - 1))
                self.inc(inst, "sPE", f"lg{tp}")
            if self.on("scalar") or self.on("plan"):
                self.wait(eng, "sPE", f"lg{tp}")
                inst = None
                if self.on("scalar"):
                    eng.activation(self.lgT[0:24, tp * 1024:tp * 1024 + 512],
                                   self.PS[4][0:24, :], AF.Copy)
                    inst = eng.activation(
                        self.lgT[0:24, tp * 1024 + 512:(tp + 1) * 1024],
                        self.PS[5][0:24, :], AF.Copy)
                self.inc(inst, "sAE", f"evlg{tp}")
        for piece in range(NT):
            if self.on("tensor") or self.on("plan"):
                self.wait(eng, "sAE", "evlg1" if piece == 0 else f"lgte{piece-1}")
                inst = None
                if self.on("tensor"):
                    inst = eng.transpose(
                        self.PS[6][:, 0:24],
                        self.lgT[0:24, piece * 128:(piece + 1) * 128],
                        self.eye_f32[0:24, 0:24])
                self.inc(inst, "sPE", f"lgtr{piece}")
            if self.on("scalar") or self.on("plan"):
                self.wait(eng, "sPE", f"lgtr{piece}")
                inst = None
                if self.on("scalar"):
                    inst = eng.activation(self.logits_tm[:, piece, :],
                                          self.PS[6][:, 0:24], AF.Copy)
                self.inc(inst, "sAE", f"lgte{piece}")
        if self.on("sync") or self.on("plan"):
            self.wait(eng, "sLD", "avs0_47")
            inst = None
            if self.on("sync"):
                inst = eng.dma_start(
                    out=self.akT[:, :, :],
                    in_=bass.AP(self.akav_dram, 0, [[10, 128], [1280, HC], [1, 10]]))
            self.inc(inst, "sLD", "akl", 16)
            inst = None
            if self.on("sync"):
                inst = eng.dma_start(
                    out=self.avT[:, :, :],
                    in_=bass.AP(self.akav_dram, HC * 1280,
                                [[10, 128], [1280, HC], [1, 10]]))
            self.inc(inst, "sLD", "avl", 16)
        if self.on("vector") or self.on("plan"):
            inst = eng.memset(self.av_sb[:, :, :], 0.0) if self.on("vector") else None
            self.inc(inst, "sVE", "avz")
        for h in range(HC):
            if self.on("tensor") or self.on("plan"):
                self.wait(eng, "sLD", "avl")
                if h > 0:
                    self.wait(eng, "sAE", f"avev{h-1}")
                inst = None
                if self.on("tensor"):
                    inst = eng.transpose(self.PS7[0:10, 0:128],
                                         self.avT[:, h, :], self.eye_bf[:, :])
                self.inc(inst, "sPE", f"avtr{h}")
            if self.on("scalar") or self.on("plan"):
                self.wait(eng, "sPE", f"avtr{h}")
                self.wait(eng, "sVE", "avz")
                inst = None
                if self.on("scalar"):
                    inst = eng.activation(self.av_sb[0:10, h, 0:128],
                                          self.PS7[0:10, 0:128], AF.Copy)
                self.inc(inst, "sAE", f"avev{h}")

    def ph3_ffn(self, eng):
        v4 = [[256, 128], [16, NT], [4, 4], [1, 4]]
        if self.on("vector") or self.on("plan"):
            inst = None
            for tt in range(NT):
                if self.on("vector"):
                    eng.scalar_tensor_tensor(
                        self.Z[:, tt * 16:(tt + 1) * 16],
                        self.logits_tm[:, tt, 0:16], self.rinv10[:, tt:tt + 1],
                        self.hb10[:, :], ALU.mult, ALU.add)
                    inst = eng.scalar_tensor_tensor(
                        self.LG[:, tt * 8:(tt + 1) * 8],
                        self.logits_tm[:, tt, 16:24], self.rinv_all[:, tt:tt + 1],
                        self.hb_b[:, 16:24], ALU.mult, ALU.add)
            self.inc(inst, "sVE", "zlg")
            inst = eng.memset(self.lnSv[:, :], 0.0) if self.on("vector") else None
            self.inc(inst, "sVE", "lnsv0")
        if self.on("vector") or self.on("plan"):
            inst = None
            if self.on("vector"):
                self.V(eng, eng.reduce_max(
                    bass.AP(self.M2, 0, [[32, 128], [1, 32]]),
                    bass.AP(self.LG, 0, [[128, 128], [4, 32], [1, 4]]), AX.X))
                inst = eng.tensor_tensor(
                    bass.AP(self.LG, 0, [[128, 128], [4, 32], [1, 4]]),
                    bass.AP(self.LG, 0, [[128, 128], [4, 32], [1, 4]]),
                    bass.AP(self.M2, 0, [[32, 128], [1, 32], [0, 4]]),
                    ALU.subtract)
            self.inc(inst, "sVE", "lgsh")
        if self.on("scalar") or self.on("plan"):
            self.wait(eng, "sVE", "lgsh")
            inst = None
            if self.on("scalar"):
                inst = eng.activation(self.E2[:, :], self.LG[:, :], AF.Exp)
            self.inc(inst, "sAE", "e2")
        if self.on("vector") or self.on("plan"):
            self.wait(eng, "sAE", "e2")
            inst = None
            if self.on("vector"):
                self.V(eng, eng.reduce_sum(
                    bass.AP(self.S2, 0, [[32, 128], [1, 32]]),
                    bass.AP(self.E2, 0, [[128, 128], [4, 32], [1, 4]]), AX.X))
                self.V(eng, eng.reciprocal(self.R2[:, :], self.S2[:, :]))
                inst = eng.tensor_tensor(
                    bass.AP(self.HP, 0, [[128, 128], [4, 32], [1, 4]]),
                    bass.AP(self.E2, 0, [[128, 128], [4, 32], [1, 4]]),
                    bass.AP(self.R2, 0, [[32, 128], [1, 32], [0, 4]]),
                    ALU.mult)
            self.inc(inst, "sVE", "hp")
            self.VM(eng, "hp")
        if self.ph3_stop <= 1:
            return
        for tt in range(NT):
            bi = tt % 2
            rows = slice(tt * 128, (tt + 1) * 128)
            if self.on("sync") or self.on("plan"):
                if tt >= 2:
                    self.wait(eng, "sVE", f"xpre{tt-2}")
                inst = eng.dma_start(out=self.xn3_sb[:, bi, :],
                                     in_=self.xn_dram[rows, :]) if self.on("sync") else None
                self.inc(inst, "sLD", f"xn3a{tt}", 16)
            if self.on("vector") or self.on("plan"):
                self.wait(eng, "sLD", f"xn3a{tt}")
                if tt >= 2:
                    self.wait(eng, "sSP", f"xpsp{tt-2}")
                inst = None
                if self.on("vector"):
                    hp = self.HP
                    xn = self.xn3_sb[:, bi, :]
                    acc = self.acc_sb[:, 0:1024]
                    self.V(eng, eng.tensor_scalar_mul(acc, xn[:, 0:1024],
                                                      hp[:, tt * 8:tt * 8 + 1]))
                    for n in range(1, 4):
                        o = acc if n < 3 else self.xpre_sb[:, bi, :]
                        inst = eng.scalar_tensor_tensor(
                            o, xn[:, n * 1024:(n + 1) * 1024],
                            hp[:, tt * 8 + n:tt * 8 + n + 1],
                            acc, ALU.mult, ALU.add)
                        if n < 3:
                            self.V(eng, inst)
                self.inc(inst, "sVE", f"xpre{tt}")
                self.VM(eng, f"xpre{tt}")
            if self.on("gpsimd") or self.on("plan"):
                self.wait(eng, "sVE", f"xpre{tt}")
                inst = eng.dma_start(out=self.xpre_dram[rows, :],
                                     in_=self.xpre_sb[:, bi, :]) if self.on("gpsimd") else None
                self.inc(inst, "sSP", f"xpsp{tt}", 16)
        if self.on("gpsimd") or self.on("plan"):
            inst = eng.dma_start(out=self.wb_sb[:, :, :],
                                 in_=self.wbranch_t[:, :, :]) if self.on("gpsimd") else None
            self.inc(inst, "sSP", "wb", 16)
        if self.on("sync") or self.on("plan"):
            self.wait(eng, "sSP", f"xpsp{NT-1}")
            for cc in range(8):
                inst = None
                if self.on("sync"):
                    inst = eng.dma_start(
                        out=self.xpreT[:, cc, :],
                        in_=self.xpre_dram[:, cc * 128:(cc + 1) * 128],
                        transpose=True)
                self.inc(inst, "sLD", f"xpT{cc}", 16)
        if self.ph3_stop <= 2:
            return
        for it in range(SINK_ITERS):
            for half in range(2):
                M = self.Mu if half == 0 else self.Mv
                if self.on("vector") or self.on("plan"):
                    if it > 0 or half > 0:
                        self.wait(eng, "sAE",
                                  f"skl{it}_0" if half == 1 else f"skl{it-1}_1")
                    src = self.lnSv if half == 0 else self.lnSu
                    pat = ([[64, 128], [4, NT], [0, 4], [1, 4]] if half == 0
                           else [[64, 128], [4, NT], [1, 4], [0, 4]])
                    epat = (v4 if half == 0
                            else [[256, 128], [16, NT], [1, 4], [4, 4]])
                    gpat = [[64, 128], [4, NT], [1, 4], [0, 4]]
                    gout = [[64, 128], [4, NT], [1, 4]]
                    Madd = self.Mv if half == 0 else self.Mu
                    inst = None
                    if self.on("vector"):
                        if it > 0 or half > 0:
                            self.V(eng, eng.tensor_add(src[:, :], src[:, :],
                                                       Madd[:, :]))
                        self.V(eng, eng.tensor_tensor(bass.AP(self.T0, 0, v4),
                                                      bass.AP(self.Z, 0, v4),
                                                      bass.AP(src, 0, pat),
                                                      ALU.subtract))
                        self.V(eng, eng.reduce_max(bass.AP(M, 0, gout),
                                                   bass.AP(self.T0, 0, epat),
                                                   AX.X))
                        inst = eng.tensor_tensor(bass.AP(self.T0, 0, epat),
                                                 bass.AP(self.T0, 0, epat),
                                                 bass.AP(M, 0, gpat),
                                                 ALU.subtract)
                    self.inc(inst, "sVE", f"skt{it}_{half}")
                if self.on("scalar") or self.on("plan"):
                    self.wait(eng, "sVE", f"skt{it}_{half}")
                    inst = None
                    if self.on("scalar"):
                        inst = eng.activation(self.E[:, :], self.T0[:, :], AF.Exp)
                    self.inc(inst, "sAE", f"ske{it}_{half}")
                if self.on("vector") or self.on("plan"):
                    self.wait(eng, "sAE", f"ske{it}_{half}")
                    dst = self.Su if half == 0 else self.Sv
                    pat = (v4 if half == 0
                           else [[256, 128], [16, NT], [1, 4], [4, 4]])
                    inst = None
                    if self.on("vector"):
                        inst = eng.reduce_sum(
                            bass.AP(dst, 0, [[64, 128], [4, NT], [1, 4]]),
                            bass.AP(self.E, 0, pat), AX.X)
                    self.inc(inst, "sVE", f"sks{it}_{half}")
                if self.on("scalar") or self.on("plan"):
                    self.wait(eng, "sVE", f"sks{it}_{half}")
                    src = self.Su if half == 0 else self.Sv
                    dst = self.lnSu if half == 0 else self.lnSv
                    inst = None
                    if self.on("scalar"):
                        inst = eng.activation(dst[:, :], src[:, :], AF.Ln)
                    self.inc(inst, "sAE", f"skl{it}_{half}")
        if self.on("vector") or self.on("plan"):
            self.wait(eng, "sAE", f"skl{SINK_ITERS-1}_1")
            inst = None
            if self.on("vector"):
                self.V(eng, eng.tensor_add(self.lnSv[:, :], self.lnSv[:, :],
                                           self.Mv[:, :]))
                self.V(eng, eng.tensor_tensor(
                    bass.AP(self.T0, 0, v4), bass.AP(self.Z, 0, v4),
                    bass.AP(self.lnSu, 0, [[64, 128], [4, NT], [1, 4], [0, 4]]),
                    ALU.subtract))
                inst = eng.tensor_tensor(bass.AP(self.T0, 0, v4),
                                         bass.AP(self.T0, 0, v4),
                                         bass.AP(self.lnSv, 0,
                                                 [[64, 128], [4, NT], [0, 4], [1, 4]]),
                                         ALU.subtract)
            self.inc(inst, "sVE", "hraw")
        if self.on("scalar") or self.on("plan"):
            self.wait(eng, "sVE", "hraw")
            inst = None
            if self.on("scalar"):
                inst = eng.activation(self.E[:, :], self.T0[:, :], AF.Exp)
            self.inc(inst, "sAE", "hexp")
        if self.on("vector") or self.on("plan"):
            self.wait(eng, "sAE", "hexp")
            self.wait(eng, "sVE", "hb10")
            v3 = [[256, 128], [16, NT], [1, 16]]
            eyepat = [[16, 128], [0, NT], [1, 16]]
            inst = None
            if self.on("vector"):
                self.V(eng, eng.tensor_tensor(
                    bass.AP(self.T0, 0, v3), bass.AP(self.E, 0, v3),
                    bass.AP(self.eyeP_b, 0, eyepat), ALU.subtract))
                inst = eng.scalar_tensor_tensor(
                    bass.AP(self.Hfin, 0, v3), bass.AP(self.T0, 0, v3),
                    self.gs[:, 0:1], bass.AP(self.eyeP_b, 0, eyepat),
                    ALU.mult, ALU.add)
            self.inc(inst, "sVE", "hfin")
        if self.ph3_stop <= 3:
            return
        for tt in range(NT):
            pa, pb = (0, 1) if tt % 2 == 0 else (2, 3)
            if self.on("tensor") or self.on("plan"):
                self.wait(eng, "sLD", "xpT7")
                self.wait(eng, "sSP", "wb")
                if tt >= 2:
                    self.wait(eng, "sAE", f"ybre{tt-2}")
                inst = None
                for ccb in range(8):
                    if self.on("tensor"):
                        lhsT = self.xpreT[:, ccb, tt * 128:(tt + 1) * 128]
                        eng.matmul(self.PS[pa][:, :], lhsT,
                                   self.wb_sb[:, ccb, 0:512],
                                   start=(ccb == 0), stop=(ccb == 7))
                        inst = eng.matmul(self.PS[pb][:, :], lhsT,
                                          self.wb_sb[:, ccb, 512:1024],
                                          start=(ccb == 0), stop=(ccb == 7))
                self.inc(inst, "sPE", f"ybr{tt}")
            if self.on("scalar") or self.on("plan"):
                self.wait(eng, "sPE", f"ybr{tt}")
                inst = None
                if self.on("scalar"):
                    eng.activation(self.y_br[:, tt, 0:512], self.PS[pa][:, :],
                                   AF.Copy)
                    inst = eng.activation(self.y_br[:, tt, 512:1024],
                                          self.PS[pb][:, :], AF.Copy)
                self.inc(inst, "sAE", f"ybre{tt}")
        if self.dbg_ybr:
            for tt in range(NT):
                if self.on("gpsimd") or self.on("plan"):
                    self.wait(eng, "sAE", f"ybre{tt}")
                    inst = None
                    if self.on("gpsimd"):
                        inst = eng.dma_start(
                            out=self.g_dram[tt * 128:(tt + 1) * 128, 0:1024],
                            in_=self.y_br[:, tt, :])
                    self.inc(inst, "sSP", f"dbgyb{tt}", 16)
            return
        if self.dbg_hfin:
            if self.on("gpsimd") or self.on("plan"):
                self.wait(eng, "sVE", "hfin")
                self.wait(eng, "sVE", "hp")
                inst = None
                if self.on("gpsimd"):
                    eng.dma_start(out=self.g_dram[0:128, 0:256],
                                  in_=self.Hfin[:, :]).then_inc(self.sems["sSP"], 16)
                    eng.dma_start(out=self.g_dram[128:256, 0:128],
                                  in_=self.HP[:, :]).then_inc(self.sems["sSP"], 16)
                    eng.dma_start(out=self.g_dram[256:384, 0:256],
                                  in_=self.Z[:, :]).then_inc(self.sems["sSP"], 16)
                    eng.dma_start(out=self.g_dram[384:512, 0:64],
                                  in_=self.lnSu[:, :]).then_inc(self.sems["sSP"], 16)
                    inst = eng.dma_start(out=self.g_dram[512:640, 0:64],
                                         in_=self.lnSv[:, :])
                if self.mode == "plan":
                    self.cnt["sSP"] += 64
                self.inc(inst, "sSP", "dbghf", 16)
            return
        if self.ph3_stop <= 4:
            return
        for tt in range(NT):
            bi = tt % 2
            rows = slice(tt * 128, (tt + 1) * 128)
            if self.on("sync") or self.on("plan"):
                if tt >= 2:
                    self.wait(eng, "sVE", f"mix{tt-2}")
                inst = eng.dma_start(out=self.xn3_sb[:, bi, :],
                                     in_=self.xn_dram[rows, :]) if self.on("sync") else None
                self.inc(inst, "sLD", f"xn3b{tt}", 16)
            if self.on("vector") or self.on("plan"):
                self.wait(eng, "sLD", f"xn3b{tt}")
                self.wait(eng, "sVE", "hfin")
                self.wait(eng, "sAE", f"ybre{tt}")
                if tt >= 1:
                    self.wait(eng, "sAE", f"silu{tt-1}")
                inst = None
                if self.on("vector"):
                    xn = self.xn3_sb[:, bi, :]
                    for r in range(2):
                        acc = self.acc_sb[:, r * 1024:(r + 1) * 1024]
                        hf = self.Hfin
                        eng.tensor_scalar_mul(
                            acc, xn[:, 0:1024],
                            hf[:, tt * 16 + 4 * r:tt * 16 + 4 * r + 1])
                        for jj in range(1, 4):
                            eng.scalar_tensor_tensor(
                                acc, xn[:, jj * 1024:(jj + 1) * 1024],
                                hf[:, tt * 16 + 4 * r + jj:tt * 16 + 4 * r + jj + 1],
                                acc, ALU.mult, ALU.add)
                        inst = eng.scalar_tensor_tensor(
                            acc, self.y_br[:, tt, :],
                            self.HP[:, tt * 8 + 4 + r:tt * 8 + 4 + r + 1],
                            acc, ALU.mult, ALU.add)
                self.inc(inst, "sVE", f"mix{tt}")
            if self.on("scalar") or self.on("plan"):
                self.wait(eng, "sVE", f"mix{tt}")
                if tt >= 2:
                    self.wait(eng, "sSP", f"gsp{tt-2}")
                inst = None
                if self.on("scalar"):
                    inst = eng.activation(self.g_sb[:, :], self.acc_sb[:, :],
                                          AF.Silu)
                self.inc(inst, "sAE", f"silu{tt}")
            if self.on("gpsimd") or self.on("plan"):
                self.wait(eng, "sAE", f"silu{tt}")
                inst = eng.dma_start(out=self.g_dram[rows, :],
                                     in_=self.g_sb[:, :]) if self.on("gpsimd") else None
                self.inc(inst, "sSP", f"gsp{tt}", 16)

    def ph4_attn(self, eng):
        if self.on("gpsimd") or self.on("plan"):
            for q in range(4):
                inst = None
                if self.on("gpsimd"):
                    inst = eng.dma_start(
                        out=self.wp_sb[:, q * 8:(q + 1) * 8, :],
                        in_=self.wprojT_t[0, :, q * 8:(q + 1) * 8, :])
                self.inc(inst, "sSP", f"wp0_{q}", 16)
        for h in range(HC):
            hb = h % 2
            if self.on("sync") or self.on("plan"):
                if h >= 2:
                    self.wait(eng, "sPE", f"yadj{h-2}_7")
                inst = eng.dma_start(out=self.kT_sb[:, hb, :],
                                     in_=self.qkT_dram[16 + h, :, :]) if self.on("sync") else None
                self.inc(inst, "sLD", f"kT{h}", 16)
                inst = eng.dma_start(out=self.qT_sb[:, hb, :],
                                     in_=self.qkT_dram[h, :, :]) if self.on("sync") else None
                self.inc(inst, "sLD", f"qT{h}", 16)
                for kc in range(16):
                    inst = None
                    if self.on("sync"):
                        inst = eng.dma_start(
                            out=self.v_sb[:, hb, kc, 0:128],
                            in_=self.qkT_dram[32 + h, :, kc * 128:(kc + 1) * 128],
                            transpose=True)
                    self.inc(inst, "sLD", f"v{h}_{kc}", 16)
            if h < 2 and (self.on("vector") or self.on("plan")):
                inst = None
                if self.on("vector"):
                    inst = eng.memset(self.v_sb[:, hb, :, 128:129], 1.0)
                self.inc(inst, "sVE", f"vones{h}")
            if self.on("tensor") or self.on("plan"):
                if h > 0:
                    self.wait(eng, "sAE", f"ae{h-1}")
                if h == 0:
                    self.wait(eng, "sLD", "akl")
                self.wait(eng, "sLD", f"qT{h}")
                inst = None
                for u in range(16):
                    if self.on("tensor"):
                        inst = eng.matmul(self.PS[4][:, 10 * u:10 * u + 10],
                                          self.qT_sb[:, hb, u * 128:(u + 1) * 128],
                                          self.akT[:, h, :], start=True, stop=True,
                                          skip_group_check=True)
                self.inc(inst, "sPE", f"as{h}")
            if self.on("scalar") or self.on("plan"):
                self.wait(eng, "sPE", f"as{h}")
                if h > 0:
                    self.wait(eng, "sVE", f"apv{h-1}_7")
                inst = None
                for u in range(16):
                    if self.on("scalar"):
                        inst = eng.activation(self.ea_sb[:, 10 * u:10 * u + 10],
                                              self.PS[4][:, 10 * u:10 * u + 10],
                                              AF.Exp, scale=np.float32(SC_ADPT),
                                              accum_out=self.asum_sb[:, u:u + 1])
                self.inc(inst, "sAE", f"ae{h}")
            if self.on("vector") or self.on("plan"):
                self.wait(eng, "sAE", f"ae{h}")
                inst = None
                if self.on("vector"):
                    self.V(eng, eng.reciprocal(self.gra_sb[:, :], self.asum_sb[:, :]))
                    inst = eng.tensor_scalar_mul(self.gra_sb[:, :],
                                                 self.gra_sb[:, :],
                                                 self.gating_b[:, 0:1])
                self.inc(inst, "sVE", f"gra{h}")
                self.VM(eng, f"gra{h}")
            for s in range(8):
                nkc = 2 * s + 2
                ye, yo = (2, 3) if s % 2 == 0 else (5, 6)
                yprev = (f"yev{h}_{s-2}" if s >= 2 else
                         (f"yev{h-1}_{s+6}" if h > 0 else None))
                skey = f"{h}_{s}"
                for kc in range(nkc + 1):
                    pb = kc % 2
                    if kc < nkc:
                        key = f"{h}_{s}_{kc}"
                        if self.on("tensor") or self.on("plan"):
                            if s == 0 and kc == 0:
                                self.wait(eng, "sLD", f"v{h}_15")
                                self.wait(eng, "sVE", f"vones{min(h, 1)}")
                            prev = (f"pT{h}_{s}_{kc-2}" if kc >= 2 else
                                    (f"pT{h}_{s-1}_{2*s-2+pb}" if s > 0 else
                                     (f"pT{h-1}_7_{14+pb}" if h > 0 else None)))
                            if prev:
                                self.wait(eng, "sAE", prev)
                            inst = None
                            if self.on("tensor"):
                                inst = eng.matmul(
                                    self.PS[pb][:, 0:256],
                                    self.kT_sb[:, hb, kc * 128:(kc + 1) * 128],
                                    self.qT_sb[:, hb, s * 256:(s + 1) * 256],
                                    start=True, stop=True)
                            self.inc(inst, "sPE", f"sT{key}")
                        if self.on("scalar") or self.on("plan"):
                            self.wait(eng, "sPE", f"sT{key}")
                            prev = (f"ym{h}_{s}_{kc-2}" if kc >= 2 else
                                    (f"ym{h}_{s-1}_{2*s-2+pb}" if s > 0 else
                                     (f"ym{h-1}_7_{14+pb}" if h > 0 else None)))
                            if prev:
                                self.wait(eng, "sPE", prev)
                            inst = None
                            if self.on("scalar"):
                                inst = eng.activation(self.pT_sb[:, pb, :],
                                                      self.PS[pb][:, 0:256], AF.Exp,
                                                      scale=np.float32(SC_MAIN),
                                                      bias=np.float32(-20.0))
                            self.inc(inst, "sAE", f"pT{key}")
                        if kc >= nkc - 2 and (self.on("vector") or self.on("plan")):
                            self.wait(eng, "sAE", f"pT{key}")
                            m = self.trapA if kc == nkc - 2 else self.trapB
                            inst = None
                            if self.on("vector"):
                                inst = eng.tensor_mul(self.pT_sb[:, pb, :],
                                                      self.pT_sb[:, pb, :], m[:, :])
                            self.inc(inst, "sVE", f"mk{key}")
                    if kc >= 1:
                        kcy = kc - 1
                        pby = kcy % 2
                        ykey = f"{h}_{s}_{kcy}"
                        if self.on("tensor") or self.on("plan"):
                            if kcy >= nkc - 2:
                                self.wait(eng, "sVE", f"mk{ykey}")
                            else:
                                self.wait(eng, "sAE", f"pT{ykey}")
                            if kcy == 0 and yprev:
                                self.wait(eng, "sVE", yprev)
                            inst = None
                            if self.on("tensor"):
                                eng.matmul(self.PS[ye][:, 0:129],
                                           self.pT_sb[:, pby, 0:128],
                                           self.v_sb[:, hb, kcy, 0:129],
                                           start=(kcy == 0), stop=False,
                                           skip_group_check=True)
                                inst = eng.matmul(self.PS[yo][:, 0:129],
                                                  self.pT_sb[:, pby, 128:256],
                                                  self.v_sb[:, hb, kcy, 0:129],
                                                  start=(kcy == 0), stop=False,
                                                  skip_group_check=True)
                            self.inc(inst, "sPE", f"ym{ykey}")
                p7 = slice(0, 256) if s % 2 == 0 else slice(256, 512)
                if self.on("vector") or self.on("plan"):
                    self.wait(eng, "sPE", f"ym{h}_{s}_{nkc-1}")
                    prevtr = (f"aptr{h}_{s-1}" if s > 0 else
                              (f"aptr{h-1}_7" if h > 0 else None))
                    if prevtr:
                        self.wait(eng, "sPE", prevtr)
                    inst = None
                    if self.on("vector"):
                        a = self.ascr
                        self.V(eng, eng.tensor_mul(a[:, 2:3],
                                                   self.gra_sb[:, 2 * s:2 * s + 1],
                                                   self.PS[ye][:, 128:129]))
                        self.V(eng, eng.tensor_mul(a[:, 3:4],
                                                   self.gra_sb[:, 2 * s + 1:2 * s + 2],
                                                   self.PS[yo][:, 128:129]))
                        self.V(eng, eng.tensor_scalar_mul(
                            self.ap_sb[:, 0:10],
                            self.ea_sb[:, 20 * s:20 * s + 10], a[:, 2:3]))
                        inst = eng.tensor_scalar_mul(
                            self.ap_sb[:, 10:20],
                            self.ea_sb[:, 20 * s + 10:20 * s + 20], a[:, 3:4])
                    self.inc(inst, "sVE", f"apv{skey}")
                    self.VM(eng, f"apv{skey}")
                if self.on("tensor") or self.on("plan"):
                    self.wait(eng, "sVE", f"apv{skey}")
                    prev = (f"apev{h}_{s-2}" if s >= 2 else
                            (f"apev{h-1}_{s+6}" if h > 0 else None))
                    if prev:
                        self.wait(eng, "sAE", prev)
                    inst = None
                    if self.on("tensor"):
                        eng.transpose(self.PS7[0:10, p7.start:p7.start + 128],
                                      self.ap_sb[:, 0:10], self.eye_bf[:, :])
                        inst = eng.transpose(self.PS7[0:10, p7.start + 128:p7.stop],
                                             self.ap_sb[:, 10:20], self.eye_bf[:, :])
                    self.inc(inst, "sPE", f"aptr{skey}")
                if self.on("scalar") or self.on("plan"):
                    self.wait(eng, "sPE", f"aptr{skey}")
                    prev = (f"yadj{h}_{s-2}" if s >= 2 else
                            (f"yadj{h-1}_{s+6}" if h > 0 else None))
                    if prev:
                        self.wait(eng, "sPE", prev)
                    inst = None
                    if self.on("scalar"):
                        inst = eng.activation(self.apT_sb[0:10, s % 2, :],
                                              self.PS7[0:10, p7], AF.Copy)
                    self.inc(inst, "sAE", f"apev{skey}")
                if self.on("tensor") or self.on("plan"):
                    self.wait(eng, "sAE", f"apev{skey}")
                    inst = None
                    if self.on("tensor"):
                        eng.matmul(self.PS[ye][:, 0:129],
                                   self.apT_sb[0:10, s % 2, 0:128],
                                   self.av_sb[0:10, h, :], start=False, stop=True,
                                   skip_group_check=True)
                        inst = eng.matmul(self.PS[yo][:, 0:129],
                                          self.apT_sb[0:10, s % 2, 128:256],
                                          self.av_sb[0:10, h, :], start=False,
                                          stop=True, skip_group_check=True)
                    self.inc(inst, "sPE", f"yadj{skey}")
                if self.on("vector") or self.on("plan"):
                    self.wait(eng, "sPE", f"yadj{skey}")
                    prev = (f"ysp{h}_{s-2}" if s >= 2 else
                            (f"ysp{h-1}_{s+6}" if h > 0 else None))
                    if prev:
                        self.wait(eng, "sSP", prev)
                    inst = None
                    if self.on("vector"):
                        a = self.ascr
                        self.V(eng, eng.reciprocal(a[:, 4:5], self.PS[ye][:, 128:129]))
                        self.V(eng, eng.reciprocal(a[:, 5:6], self.PS[yo][:, 128:129]))
                        self.V(eng, eng.tensor_scalar_mul(self.ysb[:, s % 2, 0, :],
                                                          self.PS[ye][:, 0:128],
                                                          a[:, 4:5]))
                        inst = eng.tensor_scalar_mul(self.ysb[:, s % 2, 1, :],
                                                     self.PS[yo][:, 0:128], a[:, 5:6])
                    self.inc(inst, "sVE", f"yev{skey}")
                    self.VM(eng, f"yev{skey}")
                if self.on("gpsimd") or self.on("plan"):
                    self.wait(eng, "sVE", f"yev{skey}")
                    inst = None
                    if self.on("gpsimd"):
                        eng.dma_start(
                            out=self.y_dram[s * 256:s * 256 + 128,
                                            h * 128:(h + 1) * 128],
                            in_=self.ysb[:, s % 2, 0, :]).then_inc(self.sems["sSP"], 16)
                        inst = eng.dma_start(
                            out=self.y_dram[s * 256 + 128:(s + 1) * 256,
                                            h * 128:(h + 1) * 128],
                            in_=self.ysb[:, s % 2, 1, :])
                    if self.mode == "plan":
                        self.cnt["sSP"] += 16
                    self.inc(inst, "sSP", f"ysp{skey}", 16)

    def ph5_gather(self, eng):
        if self.on("gpsimd") or self.on("plan"):
            self.wait(eng, "sSP", f"ysp{HC-1}_7")
            inst = None
            if self.on("gpsimd"):
                pid = self.nc.partition_id(engines=[mybir.EngineType.Pool])
                slot = pid % 2
                other = 1 - slot
                eng.dma_start(out=self.sh_y[bass.ds(slot, 1), :, :],
                              in_=self.y_dram[:, :]).then_inc(self.sem_poll, 16)
                eng.wait_ge(self.sem_poll, 16)
                eng.dma_start(out=self.sh_f[bass.ds(slot, 1), :],
                              in_=self.ntl[:, :]).then_inc(self.sem_poll, 16)
                eng.wait_ge(self.sem_poll, 32)
                with (eng.register("pf") as pf, eng.register("nr") as nr,
                      eng.register("df") as df, eng.register("pc") as pc):
                    eng.reg_load(nr, self.ntl[0:1, 0:1])
                    eng.reg_mov(df, 1)
                    eng.reg_mov(pc, 32)
                    with eng.While(df):
                        eng.reg_add(pc, pc, 16)
                        eng.dma_start(
                            out=self.ptl[:, :],
                            in_=self.sh_f[bass.ds(other, 1), :]
                        ).then_inc(self.sem_poll, 16)
                        eng.wait_ge(self.sem_poll, pc)
                        eng.reg_load(pf, self.ptl[0:1, 0:1])
                        eng.reg_sub(df, pf, nr)
                inst = eng.nop()
            self.inc(inst, "sCC", "ag", 16)
        if self.on("sync"):
            eng.wait_ge(self.sems["sCC"], self.M["ag"])

    def ph6_proj(self, eng):
        for og in range(2):
            if og == 1 and (self.on("gpsimd") or self.on("plan")):
                self.wait(eng, "sPE", f"pj0_{NT-1}_1")
                for q in range(4):
                    inst = None
                    if self.on("gpsimd"):
                        inst = eng.dma_start(
                            out=self.wp_sb[:, q * 8:(q + 1) * 8, :],
                            in_=self.wprojT_t[1, :, q * 8:(q + 1) * 8, :])
                    self.inc(inst, "sSP", f"wp1_{q}", 16)
            def yt_loads(tp2, og2):
                tb2 = tp2 % 2
                if self.on("sync") or self.on("plan"):
                    prev = None
                    if tp2 >= 2:
                        prev = (og2, 2 * tp2 - 3)
                    elif og2 == 1:
                        prev = (0, NT - 4 + 2 * tp2 + 1)
                    if prev:
                        self.wait(eng, "sPE", f"pj{prev[0]}_{prev[1]}_1")
                    for cc in range(NCC):
                        blk = 0 if cc < 16 else 1
                        coff = (cc % 16) * 128
                        inst = None
                        if self.on("sync"):
                            inst = eng.dma_start(
                                out=self.yT_sb[:, tb2, cc, :],
                                in_=self.sh_y[blk, tp2 * 256:(tp2 + 1) * 256,
                                              coff:coff + 128],
                                transpose=True)
                        self.inc(inst, "sLD", f"yT{og2}_{tp2}_{cc}", 16)

            if og == 0:
                yt_loads(0, 0)
            for tp in range(NT // 2):
                tb = tp % 2
                if tp + 1 < NT // 2:
                    yt_loads(tp + 1, og)
                elif og == 0:
                    yt_loads(0, 1)
                for ti in range(2):
                    tt = 2 * tp + ti
                    rows = slice(tt * 128, (tt + 1) * 128)
                    if self.on("scalar") or self.on("plan"):
                        for ow in range(2):
                            cols = slice(og * 1024 + ow * 512,
                                         og * 1024 + (ow + 1) * 512)
                            if tt >= 2 or og == 1:
                                po, pt = (og, tt - 2) if tt >= 2 else (0, NT - 2 + tt)
                                self.wait(eng, "sOUT", f"os{po}_{pt}_{ow}")
                            inst = None
                            if self.on("scalar"):
                                inst = eng.dma_start(out=self.gx_sb[:, tt % 2, 0, ow, :],
                                                     in_=self.g_dram[rows, cols])
                            self.inc(inst, "sXG", f"g{og}_{tt}_{ow}", 16)
                            inst = None
                            if self.on("scalar"):
                                inst = eng.dma_start(out=self.gx_sb[:, tt % 2, 1, ow, :],
                                                     in_=self.xc[rows, cols])
                            self.inc(inst, "sXG", f"xc{og}_{tt}_{ow}", 16)
                    for ow in range(2):
                        ps = self.PS[2 * (tt % 2) + ow]
                        if self.on("tensor") or self.on("plan"):
                            if ow == 0 and ti == 0:
                                self.wait(eng, "sLD", f"yT{og}_{tp}_{NCC-1}")
                                self.wait(eng, "sSP", f"wp{og}_3")
                            if tt >= 2 or og == 1:
                                po, pt = (og, tt - 2) if tt >= 2 else (0, NT - 2 + tt)
                                self.wait(eng, "sVE", f"cmb{po}_{pt}_{ow}")
                            inst = None
                            for cc in range(NCC):
                                if self.on("tensor"):
                                    inst = eng.matmul(
                                        ps[:, :], self.yT_sb[:, tb, cc,
                                                             ti * 128:(ti + 1) * 128],
                                        self.wp_sb[:, cc, ow * 512:(ow + 1) * 512],
                                        start=(cc == 0), stop=(cc == NCC - 1))
                            self.inc(inst, "sPE", f"pj{og}_{tt}_{ow}")
                        if self.on("vector") or self.on("plan"):
                            self.wait(eng, "sPE", f"pj{og}_{tt}_{ow}")
                            self.wait(eng, "sXG", f"g{og}_{tt}_{ow}")
                            self.wait(eng, "sXG", f"xc{og}_{tt}_{ow}")
                            inst = None
                            if self.on("vector"):
                                self.V(eng, eng.tensor_mul(
                                    self.t6scr[:, :], ps[:, :],
                                    self.gx_sb[:, tt % 2, 0, ow, :]))
                                inst = eng.tensor_add(self.gx_sb[:, tt % 2, 1, ow, :],
                                                      self.t6scr[:, :],
                                                      self.gx_sb[:, tt % 2, 1, ow, :])
                            self.inc(inst, "sVE", f"cmb{og}_{tt}_{ow}")
                            self.VM(eng, f"cmb{og}_{tt}_{ow}")
                        if self.on("sync") or self.on("plan"):
                            self.wait(eng, "sVE", f"cmb{og}_{tt}_{ow}")
                            cols = slice(og * 1024 + ow * 512,
                                         og * 1024 + (ow + 1) * 512)
                            inst = None
                            if self.on("sync"):
                                inst = eng.dma_start(out=self.out[rows, cols],
                                                     in_=self.gx_sb[:, tt % 2, 1, ow, :])
                            if self.mode == "plan":
                                self.cnt["sOUT"] += 16
                                self.M[f"os{og}_{tt}_{ow}"] = self.cnt["sOUT"]
                            elif inst is not None:
                                inst.then_inc(self.sems["sOUT"], 16)
        final = f"os1_{NT-1}_1"
        if self.on("sync"):
            eng.wait_ge(self.sems["sOUT"], self.M[final])
        if self.on("gpsimd"):
            eng.wait_ge(self.sems["sOUT"], self.M[final])


_NC_CACHE = None


def _build():
    global _NC_CACHE
    if _NC_CACHE is None:
        _NC_CACHE = Prog().build()
    return _NC_CACHE


def kernel(L0_attn_x, L0_attn_w_attn, L0_attn_w_proj, L0_attn_adapter_wte,
           L0_attn_gating, L0_ffn_W_res, L0_ffn_W_pre, L0_ffn_W_post,
           L0_ffn_H_res_base, L0_ffn_H_pre_base, L0_ffn_H_post_base,
           L0_ffn_gate, L0_ffn_W_branch):
    x = np.asarray(L0_attn_x, dtype=np.float32)
    w_attn = np.asarray(L0_attn_w_attn, dtype=np.float32)
    w_proj = np.asarray(L0_attn_w_proj, dtype=np.float32)
    wte = np.asarray(L0_attn_adapter_wte, dtype=np.float32)
    gating = np.ascontiguousarray(
        np.asarray(L0_attn_gating, dtype=np.float32).reshape(1, 1))
    W_res = np.asarray(L0_ffn_W_res, dtype=np.float32)
    W_pre = np.asarray(L0_ffn_W_pre, dtype=np.float32)
    W_post = np.asarray(L0_ffn_W_post, dtype=np.float32)
    Hres_b = np.asarray(L0_ffn_H_res_base, dtype=np.float32)
    Hpre_b = np.asarray(L0_ffn_H_pre_base, dtype=np.float32)
    Hpost_b = np.asarray(L0_ffn_H_post_base, dtype=np.float32)
    gate = np.ascontiguousarray(
        np.asarray(L0_ffn_gate, dtype=np.float32).reshape(1, 1))
    W_br = np.asarray(L0_ffn_W_branch, dtype=np.float32)

    nc = _build()
    nonce_arr = np.full((1, 64), np.random.randint(1, 2**31 - 1), np.int32)
    wbranch_t = np.ascontiguousarray(W_br.reshape(8, 128, 1024).transpose(1, 0, 2))
    wteT_t = np.ascontiguousarray(wte.reshape(10, 32, 128).transpose(2, 1, 0))
    in_maps = []
    for core in range(8):
        b, j = core >> 1, core & 1
        perm = np.concatenate([
            np.r_[(2 * i) * 128:(2 * i) * 128 + 64,
                  (2 * i + 1) * 128:(2 * i + 1) * 128 + 64,
                  (2 * i) * 128 + 64:(2 * i) * 128 + 128,
                  (2 * i + 1) * 128 + 64:(2 * i + 1) * 128 + 128]
            for i in range(8)])
        rows = [w_attn[g * C + 2048 * j: g * C + 2048 * (j + 1)] for g in range(3)]
        w_slice = np.concatenate([rows[0][perm], rows[1][perm], rows[2]], axis=0)
        wqkvT_t = np.ascontiguousarray(
            w_slice.reshape(48, 128, 32, 128).transpose(0, 3, 2, 1))
        wp_rows = w_proj[2048 * j: 2048 * (j + 1)]
        wprojT_t = np.ascontiguousarray(
            wp_rows.reshape(2, 1024, 32, 128).transpose(0, 3, 2, 1))
        iorder = [0, 1, 2, 3] if j == 0 else [2, 3, 0, 1]
        W_all = np.concatenate([
            W_res.reshape(4, 4, C)[iorder].reshape(16, C),
            W_pre, W_post[iorder]], axis=0)
        wffnT_t = np.ascontiguousarray(W_all.reshape(24, 32, 128).transpose(2, 1, 0))
        hbase = np.ascontiguousarray(np.concatenate(
            [Hres_b[iorder].reshape(16), Hpre_b, Hpost_b[iorder]]).reshape(1, 24))
        eyeP = np.ascontiguousarray(
            np.eye(4, dtype=np.float32)[iorder].reshape(1, 16))
        in_maps.append({
            "nonce": nonce_arr,
            "x_full": np.ascontiguousarray(x[b]),
            "xc": np.ascontiguousarray(x[b][:, 2048 * j: 2048 * (j + 1)]),
            "wqkvT_t": wqkvT_t,
            "wprojT_t": wprojT_t,
            "wbranch_t": wbranch_t,
            "wffnT_t": wffnT_t,
            "wteT_t": wteT_t,
            "hbase": hbase,
            "eyeP": eyeP,
            "gating": gating,
            "gate": gate,
        })
    res = run_bass_kernel_spmd(nc, in_maps, core_ids=list(range(8)))
    out = np.empty((B, T, C), dtype=np.float32)
    for core in range(8):
        b, j = core >> 1, core & 1
        out[b][:, 2048 * j: 2048 * (j + 1)] = res.results[core]["out"]
    return out


# version 08
# speedup vs baseline: 1.0286x; 1.0286x over previous
import numpy as np
import concourse.bass as bass
import concourse.mybir as mybir
from concourse.bass_utils import run_bass_kernel_spmd

F32 = mybir.dt.float32
BF16 = mybir.dt.bfloat16
AF = mybir.ActivationFunctionType
ALU = mybir.AluOpType
AX = mybir.AxisListType

B = 4
T = 2048
C = 4096
HC = 16
NT = T // 128
NCC = C // 128
LN_EPS = 1e-5
RMS_EPS = 1.1920929e-07
SINK_ITERS = 50
SC_MAIN = float(np.float32(1.0 / np.sqrt(np.float32(128.0))))
SC_ADPT = 1.0 / 64.0
CB = 98304


def _consts():
    theta = 1.0 / (10000.0 ** (np.arange(0, 128, 2, dtype=np.float64) / 128.0))
    idx = np.outer(np.arange(T, dtype=np.float64), theta)
    cos2 = np.cos(idx).astype(np.float32)
    sin2 = np.sin(idx).astype(np.float32)
    cosT = np.ascontiguousarray(np.concatenate([cos2, cos2], 1).T)
    sinT = np.concatenate([sin2, sin2], 1).T.copy()
    import ml_dtypes
    kl = np.arange(128)[:, None]
    ql = np.arange(256)[None, :]
    trapA = (kl <= ql).astype(ml_dtypes.bfloat16)
    trapB = ((kl + 128) <= ql).astype(ml_dtypes.bfloat16)
    return (cosT, np.ascontiguousarray(sinT), trapA, trapB,
            np.eye(128, dtype=ml_dtypes.bfloat16),
            np.eye(128, dtype=np.float32))


class Prog:

    def __init__(self, nphase=99, dump=None):
        self.nphase = nphase
        self.ph3_stop = 99
        self.dbg_ybr = False
        self.dbg_hfin = False
        self.dump = dump
        nc = bass.Bass("TRN2", target_bir_lowering=False,
                       detect_race_conditions=False)
        self.nc = nc
        d = nc.dram_tensor
        self.x_full = d("x_full", [T, C], F32, kind="ExternalInput")
        self.xc = d("xc", [T, 2048], F32, kind="ExternalInput")
        self.wqkvT_t = d("wqkvT_t", [48, 128, NCC, 128], F32, kind="ExternalInput")
        self.wprojT_t = d("wprojT_t", [2, 128, NCC, 1024], F32, kind="ExternalInput")
        self.wbranch_t = d("wbranch_t", [128, 8, 1024], F32, kind="ExternalInput")
        self.wffnT_t = d("wffnT_t", [128, NCC, 24], F32, kind="ExternalInput")
        self.wteT_t = d("wteT_t", [128, NCC, 10], F32, kind="ExternalInput")
        self.hbase = d("hbase", [1, 24], F32, kind="ExternalInput")
        self.eyeP = d("eyeP", [1, 16], F32, kind="ExternalInput")
        self.gating = d("gating", [1, 1], F32, kind="ExternalInput")
        self.gate = d("gate", [1, 1], F32, kind="ExternalInput")
        self.out = d("out", [T, 2048], F32, kind="ExternalOutput")
        self.dbg = None
        self.xn_dram = d("xn_dram", [T, C], BF16)
        self.qkT_dram = d("qkT_dram", [48, 128, T], BF16)
        self.xpre_dram = d("xpre_dram", [T, 1024], BF16)
        self.y_dram = d("y_dram", [T, 2048], BF16)
        self.sh_y = d("sh_y", [2, T, 2048], BF16, addr_space="Shared")
        self.sh_f = d("sh_f", [2, 64], mybir.dt.int32, addr_space="Shared")
        self.nonce = d("nonce", [1, 64], mybir.dt.int32, kind="ExternalInput")
        self.g_dram = d("g_dram", [T, 2048], F32)
        self.akav_dram = d("akav_dram", [2, HC, 128, 10], BF16)
        cosT, sinNT, trapA, trapB, eye_bf, eye_f32 = _consts()
        self.cosT_d = nc.inline_tensor(cosT, "cosT_d")
        self.sinP_d = nc.inline_tensor(sinNT, "sinP_d")
        self.trapA_d = nc.inline_tensor(trapA, "trapA_d")
        self.trapB_d = nc.inline_tensor(trapB, "trapB_d")
        self.eyebf_d = nc.inline_tensor(eye_bf, "eyebf_d")
        self.eyef32_d = nc.inline_tensor(eye_f32, "eyef32_d")

        A = nc.alloc_sbuf_tensor
        self.trapA = A("trapA", [128, 256], BF16)
        self.trapB = A("trapB", [128, 256], BF16)
        self.eye_bf = A("eye_bf", [128, 128], BF16)
        self.eye_f32 = A("eye_f32", [128, 128], F32)
        self.hb_b = A("hb_b", [128, 24], F32)
        self.hb10 = A("hb10", [128, 16], F32)
        self.eyeP_b = A("eyeP_b", [128, 16], F32)
        self.gating_b = A("gating_b", [128, 1], F32)
        self.gate_b = A("gate_b", [128, 1], F32)
        self.gs = A("gs", [128, 1], F32)
        self.rinv_all = A("rinv_all", [128, NT], F32)
        self.rinv10 = A("rinv10", [128, NT], F32)
        self.logits_tm = A("logits_tm", [128, NT, 24], F32)
        self.akT = A("akT", [128, HC, 10], BF16)
        self.avT = A("avT", [128, HC, 10], BF16)
        self.av_sb = A("av_sb", [16, HC, 129], BF16)
        self.wffn_sb = A("wffn_sb", [128, NCC, 24], BF16)
        self.wte_sb = A("wte_sb", [128, NCC, 10], BF16)
        self.s_tiny = A("s_tiny", [128, 16], F32)
        self.ntl = A("ntl", [1, 64], mybir.dt.int32)
        self.ptl = A("ptl", [1, 64], mybir.dt.int32)

        self.arena = A("arena", [128, 159744], mybir.dt.uint8)
        base = nc.lookup_mloc(self.arena).addr
        self._b = base
        P = self._place
        self.sqscr = P("sqscr", 0, [128, 4096], BF16)
        self.xnT = P("xnT", 0, [128, NCC, 1024], BF16)
        self.y_br = P("y_br", 0, [128, NT, 1024], F32)
        self.wp_sb = P("wp_sb", 0, [128, NCC, 1024], BF16)
        self.w_col = P("w_col", 65536, [128, 4, NCC, 128], BF16)
        self.xpreT = P("xpreT", 65536, [128, 8, T], BF16)
        self.yT_sb = P("yT_sb", 65536, [128, 2, NCC, 256], BF16)
        self.gx_sb = P("gx_sb", CB, [128, 2, 2, 2, 512], F32)
        self.t6scr = P("t6scr", CB + 16384, [128, 512], F32)
        self.x_sb = P("x_sb", CB, [128, 2, C], F32)
        self.xn1_sb = P("xn1_sb", CB + 32768, [128, 2, C], BF16)
        self.lgT = P("lgT", CB, [24, T], F32)
        self.cosT = P("cosT", CB + 8192, [128, T], F32)
        self.sinP = P("sinP", CB + 16384, [128, T], F32)
        self.cscr = P("cscr", CB + 24576, [128, 512], F32)
        self.s1scr = P("s1scr", CB + 26624, [128, 512], F32)
        self.ropeE = P("ropeE", CB + 28672, [128, 1024], BF16)
        self.ropeO = P("ropeO", CB + 30720, [128, 1024], BF16)
        self.akmix = P("akmix", CB + 32768 - 128, [128, 2, 10], BF16)
        self.Z = P("Z", CB, [128, NT * 16], F32)
        self.T0 = P("T0", CB + 1024, [128, NT * 16], F32)
        self.E = P("E", CB + 2048, [128, NT * 16], F32)
        self.Su = P("Su", CB + 3072, [128, NT * 4], F32)
        self.lnSu = P("lnSu", CB + 3328, [128, NT * 4], F32)
        self.Sv = P("Sv", CB + 3584, [128, NT * 4], F32)
        self.lnSv = P("lnSv", CB + 3840, [128, NT * 4], F32)
        self.LG = P("LG", CB + 4096, [128, NT * 8], F32)
        self.E2 = P("E2", CB + 4608, [128, NT * 8], F32)
        self.S2 = P("S2", CB + 5120, [128, NT * 2], F32)
        self.R2 = P("R2", CB + 5248, [128, NT * 2], F32)
        self.HP = P("HP", CB + 5376, [128, NT * 8], F32)
        self.Hfin = P("Hfin", CB + 5888, [128, NT * 16], F32)
        self.Mu = P("Mu", CB + 6912, [128, NT * 4], F32)
        self.Mv = P("Mv", CB + 7168, [128, NT * 4], F32)
        self.lnR = P("lnR", CB + 7424, [128, NT * 4], F32)
        self.M2 = P("M2", CB + 7680, [128, NT * 2], F32)
        self.xpre_sb = P("xpre_sb", CB + 8192, [128, 2, 1024], BF16)
        self.wb_sb = P("wb_sb", CB + 12288, [128, 8, 1024], BF16)
        self.xn3_sb = P("xn3_sb", CB + 28672, [128, 2, C], BF16)
        self.acc_sb = P("acc_sb", CB + 45056, [128, 2048], F32)
        self.g_sb = P("g_sb", CB + 53248, [128, 2048], F32)
        self.kT_sb = P("kT_sb", 65536, [128, 2, T], BF16)
        self.qT_sb = P("qT_sb", 73728, [128, 2, T], BF16)
        self.v_sb = P("v_sb", 81920, [128, 2, 16, 144], BF16)
        self.pT_sb = P("pT_sb", 91136, [128, 2, 256], BF16)
        self.ea_sb = P("ea_sb", 92160, [128, 2, 160], F32)
        self.asum_sb = P("asum_sb", 95584, [128, 2, 16], F32)
        self.gra_sb = P("gra_sb", 95712, [128, 2, 16], F32)
        self.ap_sb = P("ap_sb", 93440, [128, 20], BF16)
        self.apT_sb = P("apT_sb", 93504, [16, 2, 256], BF16)
        self.ysb = P("ysb", 94528, [128, 2, 2, 128], BF16)
        self.ascr = P("ascr", 95552, [128, 8], F32)

        self.PS = [nc.alloc_psum_tensor(f"PS{i}", [128, 512], F32) for i in range(7)]
        self.PS7 = nc.alloc_psum_tensor("PS7", [128, 1024], BF16)

        self.sems = {}
        for name in ["sLD", "sSP", "sPE", "sAE", "sVE", "sOUT", "sCC", "sXG"]:
            self.sems[name] = nc.alloc_semaphore(name)
        self.sem_poll = nc.alloc_semaphore("sPOLL")
        self.sem_vc = nc.alloc_semaphore("sVC")
        self._vc = 0
        self.sem_ac = nc.alloc_semaphore("sAC")
        self._ac = 0
        self.sem_bar = nc.alloc_semaphore("sBAR")
        self._nbar = 0
        self.M = {}
        self.cnt = {k: 0 for k in self.sems}
        self.mode = "plan"

    def _place(self, name, off, shape, dtype):
        nbytes = int(np.prod(shape[1:])) * mybir.dt.size(dtype)
        assert off + nbytes <= 159744, (name, off, nbytes)
        return self.nc.alloc_sbuf_tensor_at(name, shape, dtype,
                                            offset=self._b + off)

    def inc(self, inst, sem, key, n=1):
        if self.mode == "plan":
            self.cnt[sem] += n
            assert key not in self.M, key
            self.M[key] = self.cnt[sem]
        else:
            assert inst is not None, f"milestone {key} has no instruction"
            inst.then_inc(self.sems[sem], n)
        return inst

    def V(self, eng, inst):
        if self.mode == "vector":
            self._vc += 1
            inst.then_inc(self.sem_vc, 1)
            eng.wait_ge(self.sem_vc, self._vc)
        return inst

    def SA(self, eng, inst):
        if self.mode == "scalar":
            self._ac += 1
            inst.then_inc(self.sem_ac, 1)
            eng.wait_ge(self.sem_ac, self._ac)
        return inst

    def VM(self, eng, key):
        if self.mode == "vector":
            eng.wait_ge(self.sems["sVE"], self.M[key])

    def wait(self, eng, sem, key):
        if self.mode == "plan":
            return
        eng.wait_ge(self.sems[sem], self.M[key])

    def on(self, which):
        return self.mode == which

    def build(self):
        with self.nc.Block() as block:
            self.mode = "plan"
            self._walk(None)
            for mode in ["sync", "tensor", "scalar", "vector", "gpsimd"]:
                self.mode = mode
                getattr(block, mode)(lambda eng: self._walk(eng))
        return self.nc

    def _walk(self, eng):
        phases = [self.ph0_consts, self.ph1_ln, self.ph2_qkv, self.ph3_ffn,
                  self.ph4_attn, self.ph5_gather, self.ph6_proj]
        for i, fn in enumerate(phases):
            if i > self.nphase:
                break
            fn(eng)
            if i < len(phases) - 1:
                self.barrier(eng, i)
        if self.dump is not None:
            self.ph_dump(eng)

    def ph_dump(self, eng):
        src = getattr(self, self.dump)
        n = int(np.prod(src.shape))
        if self.dbg is None:
            self.dbg = self.nc.dram_tensor("dbg", [1, n], src.dtype,
                                           kind="ExternalOutput")
        if self.on("gpsimd") or self.on("plan"):
            inst = None
            if self.on("gpsimd"):
                inst = eng.dma_start(out=self.dbg[0, :], in_=src.ap().flatten())
            self.inc(inst, "sSP", "dumpd", 16)
            self.wait(eng, "sSP", "dumpd")

    def barrier(self, eng, i):
        marks = {"sync": "sLD", "tensor": "sPE", "scalar": "sAE",
                 "vector": "sVE", "gpsimd": "sSP"}
        if self.mode == "plan":
            self._nbar += 1
            for e, sm in marks.items():
                self.M[f"bar{i}_{e}"] = self.cnt[sm]
            self.M[f"bar{i}"] = 5 * self._nbar
            return
        eng.wait_ge(self.sems[marks[self.mode]], self.M[f"bar{i}_{self.mode}"])
        eng.nop().then_inc(self.sem_bar, 1)
        eng.wait_ge(self.sem_bar, self.M[f"bar{i}"])

    def ph0_consts(self, eng):
        if self.on("sync") or self.on("plan"):
            loads = [
                (self.trapA[:, :], self.trapA_d[:, :]),
                (self.trapB[:, :], self.trapB_d[:, :]),
                (self.eye_bf[:, :], self.eyebf_d[:, :]),
                (self.eye_f32[:, :], self.eyef32_d[:, :]),
                (self.hb_b[:, :], bass.AP(self.hbase, 0, [[0, 128], [1, 24]])),
                (self.eyeP_b[:, :], bass.AP(self.eyeP, 0, [[0, 128], [1, 16]])),
                (self.gating_b[:, :], bass.AP(self.gating, 0, [[0, 128], [1, 1]])),
                (self.gate_b[:, :], bass.AP(self.gate, 0, [[0, 128], [1, 1]])),
            ]
            for i, (dst, src) in enumerate(loads):
                inst = eng.dma_start(out=dst, in_=src) if self.on("sync") else None
                self.inc(inst, "sLD", f"c{i}", 16)
        if self.on("gpsimd") or self.on("plan"):
            casts = [(self.wffn_sb[:, :, :], self.wffnT_t[:, :, :]),
                     (self.wte_sb[:, :, :], self.wteT_t[:, :, :]),
                     (self.ntl[:, :], self.nonce[:, :])]
            for i, (dst, src) in enumerate(casts):
                inst = eng.dma_start(out=dst, in_=src) if self.on("gpsimd") else None
                self.inc(inst, "sSP", f"cw{i}", 16)
        if self.on("scalar") or self.on("plan"):
            self.wait(eng, "sLD", "c7")
            inst = None
            if self.on("scalar"):
                inst = eng.activation(self.s_tiny[:, 0:1], self.gate_b[:, :],
                                      AF.Exp, scale=np.float32(-1.0))
            self.inc(inst, "sAE", "gsx")
        if self.on("vector") or self.on("plan"):
            self.wait(eng, "sAE", "gsx")
            self.wait(eng, "sLD", "c4")
            inst = None
            if self.on("vector"):
                self.V(eng, eng.tensor_scalar_add(self.s_tiny[:, 1:2],
                                                  self.s_tiny[:, 0:1], 1.0))
                eng.reciprocal(self.gs[:, :], self.s_tiny[:, 1:2])
                inst = eng.tensor_scalar_mul(self.hb10[:, :], self.hb_b[:, 0:16], 10.0)
            self.inc(inst, "sVE", "hb10")

    def ph1_ln(self, eng):
        inv_c = 1.0 / C
        for tp in range(NT // 2):
            uu = (2 * tp, 2 * tp + 1)
            if self.on("sync") or self.on("plan"):
                for tt in uu:
                    bi = tt % 2
                    if tt >= 2:
                        self.wait(eng, "sVE", f"xn{tt-2}")
                        self.wait(eng, "sAE", f"rsq{tt-2}")
                    inst = eng.dma_start(out=self.x_sb[:, bi, :],
                                         in_=self.x_full[tt * 128:(tt + 1) * 128, :]) if self.on("sync") else None
                    self.inc(inst, "sLD", f"x{tt}", 16)
            if self.on("vector") or self.on("plan"):
                st = self.s_tiny
                for tt in uu:
                    bi = tt % 2; o = 8 * bi
                    self.wait(eng, "sLD", f"x{tt}")
                    if self.on("vector"):
                        self.V(eng, eng.reduce_sum(st[:, o:o+1],
                                                   self.x_sb[:, bi, :], AX.X))
                for tt in uu:
                    bi = tt % 2; o = 8 * bi
                    self.wait(eng, "sAE", f"sq{tt}")
                    inst = None
                    if self.on("vector"):
                        self.V(eng, eng.tensor_scalar_mul(st[:, o+2:o+3],
                                                          st[:, o:o+1], inv_c))
                        self.V(eng, eng.tensor_mul(st[:, o+3:o+4], st[:, o+2:o+3],
                                                   st[:, o+2:o+3]))
                        self.V(eng, eng.tensor_scalar_mul(st[:, o+4:o+5],
                                                          st[:, o+1:o+2], inv_c))
                        inst = eng.tensor_sub(st[:, o+5:o+6], st[:, o+4:o+5],
                                              st[:, o+3:o+4])
                    self.inc(inst, "sVE", f"var{tt}")
                for tt in uu:
                    bi = tt % 2; o = 8 * bi
                    self.wait(eng, "sAE", f"rstd{tt}")
                    if tt >= 2:
                        self.wait(eng, "sSP", f"xnsp{tt-2}")
                    inst = None
                    if self.on("vector"):
                        inst = eng.tensor_scalar(self.xn1_sb[:, bi, :],
                                                 self.x_sb[:, bi, :],
                                                 st[:, o+2:o+3], st[:, o+7:o+8],
                                                 ALU.subtract, ALU.mult)
                    self.inc(inst, "sVE", f"xn{tt}")
            if self.on("scalar") or self.on("plan"):
                st = self.s_tiny
                for tt in uu:
                    bi = tt % 2; o = 8 * bi
                    self.wait(eng, "sLD", f"x{tt}")
                    inst = None
                    if self.on("scalar"):
                        inst = eng.activation(self.sqscr[:, :], self.x_sb[:, bi, :],
                                              AF.Square, accum_out=st[:, o+1:o+2])
                    self.inc(inst, "sAE", f"sq{tt}")
                for tt in uu:
                    bi = tt % 2; o = 8 * bi
                    self.wait(eng, "sVE", f"var{tt}")
                    inst = None
                    if self.on("scalar"):
                        self.SA(eng, eng.activation(st[:, o+6:o+7], st[:, o+5:o+6],
                                                    AF.Ln, bias=np.float32(LN_EPS)))
                        inst = eng.activation(st[:, o+7:o+8], st[:, o+6:o+7],
                                              AF.Exp, scale=np.float32(-0.5))
                    self.inc(inst, "sAE", f"rstd{tt}")
                for tt in uu:
                    bi = tt % 2; o = 8 * bi
                    self.wait(eng, "sVE", f"xn{tt}")
                    inst = None
                    if self.on("scalar"):
                        inst = eng.activation(self.sqscr[:, :], self.xn1_sb[:, bi, :],
                                              AF.Square, accum_out=st[:, o+1:o+2])
                    self.inc(inst, "sAE", f"rsq{tt}")
                for tt in uu:
                    bi = tt % 2; o = 8 * bi
                    if self.on("scalar"):
                        eng.wait_ge(self.sems["sAE"], self.M[f"rsq{tt}"])
                    inst = None
                    if self.on("scalar"):
                        self.SA(eng, eng.activation(st[:, o+6:o+7], st[:, o+1:o+2],
                                                    AF.Ln, scale=np.float32(inv_c),
                                                    bias=np.float32(RMS_EPS)))
                        inst = eng.activation(self.rinv_all[:, tt:tt + 1],
                                              st[:, o+6:o+7], AF.Exp,
                                              scale=np.float32(-0.5))
                    self.inc(inst, "sAE", f"rinv{tt}")
            if self.on("gpsimd") or self.on("plan"):
                for tt in uu:
                    bi = tt % 2
                    self.wait(eng, "sVE", f"xn{tt}")
                    inst = eng.dma_start(out=self.xn_dram[tt * 128:(tt + 1) * 128, :],
                                         in_=self.xn1_sb[:, bi, :]) if self.on("gpsimd") else None
                    self.inc(inst, "sSP", f"xnsp{tt}", 16)
        if self.on("vector") or self.on("plan"):
            self.wait(eng, "sAE", f"rinv{NT-1}")
            inst = None
            if self.on("vector"):
                inst = eng.tensor_scalar_mul(self.rinv10[:, :], self.rinv_all[:, :],
                                             10.0)
            self.inc(inst, "sVE", "rinvten")

    def ph2_qkv(self, eng):
        if self.on("sync") or self.on("plan"):
            for i, (dst, src) in enumerate([(self.cosT, self.cosT_d),
                                            (self.sinP, self.sinP_d)]):
                inst = eng.dma_start(out=dst[:, :], in_=src[:, :]) if self.on("sync") else None
                self.inc(inst, "sLD", f"rt{i}", 16)
        bprev = {}
        rprev = {}
        akprev = None
        for tp in range(2):
            if self.on("sync") or self.on("plan"):
                if tp == 1:
                    self.wait(eng, "sPE", "lg0")
                for cc in range(NCC):
                    src = self.xn_dram[tp * 1024:(tp + 1) * 1024,
                                       cc * 128:(cc + 1) * 128]
                    inst = None
                    if self.on("sync"):
                        inst = eng.dma_start(out=self.xnT[:, cc, :], in_=src,
                                             transpose=True)
                    self.inc(inst, "sLD", f"xnT{tp}_{cc}", 16)
            for p in range(16):
                is_k = p >= 8
                sa, sb = (0, 1) if p % 2 == 0 else (2, 3)
                if self.on("gpsimd") or self.on("plan"):
                    if p >= 2:
                        self.wait(eng, "sPE", f"qk{tp}_{p-2}_1")
                    elif tp == 1:
                        self.wait(eng, "sPE", "lg0")
                    for z, slot in ((0, sa), (1, sb)):
                        inst = None
                        if self.on("gpsimd"):
                            inst = eng.dma_start(out=self.w_col[:, slot, :, :],
                                                 in_=self.wqkvT_t[2 * p + z, :, :, :])
                        self.inc(inst, "sSP", f"wc{tp}_{p}_{z}", 16)
                for win in range(2):
                    pse = self.PS[2 * win]
                    pso = self.PS[2 * win + 1]
                    key = f"qk{tp}_{p}_{win}"
                    rider = is_k and tp == 0 and win == 0
                    if self.on("tensor") or self.on("plan"):
                        self.wait(eng, "sSP", f"wc{tp}_{p}_{1}")
                        self.wait(eng, "sLD", f"xnT{tp}_{NCC-1}")
                        tag = f"b{win}"
                        if tag in bprev:
                            self.wait(eng, bprev[tag][0], bprev[tag][1])
                        bprev[tag] = ("sVE", f"rp{key}")
                        if rider:
                            for tg in ("r4", "r5"):
                                if tg in bprev:
                                    self.wait(eng, bprev[tg][0], bprev[tg][1])
                                bprev[tg] = ("sAE", f"akm{tp}_{p}")
                        inst = None
                        for c in range(NCC):
                            if self.on("tensor"):
                                eng.matmul(pse[:, :], self.w_col[:, sa, c, :],
                                           self.xnT[:, c, win * 512:(win + 1) * 512],
                                           start=(c == 0), stop=(c == NCC - 1))
                                inst = eng.matmul(
                                    pso[:, :], self.w_col[:, sb, c, :],
                                    self.xnT[:, c, win * 512:(win + 1) * 512],
                                    start=(c == 0), stop=(c == NCC - 1))
                                if rider:
                                    eng.matmul(self.PS[4][:, 0:10],
                                               self.w_col[:, sa, c, :],
                                               self.wte_sb[:, c, :],
                                               start=(c == 0), stop=(c == NCC - 1))
                                    inst = eng.matmul(self.PS[5][:, 0:10],
                                                      self.w_col[:, sb, c, :],
                                                      self.wte_sb[:, c, :],
                                                      start=(c == 0),
                                                      stop=(c == NCC - 1))
                        self.inc(inst, "sPE", key)
                    if self.on("vector") or self.on("plan"):
                        self.wait(eng, "sPE", key)
                        if win == 0 and tp == 0 and p == 0:
                            self.wait(eng, "sLD", "rt1")
                        for tag in ("rE", "rO"):
                            if win == 0 and tag in rprev:
                                self.wait(eng, "sLD", rprev[tag])
                        half = slice(tp * 1024 + win * 512,
                                     tp * 1024 + (win + 1) * 512)
                        wsl = slice(win * 512, (win + 1) * 512)
                        inst = None
                        if self.on("vector"):
                            self.V(eng, eng.tensor_mul(self.cscr[:, :], pse[:, :],
                                                       self.cosT[:, half]))
                            self.V(eng, eng.tensor_mul(self.s1scr[:, :], pso[:, :],
                                                       self.sinP[:, half]))
                            self.V(eng, eng.tensor_sub(self.ropeE[:, wsl],
                                                       self.cscr[:, :],
                                                       self.s1scr[:, :]))
                            self.V(eng, eng.tensor_mul(self.cscr[:, :], pso[:, :],
                                                       self.cosT[:, half]))
                            self.V(eng, eng.tensor_mul(self.s1scr[:, :], pse[:, :],
                                                       self.sinP[:, half]))
                            inst = eng.tensor_add(self.ropeO[:, wsl],
                                                  self.cscr[:, :], self.s1scr[:, :])
                        self.inc(inst, "sVE", f"rp{key}")
                        self.VM(eng, f"rp{key}")
                if is_k and tp == 0:
                    if self.on("scalar") or self.on("plan"):
                        self.wait(eng, "sPE", f"qk{tp}_{p}_0")
                        if akprev is not None:
                            self.wait(eng, "sLD", akprev)
                        inst = None
                        if self.on("scalar"):
                            eng.activation(self.akmix[:, 0, :], self.PS[4][:, 0:10],
                                           AF.Copy)
                            inst = eng.activation(self.akmix[:, 1, :],
                                                  self.PS[5][:, 0:10], AF.Copy)
                        self.inc(inst, "sAE", f"akm{tp}_{p}")
                    if self.on("sync") or self.on("plan"):
                        self.wait(eng, "sAE", f"akm{tp}_{p}")
                        hA = 2 * (p - 8)
                        for z in range(2):
                            dst = bass.AP(
                                self.akav_dram,
                                (0 * HC + hA) * 1280 + z * 640,
                                [[1280, 2], [10, 64], [1, 10]])
                            inst = None
                            if self.on("sync"):
                                inst = eng.dma_start(out=dst,
                                                     in_=self.akmix[:, z, :])
                            self.inc(inst, "sLD", f"aks{tp}_{p}_{z}", 16)
                        akprev = f"aks{tp}_{p}_1"
                if self.on("sync") or self.on("plan"):
                    self.wait(eng, "sVE", f"rpqk{tp}_{p}_1")
                    hA = 2 * (p % 8) + (16 if is_k else 0)
                    half = slice(tp * 1024, (tp + 1) * 1024)
                    outs = [
                        (self.qkT_dram[hA, 0:64, half], self.ropeE[0:64, :]),
                        (self.qkT_dram[hA + 1, 0:64, half], self.ropeE[64:128, :]),
                        (self.qkT_dram[hA, 64:128, half], self.ropeO[0:64, :]),
                        (self.qkT_dram[hA + 1, 64:128, half], self.ropeO[64:128, :]),
                    ]
                    for z, (dst, srcb) in enumerate(outs):
                        inst = None
                        if self.on("sync"):
                            inst = eng.dma_start(out=dst, in_=srcb)
                        self.inc(inst, "sLD", f"spl{tp}_{p}_{z}", 16)
                    rprev["rE"] = f"spl{tp}_{p}_1"
                    rprev["rO"] = f"spl{tp}_{p}_3"
            for oc in range(32, 48):
                h = oc - 32
                sa, sb = (0, 1) if oc % 2 == 0 else (2, 3)
                rider = tp == 0
                key = f"v{tp}_{oc}"
                if self.on("gpsimd") or self.on("plan"):
                    pk = (f"v{tp}_{oc-2}" if oc >= 34 else f"qk{tp}_{14+(oc-32)}_1")
                    self.wait(eng, "sPE", pk)
                    inst = None
                    if self.on("gpsimd"):
                        inst = eng.dma_start(out=self.w_col[:, sa, :, :],
                                             in_=self.wqkvT_t[oc, :, :, :])
                    self.inc(inst, "sSP", f"wcv{tp}_{oc}", 16)
                if self.on("tensor") or self.on("plan"):
                    self.wait(eng, "sSP", f"wcv{tp}_{oc}")
                    psa, psb = self.PS[sa], self.PS[sb]
                    tag = "b0" if sa == 0 else "b1"
                    if tag in bprev:
                        self.wait(eng, bprev[tag][0], bprev[tag][1])
                    bprev[tag] = ("sAE", f"ev{key}")
                    if rider:
                        tg = "r4" if oc % 2 == 0 else "r5"
                        if tg in bprev:
                            self.wait(eng, bprev[tg][0], bprev[tg][1])
                        bprev[tg] = ("sAE", f"akmv{tp}_{oc}")
                    inst = None
                    for c in range(NCC):
                        if self.on("tensor"):
                            eng.matmul(psa[:, :], self.w_col[:, sa, c, :],
                                       self.xnT[:, c, 0:512],
                                       start=(c == 0), stop=(c == NCC - 1))
                            inst = eng.matmul(psb[:, :], self.w_col[:, sa, c, :],
                                              self.xnT[:, c, 512:1024],
                                              start=(c == 0), stop=(c == NCC - 1))
                            if rider:
                                inst = eng.matmul(
                                    self.PS[4 + (oc % 2)][:, 0:10],
                                    self.w_col[:, sa, c, :], self.wte_sb[:, c, :],
                                    start=(c == 0), stop=(c == NCC - 1))
                    self.inc(inst, "sPE", key)
                if self.on("scalar") or self.on("plan"):
                    self.wait(eng, "sPE", key)
                    if "rE" in rprev:
                        self.wait(eng, "sLD", rprev["rE"])
                    inst = None
                    if self.on("scalar"):
                        eng.activation(self.ropeE[:, 0:512], self.PS[sa][:, :],
                                       AF.Copy)
                        inst = eng.activation(self.ropeE[:, 512:1024],
                                              self.PS[sb][:, :], AF.Copy)
                    self.inc(inst, "sAE", f"ev{key}")
                    if rider:
                        if akprev is not None:
                            self.wait(eng, "sLD", akprev)
                        inst = None
                        if self.on("scalar"):
                            inst = eng.activation(self.akmix[:, 0, :],
                                                  self.PS[4 + (oc % 2)][:, 0:10],
                                                  AF.Copy)
                        self.inc(inst, "sAE", f"akmv{tp}_{oc}")
                if self.on("sync") or self.on("plan"):
                    self.wait(eng, "sAE", f"ev{key}")
                    inst = None
                    if self.on("sync"):
                        inst = eng.dma_start(
                            out=self.qkT_dram[oc, :, tp * 1024:(tp + 1) * 1024],
                            in_=self.ropeE[:, :])
                    self.inc(inst, "sLD", f"vsp{key}", 16)
                    rprev["rE"] = f"vsp{key}"
                    if rider:
                        self.wait(eng, "sAE", f"akmv{tp}_{oc}")
                        inst = None
                        if self.on("sync"):
                            inst = eng.dma_start(
                                out=self.akav_dram[1, h, :, :],
                                in_=self.akmix[:, 0, :])
                        self.inc(inst, "sLD", f"avs{tp}_{oc}", 16)
                        akprev = f"avs{tp}_{oc}"
            if self.on("tensor") or self.on("plan"):
                self.wait(eng, "sLD", f"xnT{tp}_{NCC-1}")
                for tg in ("r4", "r5"):
                    if tg in bprev:
                        self.wait(eng, bprev[tg][0], bprev[tg][1])
                    bprev[tg] = ("sAE", f"evlg{tp}")
                inst = None
                for c in range(NCC):
                    if self.on("tensor"):
                        lhsT = self.wffn_sb[:, c, :]
                        eng.matmul(self.PS[4][0:24, :], lhsT,
                                   self.xnT[:, c, 0:512],
                                   start=(c == 0), stop=(c == NCC - 1))
                        inst = eng.matmul(self.PS[5][0:24, :], lhsT,
                                          self.xnT[:, c, 512:1024],
                                          start=(c == 0), stop=(c == NCC - 1))
                self.inc(inst, "sPE", f"lg{tp}")
            if self.on("scalar") or self.on("plan"):
                self.wait(eng, "sPE", f"lg{tp}")
                inst = None
                if self.on("scalar"):
                    eng.activation(self.lgT[0:24, tp * 1024:tp * 1024 + 512],
                                   self.PS[4][0:24, :], AF.Copy)
                    inst = eng.activation(
                        self.lgT[0:24, tp * 1024 + 512:(tp + 1) * 1024],
                        self.PS[5][0:24, :], AF.Copy)
                self.inc(inst, "sAE", f"evlg{tp}")
        for piece in range(NT):
            if self.on("tensor") or self.on("plan"):
                self.wait(eng, "sAE", "evlg1" if piece == 0 else f"lgte{piece-1}")
                inst = None
                if self.on("tensor"):
                    inst = eng.transpose(
                        self.PS[6][:, 0:24],
                        self.lgT[0:24, piece * 128:(piece + 1) * 128],
                        self.eye_f32[0:24, 0:24])
                self.inc(inst, "sPE", f"lgtr{piece}")
            if self.on("scalar") or self.on("plan"):
                self.wait(eng, "sPE", f"lgtr{piece}")
                inst = None
                if self.on("scalar"):
                    inst = eng.activation(self.logits_tm[:, piece, :],
                                          self.PS[6][:, 0:24], AF.Copy)
                self.inc(inst, "sAE", f"lgte{piece}")
        if self.on("sync") or self.on("plan"):
            self.wait(eng, "sLD", "avs0_47")
            inst = None
            if self.on("sync"):
                inst = eng.dma_start(
                    out=self.akT[:, :, :],
                    in_=bass.AP(self.akav_dram, 0, [[10, 128], [1280, HC], [1, 10]]))
            self.inc(inst, "sLD", "akl", 16)
            inst = None
            if self.on("sync"):
                inst = eng.dma_start(
                    out=self.avT[:, :, :],
                    in_=bass.AP(self.akav_dram, HC * 1280,
                                [[10, 128], [1280, HC], [1, 10]]))
            self.inc(inst, "sLD", "avl", 16)
        if self.on("vector") or self.on("plan"):
            inst = eng.memset(self.av_sb[:, :, :], 0.0) if self.on("vector") else None
            self.inc(inst, "sVE", "avz")
        for h in range(HC):
            if self.on("tensor") or self.on("plan"):
                self.wait(eng, "sLD", "avl")
                if h > 0:
                    self.wait(eng, "sAE", f"avev{h-1}")
                inst = None
                if self.on("tensor"):
                    inst = eng.transpose(self.PS7[0:10, 0:128],
                                         self.avT[:, h, :], self.eye_bf[:, :])
                self.inc(inst, "sPE", f"avtr{h}")
            if self.on("scalar") or self.on("plan"):
                self.wait(eng, "sPE", f"avtr{h}")
                self.wait(eng, "sVE", "avz")
                inst = None
                if self.on("scalar"):
                    inst = eng.activation(self.av_sb[0:10, h, 0:128],
                                          self.PS7[0:10, 0:128], AF.Copy)
                self.inc(inst, "sAE", f"avev{h}")

    def ph3_ffn(self, eng):
        v4 = [[256, 128], [16, NT], [4, 4], [1, 4]]
        if self.on("vector") or self.on("plan"):
            inst = None
            for tt in range(NT):
                if self.on("vector"):
                    eng.scalar_tensor_tensor(
                        self.Z[:, tt * 16:(tt + 1) * 16],
                        self.logits_tm[:, tt, 0:16], self.rinv10[:, tt:tt + 1],
                        self.hb10[:, :], ALU.mult, ALU.add)
                    inst = eng.scalar_tensor_tensor(
                        self.LG[:, tt * 8:(tt + 1) * 8],
                        self.logits_tm[:, tt, 16:24], self.rinv_all[:, tt:tt + 1],
                        self.hb_b[:, 16:24], ALU.mult, ALU.add)
            self.inc(inst, "sVE", "zlg")
            inst = eng.memset(self.lnSv[:, :], 0.0) if self.on("vector") else None
            self.inc(inst, "sVE", "lnsv0")
        if self.on("vector") or self.on("plan"):
            inst = None
            if self.on("vector"):
                self.V(eng, eng.reduce_max(
                    bass.AP(self.M2, 0, [[32, 128], [1, 32]]),
                    bass.AP(self.LG, 0, [[128, 128], [4, 32], [1, 4]]), AX.X))
                inst = eng.tensor_tensor(
                    bass.AP(self.LG, 0, [[128, 128], [4, 32], [1, 4]]),
                    bass.AP(self.LG, 0, [[128, 128], [4, 32], [1, 4]]),
                    bass.AP(self.M2, 0, [[32, 128], [1, 32], [0, 4]]),
                    ALU.subtract)
            self.inc(inst, "sVE", "lgsh")
        if self.on("scalar") or self.on("plan"):
            self.wait(eng, "sVE", "lgsh")
            inst = None
            if self.on("scalar"):
                inst = eng.activation(self.E2[:, :], self.LG[:, :], AF.Exp)
            self.inc(inst, "sAE", "e2")
        if self.on("vector") or self.on("plan"):
            self.wait(eng, "sAE", "e2")
            inst = None
            if self.on("vector"):
                self.V(eng, eng.reduce_sum(
                    bass.AP(self.S2, 0, [[32, 128], [1, 32]]),
                    bass.AP(self.E2, 0, [[128, 128], [4, 32], [1, 4]]), AX.X))
                self.V(eng, eng.reciprocal(self.R2[:, :], self.S2[:, :]))
                inst = eng.tensor_tensor(
                    bass.AP(self.HP, 0, [[128, 128], [4, 32], [1, 4]]),
                    bass.AP(self.E2, 0, [[128, 128], [4, 32], [1, 4]]),
                    bass.AP(self.R2, 0, [[32, 128], [1, 32], [0, 4]]),
                    ALU.mult)
            self.inc(inst, "sVE", "hp")
            self.VM(eng, "hp")
        if self.ph3_stop <= 1:
            return
        for tt in range(NT):
            bi = tt % 2
            rows = slice(tt * 128, (tt + 1) * 128)
            if self.on("sync") or self.on("plan"):
                if tt >= 2:
                    self.wait(eng, "sVE", f"xpre{tt-2}")
                inst = eng.dma_start(out=self.xn3_sb[:, bi, :],
                                     in_=self.xn_dram[rows, :]) if self.on("sync") else None
                self.inc(inst, "sLD", f"xn3a{tt}", 16)
            if self.on("vector") or self.on("plan"):
                self.wait(eng, "sLD", f"xn3a{tt}")
                if tt >= 2:
                    self.wait(eng, "sSP", f"xpsp{tt-2}")
                inst = None
                if self.on("vector"):
                    hp = self.HP
                    xn = self.xn3_sb[:, bi, :]
                    acc = self.acc_sb[:, 0:1024]
                    self.V(eng, eng.tensor_scalar_mul(acc, xn[:, 0:1024],
                                                      hp[:, tt * 8:tt * 8 + 1]))
                    for n in range(1, 4):
                        o = acc if n < 3 else self.xpre_sb[:, bi, :]
                        inst = eng.scalar_tensor_tensor(
                            o, xn[:, n * 1024:(n + 1) * 1024],
                            hp[:, tt * 8 + n:tt * 8 + n + 1],
                            acc, ALU.mult, ALU.add)
                        if n < 3:
                            self.V(eng, inst)
                self.inc(inst, "sVE", f"xpre{tt}")
                self.VM(eng, f"xpre{tt}")
            if self.on("gpsimd") or self.on("plan"):
                self.wait(eng, "sVE", f"xpre{tt}")
                inst = eng.dma_start(out=self.xpre_dram[rows, :],
                                     in_=self.xpre_sb[:, bi, :]) if self.on("gpsimd") else None
                self.inc(inst, "sSP", f"xpsp{tt}", 16)
        if self.on("gpsimd") or self.on("plan"):
            inst = eng.dma_start(out=self.wb_sb[:, :, :],
                                 in_=self.wbranch_t[:, :, :]) if self.on("gpsimd") else None
            self.inc(inst, "sSP", "wb", 16)
        if self.on("sync") or self.on("plan"):
            self.wait(eng, "sSP", f"xpsp{NT-1}")
            for cc in range(8):
                inst = None
                if self.on("sync"):
                    inst = eng.dma_start(
                        out=self.xpreT[:, cc, :],
                        in_=self.xpre_dram[:, cc * 128:(cc + 1) * 128],
                        transpose=True)
                self.inc(inst, "sLD", f"xpT{cc}", 16)
        if self.ph3_stop <= 2:
            return
        for it in range(SINK_ITERS):
            for half in range(2):
                M = self.Mu if half == 0 else self.Mv
                if self.on("vector") or self.on("plan"):
                    if it > 0 or half > 0:
                        self.wait(eng, "sAE",
                                  f"skl{it}_0" if half == 1 else f"skl{it-1}_1")
                    src = self.lnSv if half == 0 else self.lnSu
                    pat = ([[64, 128], [4, NT], [0, 4], [1, 4]] if half == 0
                           else [[64, 128], [4, NT], [1, 4], [0, 4]])
                    epat = (v4 if half == 0
                            else [[256, 128], [16, NT], [1, 4], [4, 4]])
                    gpat = [[64, 128], [4, NT], [1, 4], [0, 4]]
                    gout = [[64, 128], [4, NT], [1, 4]]
                    Madd = self.Mv if half == 0 else self.Mu
                    inst = None
                    if self.on("vector"):
                        if it > 0 or half > 0:
                            self.V(eng, eng.tensor_add(src[:, :], src[:, :],
                                                       Madd[:, :]))
                        self.V(eng, eng.tensor_tensor(bass.AP(self.T0, 0, v4),
                                                      bass.AP(self.Z, 0, v4),
                                                      bass.AP(src, 0, pat),
                                                      ALU.subtract))
                        self.V(eng, eng.reduce_max(bass.AP(M, 0, gout),
                                                   bass.AP(self.T0, 0, epat),
                                                   AX.X))
                        inst = eng.tensor_tensor(bass.AP(self.T0, 0, epat),
                                                 bass.AP(self.T0, 0, epat),
                                                 bass.AP(M, 0, gpat),
                                                 ALU.subtract)
                    self.inc(inst, "sVE", f"skt{it}_{half}")
                if self.on("scalar") or self.on("plan"):
                    self.wait(eng, "sVE", f"skt{it}_{half}")
                    inst = None
                    if self.on("scalar"):
                        inst = eng.activation(self.E[:, :], self.T0[:, :], AF.Exp)
                    self.inc(inst, "sAE", f"ske{it}_{half}")
                if self.on("vector") or self.on("plan"):
                    self.wait(eng, "sAE", f"ske{it}_{half}")
                    dst = self.Su if half == 0 else self.Sv
                    pat = (v4 if half == 0
                           else [[256, 128], [16, NT], [1, 4], [4, 4]])
                    inst = None
                    if self.on("vector"):
                        inst = eng.reduce_sum(
                            bass.AP(dst, 0, [[64, 128], [4, NT], [1, 4]]),
                            bass.AP(self.E, 0, pat), AX.X)
                    self.inc(inst, "sVE", f"sks{it}_{half}")
                if self.on("scalar") or self.on("plan"):
                    self.wait(eng, "sVE", f"sks{it}_{half}")
                    src = self.Su if half == 0 else self.Sv
                    dst = self.lnSu if half == 0 else self.lnSv
                    inst = None
                    if self.on("scalar"):
                        inst = eng.activation(dst[:, :], src[:, :], AF.Ln)
                    self.inc(inst, "sAE", f"skl{it}_{half}")
        if self.on("vector") or self.on("plan"):
            self.wait(eng, "sAE", f"skl{SINK_ITERS-1}_1")
            inst = None
            if self.on("vector"):
                self.V(eng, eng.tensor_add(self.lnSv[:, :], self.lnSv[:, :],
                                           self.Mv[:, :]))
                self.V(eng, eng.tensor_tensor(
                    bass.AP(self.T0, 0, v4), bass.AP(self.Z, 0, v4),
                    bass.AP(self.lnSu, 0, [[64, 128], [4, NT], [1, 4], [0, 4]]),
                    ALU.subtract))
                inst = eng.tensor_tensor(bass.AP(self.T0, 0, v4),
                                         bass.AP(self.T0, 0, v4),
                                         bass.AP(self.lnSv, 0,
                                                 [[64, 128], [4, NT], [0, 4], [1, 4]]),
                                         ALU.subtract)
            self.inc(inst, "sVE", "hraw")
        if self.on("scalar") or self.on("plan"):
            self.wait(eng, "sVE", "hraw")
            inst = None
            if self.on("scalar"):
                inst = eng.activation(self.E[:, :], self.T0[:, :], AF.Exp)
            self.inc(inst, "sAE", "hexp")
        if self.on("vector") or self.on("plan"):
            self.wait(eng, "sAE", "hexp")
            self.wait(eng, "sVE", "hb10")
            v3 = [[256, 128], [16, NT], [1, 16]]
            eyepat = [[16, 128], [0, NT], [1, 16]]
            inst = None
            if self.on("vector"):
                self.V(eng, eng.tensor_tensor(
                    bass.AP(self.T0, 0, v3), bass.AP(self.E, 0, v3),
                    bass.AP(self.eyeP_b, 0, eyepat), ALU.subtract))
                inst = eng.scalar_tensor_tensor(
                    bass.AP(self.Hfin, 0, v3), bass.AP(self.T0, 0, v3),
                    self.gs[:, 0:1], bass.AP(self.eyeP_b, 0, eyepat),
                    ALU.mult, ALU.add)
            self.inc(inst, "sVE", "hfin")
        if self.ph3_stop <= 3:
            return
        for tt in range(NT):
            pa, pb = (0, 1) if tt % 2 == 0 else (2, 3)
            if self.on("tensor") or self.on("plan"):
                self.wait(eng, "sLD", "xpT7")
                self.wait(eng, "sSP", "wb")
                if tt >= 2:
                    self.wait(eng, "sAE", f"ybre{tt-2}")
                inst = None
                for ccb in range(8):
                    if self.on("tensor"):
                        lhsT = self.xpreT[:, ccb, tt * 128:(tt + 1) * 128]
                        eng.matmul(self.PS[pa][:, :], lhsT,
                                   self.wb_sb[:, ccb, 0:512],
                                   start=(ccb == 0), stop=(ccb == 7))
                        inst = eng.matmul(self.PS[pb][:, :], lhsT,
                                          self.wb_sb[:, ccb, 512:1024],
                                          start=(ccb == 0), stop=(ccb == 7))
                self.inc(inst, "sPE", f"ybr{tt}")
            if self.on("scalar") or self.on("plan"):
                self.wait(eng, "sPE", f"ybr{tt}")
                inst = None
                if self.on("scalar"):
                    eng.activation(self.y_br[:, tt, 0:512], self.PS[pa][:, :],
                                   AF.Copy)
                    inst = eng.activation(self.y_br[:, tt, 512:1024],
                                          self.PS[pb][:, :], AF.Copy)
                self.inc(inst, "sAE", f"ybre{tt}")
        if self.dbg_ybr:
            for tt in range(NT):
                if self.on("gpsimd") or self.on("plan"):
                    self.wait(eng, "sAE", f"ybre{tt}")
                    inst = None
                    if self.on("gpsimd"):
                        inst = eng.dma_start(
                            out=self.g_dram[tt * 128:(tt + 1) * 128, 0:1024],
                            in_=self.y_br[:, tt, :])
                    self.inc(inst, "sSP", f"dbgyb{tt}", 16)
            return
        if self.dbg_hfin:
            if self.on("gpsimd") or self.on("plan"):
                self.wait(eng, "sVE", "hfin")
                self.wait(eng, "sVE", "hp")
                inst = None
                if self.on("gpsimd"):
                    eng.dma_start(out=self.g_dram[0:128, 0:256],
                                  in_=self.Hfin[:, :]).then_inc(self.sems["sSP"], 16)
                    eng.dma_start(out=self.g_dram[128:256, 0:128],
                                  in_=self.HP[:, :]).then_inc(self.sems["sSP"], 16)
                    eng.dma_start(out=self.g_dram[256:384, 0:256],
                                  in_=self.Z[:, :]).then_inc(self.sems["sSP"], 16)
                    eng.dma_start(out=self.g_dram[384:512, 0:64],
                                  in_=self.lnSu[:, :]).then_inc(self.sems["sSP"], 16)
                    inst = eng.dma_start(out=self.g_dram[512:640, 0:64],
                                         in_=self.lnSv[:, :])
                if self.mode == "plan":
                    self.cnt["sSP"] += 64
                self.inc(inst, "sSP", "dbghf", 16)
            return
        if self.ph3_stop <= 4:
            return
        for tt in range(NT):
            bi = tt % 2
            rows = slice(tt * 128, (tt + 1) * 128)
            if self.on("sync") or self.on("plan"):
                if tt >= 2:
                    self.wait(eng, "sVE", f"mix{tt-2}")
                inst = eng.dma_start(out=self.xn3_sb[:, bi, :],
                                     in_=self.xn_dram[rows, :]) if self.on("sync") else None
                self.inc(inst, "sLD", f"xn3b{tt}", 16)
            if self.on("vector") or self.on("plan"):
                self.wait(eng, "sLD", f"xn3b{tt}")
                self.wait(eng, "sVE", "hfin")
                self.wait(eng, "sAE", f"ybre{tt}")
                if tt >= 1:
                    self.wait(eng, "sAE", f"silu{tt-1}")
                inst = None
                if self.on("vector"):
                    xn = self.xn3_sb[:, bi, :]
                    for r in range(2):
                        acc = self.acc_sb[:, r * 1024:(r + 1) * 1024]
                        hf = self.Hfin
                        eng.tensor_scalar_mul(
                            acc, xn[:, 0:1024],
                            hf[:, tt * 16 + 4 * r:tt * 16 + 4 * r + 1])
                        for jj in range(1, 4):
                            eng.scalar_tensor_tensor(
                                acc, xn[:, jj * 1024:(jj + 1) * 1024],
                                hf[:, tt * 16 + 4 * r + jj:tt * 16 + 4 * r + jj + 1],
                                acc, ALU.mult, ALU.add)
                        inst = eng.scalar_tensor_tensor(
                            acc, self.y_br[:, tt, :],
                            self.HP[:, tt * 8 + 4 + r:tt * 8 + 4 + r + 1],
                            acc, ALU.mult, ALU.add)
                self.inc(inst, "sVE", f"mix{tt}")
            if self.on("scalar") or self.on("plan"):
                self.wait(eng, "sVE", f"mix{tt}")
                if tt >= 2:
                    self.wait(eng, "sSP", f"gsp{tt-2}")
                inst = None
                if self.on("scalar"):
                    inst = eng.activation(self.g_sb[:, :], self.acc_sb[:, :],
                                          AF.Silu)
                self.inc(inst, "sAE", f"silu{tt}")
            if self.on("gpsimd") or self.on("plan"):
                self.wait(eng, "sAE", f"silu{tt}")
                inst = eng.dma_start(out=self.g_dram[rows, :],
                                     in_=self.g_sb[:, :]) if self.on("gpsimd") else None
                self.inc(inst, "sSP", f"gsp{tt}", 16)

    def ph4_attn(self, eng):
        if self.on("gpsimd") or self.on("plan"):
            for q in range(4):
                inst = None
                if self.on("gpsimd"):
                    inst = eng.dma_start(
                        out=self.wp_sb[:, q * 8:(q + 1) * 8, :],
                        in_=self.wprojT_t[0, :, q * 8:(q + 1) * 8, :])
                self.inc(inst, "sSP", f"wp0_{q}", 16)
        for h in range(HC):
            hb = h % 2
            if self.on("sync") or self.on("plan"):
                if h >= 2:
                    self.wait(eng, "sPE", f"yadj{h-2}_7")
                inst = eng.dma_start(out=self.kT_sb[:, hb, :],
                                     in_=self.qkT_dram[16 + h, :, :]) if self.on("sync") else None
                self.inc(inst, "sLD", f"kT{h}", 16)
                inst = eng.dma_start(out=self.qT_sb[:, hb, :],
                                     in_=self.qkT_dram[h, :, :]) if self.on("sync") else None
                self.inc(inst, "sLD", f"qT{h}", 16)
                for kc in range(16):
                    inst = None
                    if self.on("sync"):
                        inst = eng.dma_start(
                            out=self.v_sb[:, hb, kc, 0:128],
                            in_=self.qkT_dram[32 + h, :, kc * 128:(kc + 1) * 128],
                            transpose=True)
                    self.inc(inst, "sLD", f"v{h}_{kc}", 16)
            if h < 2 and (self.on("vector") or self.on("plan")):
                inst = None
                if self.on("vector"):
                    inst = eng.memset(self.v_sb[:, hb, :, 128:129], 1.0)
                self.inc(inst, "sVE", f"vones{h}")
            if self.on("tensor") or self.on("plan"):
                if h > 0:
                    self.wait(eng, "sAE", f"ae{h-1}")
                if h == 0:
                    self.wait(eng, "sLD", "akl")
                self.wait(eng, "sLD", f"qT{h}")
                inst = None
                for u in range(16):
                    if self.on("tensor"):
                        inst = eng.matmul(self.PS[4][:, 10 * u:10 * u + 10],
                                          self.qT_sb[:, hb, u * 128:(u + 1) * 128],
                                          self.akT[:, h, :], start=True, stop=True,
                                          skip_group_check=True)
                self.inc(inst, "sPE", f"as{h}")
            if self.on("scalar") or self.on("plan"):
                self.wait(eng, "sPE", f"as{h}")
                if h >= 2:
                    self.wait(eng, "sVE", f"apv{h-2}_7")
                inst = None
                for u in range(16):
                    if self.on("scalar"):
                        inst = eng.activation(self.ea_sb[:, hb, 10 * u:10 * u + 10],
                                              self.PS[4][:, 10 * u:10 * u + 10],
                                              AF.Exp, scale=np.float32(SC_ADPT),
                                              accum_out=self.asum_sb[:, hb, u:u + 1])
                self.inc(inst, "sAE", f"ae{h}")
            if self.on("vector") or self.on("plan"):
                self.wait(eng, "sAE", f"ae{h}")
                inst = None
                if self.on("vector"):
                    self.V(eng, eng.reciprocal(self.gra_sb[:, hb, :],
                                               self.asum_sb[:, hb, :]))
                    inst = eng.tensor_scalar_mul(self.gra_sb[:, hb, :],
                                                 self.gra_sb[:, hb, :],
                                                 self.gating_b[:, 0:1])
                self.inc(inst, "sVE", f"gra{h}")
                self.VM(eng, f"gra{h}")
            for s in range(8):
                nkc = 2 * s + 2
                ye, yo = (2, 3) if s % 2 == 0 else (5, 6)
                yprev = (f"yev{h}_{s-2}" if s >= 2 else
                         (f"yev{h-1}_{s+6}" if h > 0 else None))
                skey = f"{h}_{s}"
                for kc in range(nkc + 1):
                    pb = kc % 2
                    if kc < nkc:
                        key = f"{h}_{s}_{kc}"
                        if self.on("tensor") or self.on("plan"):
                            if s == 0 and kc == 0:
                                self.wait(eng, "sLD", f"v{h}_15")
                                self.wait(eng, "sVE", f"vones{min(h, 1)}")
                            prev = (f"pT{h}_{s}_{kc-2}" if kc >= 2 else
                                    (f"pT{h}_{s-1}_{2*s-2+pb}" if s > 0 else
                                     (f"pT{h-1}_7_{14+pb}" if h > 0 else None)))
                            if prev:
                                self.wait(eng, "sAE", prev)
                            inst = None
                            if self.on("tensor"):
                                inst = eng.matmul(
                                    self.PS[pb][:, 0:256],
                                    self.kT_sb[:, hb, kc * 128:(kc + 1) * 128],
                                    self.qT_sb[:, hb, s * 256:(s + 1) * 256],
                                    start=True, stop=True)
                            self.inc(inst, "sPE", f"sT{key}")
                        if self.on("scalar") or self.on("plan"):
                            self.wait(eng, "sPE", f"sT{key}")
                            prev = (f"ym{h}_{s}_{kc-2}" if kc >= 2 else
                                    (f"ym{h}_{s-1}_{2*s-2+pb}" if s > 0 else
                                     (f"ym{h-1}_7_{14+pb}" if h > 0 else None)))
                            if prev:
                                self.wait(eng, "sPE", prev)
                            inst = None
                            if self.on("scalar"):
                                inst = eng.activation(self.pT_sb[:, pb, :],
                                                      self.PS[pb][:, 0:256], AF.Exp,
                                                      scale=np.float32(SC_MAIN),
                                                      bias=np.float32(-20.0))
                            self.inc(inst, "sAE", f"pT{key}")
                        if kc >= nkc - 2 and (self.on("vector") or self.on("plan")):
                            self.wait(eng, "sAE", f"pT{key}")
                            m = self.trapA if kc == nkc - 2 else self.trapB
                            inst = None
                            if self.on("vector"):
                                inst = eng.tensor_mul(self.pT_sb[:, pb, :],
                                                      self.pT_sb[:, pb, :], m[:, :])
                            self.inc(inst, "sVE", f"mk{key}")
                    if kc >= 1:
                        kcy = kc - 1
                        pby = kcy % 2
                        ykey = f"{h}_{s}_{kcy}"
                        if self.on("tensor") or self.on("plan"):
                            if kcy >= nkc - 2:
                                self.wait(eng, "sVE", f"mk{ykey}")
                            else:
                                self.wait(eng, "sAE", f"pT{ykey}")
                            if kcy == 0 and yprev:
                                self.wait(eng, "sVE", yprev)
                            inst = None
                            if self.on("tensor"):
                                eng.matmul(self.PS[ye][:, 0:129],
                                           self.pT_sb[:, pby, 0:128],
                                           self.v_sb[:, hb, kcy, 0:129],
                                           start=(kcy == 0), stop=False,
                                           skip_group_check=True)
                                inst = eng.matmul(self.PS[yo][:, 0:129],
                                                  self.pT_sb[:, pby, 128:256],
                                                  self.v_sb[:, hb, kcy, 0:129],
                                                  start=(kcy == 0), stop=False,
                                                  skip_group_check=True)
                            self.inc(inst, "sPE", f"ym{ykey}")
                p7 = slice(0, 256) if s % 2 == 0 else slice(256, 512)
                if self.on("vector") or self.on("plan"):
                    self.wait(eng, "sPE", f"ym{h}_{s}_{nkc-1}")
                    prevtr = (f"aptr{h}_{s-1}" if s > 0 else
                              (f"aptr{h-1}_7" if h > 0 else None))
                    if prevtr:
                        self.wait(eng, "sPE", prevtr)
                    inst = None
                    if self.on("vector"):
                        a = self.ascr
                        self.V(eng, eng.tensor_mul(a[:, 2:3],
                                                   self.gra_sb[:, hb, 2 * s:2 * s + 1],
                                                   self.PS[ye][:, 128:129]))
                        self.V(eng, eng.tensor_mul(a[:, 3:4],
                                                   self.gra_sb[:, hb, 2 * s + 1:2 * s + 2],
                                                   self.PS[yo][:, 128:129]))
                        self.V(eng, eng.tensor_scalar_mul(
                            self.ap_sb[:, 0:10],
                            self.ea_sb[:, hb, 20 * s:20 * s + 10], a[:, 2:3]))
                        inst = eng.tensor_scalar_mul(
                            self.ap_sb[:, 10:20],
                            self.ea_sb[:, hb, 20 * s + 10:20 * s + 20], a[:, 3:4])
                    self.inc(inst, "sVE", f"apv{skey}")
                    self.VM(eng, f"apv{skey}")
                if self.on("tensor") or self.on("plan"):
                    self.wait(eng, "sVE", f"apv{skey}")
                    prev = (f"apev{h}_{s-2}" if s >= 2 else
                            (f"apev{h-1}_{s+6}" if h > 0 else None))
                    if prev:
                        self.wait(eng, "sAE", prev)
                    inst = None
                    if self.on("tensor"):
                        eng.transpose(self.PS7[0:10, p7.start:p7.start + 128],
                                      self.ap_sb[:, 0:10], self.eye_bf[:, :])
                        inst = eng.transpose(self.PS7[0:10, p7.start + 128:p7.stop],
                                             self.ap_sb[:, 10:20], self.eye_bf[:, :])
                    self.inc(inst, "sPE", f"aptr{skey}")
                if self.on("scalar") or self.on("plan"):
                    self.wait(eng, "sPE", f"aptr{skey}")
                    prev = (f"yadj{h}_{s-2}" if s >= 2 else
                            (f"yadj{h-1}_{s+6}" if h > 0 else None))
                    if prev:
                        self.wait(eng, "sPE", prev)
                    inst = None
                    if self.on("scalar"):
                        inst = eng.activation(self.apT_sb[0:10, s % 2, :],
                                              self.PS7[0:10, p7], AF.Copy)
                    self.inc(inst, "sAE", f"apev{skey}")
                if self.on("tensor") or self.on("plan"):
                    self.wait(eng, "sAE", f"apev{skey}")
                    inst = None
                    if self.on("tensor"):
                        eng.matmul(self.PS[ye][:, 0:129],
                                   self.apT_sb[0:10, s % 2, 0:128],
                                   self.av_sb[0:10, h, :], start=False, stop=True,
                                   skip_group_check=True)
                        inst = eng.matmul(self.PS[yo][:, 0:129],
                                          self.apT_sb[0:10, s % 2, 128:256],
                                          self.av_sb[0:10, h, :], start=False,
                                          stop=True, skip_group_check=True)
                    self.inc(inst, "sPE", f"yadj{skey}")
                if self.on("vector") or self.on("plan"):
                    self.wait(eng, "sPE", f"yadj{skey}")
                    prev = (f"ysp{h}_{s-2}" if s >= 2 else
                            (f"ysp{h-1}_{s+6}" if h > 0 else None))
                    if prev:
                        self.wait(eng, "sSP", prev)
                    inst = None
                    if self.on("vector"):
                        a = self.ascr
                        self.V(eng, eng.reciprocal(a[:, 4:5], self.PS[ye][:, 128:129]))
                        self.V(eng, eng.reciprocal(a[:, 5:6], self.PS[yo][:, 128:129]))
                        self.V(eng, eng.tensor_scalar_mul(self.ysb[:, s % 2, 0, :],
                                                          self.PS[ye][:, 0:128],
                                                          a[:, 4:5]))
                        inst = eng.tensor_scalar_mul(self.ysb[:, s % 2, 1, :],
                                                     self.PS[yo][:, 0:128], a[:, 5:6])
                    self.inc(inst, "sVE", f"yev{skey}")
                    self.VM(eng, f"yev{skey}")
                if self.on("gpsimd") or self.on("plan"):
                    self.wait(eng, "sVE", f"yev{skey}")
                    inst = None
                    if self.on("gpsimd"):
                        eng.dma_start(
                            out=self.y_dram[s * 256:s * 256 + 128,
                                            h * 128:(h + 1) * 128],
                            in_=self.ysb[:, s % 2, 0, :]).then_inc(self.sems["sSP"], 16)
                        inst = eng.dma_start(
                            out=self.y_dram[s * 256 + 128:(s + 1) * 256,
                                            h * 128:(h + 1) * 128],
                            in_=self.ysb[:, s % 2, 1, :])
                    if self.mode == "plan":
                        self.cnt["sSP"] += 16
                    self.inc(inst, "sSP", f"ysp{skey}", 16)

    def ph5_gather(self, eng):
        if self.on("gpsimd") or self.on("plan"):
            self.wait(eng, "sSP", f"ysp{HC-1}_7")
            inst = None
            if self.on("gpsimd"):
                pid = self.nc.partition_id(engines=[mybir.EngineType.Pool])
                slot = pid % 2
                other = 1 - slot
                eng.dma_start(out=self.sh_y[bass.ds(slot, 1), :, :],
                              in_=self.y_dram[:, :]).then_inc(self.sem_poll, 16)
                eng.wait_ge(self.sem_poll, 16)
                eng.dma_start(out=self.sh_f[bass.ds(slot, 1), :],
                              in_=self.ntl[:, :]).then_inc(self.sem_poll, 16)
                eng.wait_ge(self.sem_poll, 32)
                with (eng.register("pf") as pf, eng.register("nr") as nr,
                      eng.register("df") as df, eng.register("pc") as pc):
                    eng.reg_load(nr, self.ntl[0:1, 0:1])
                    eng.reg_mov(df, 1)
                    eng.reg_mov(pc, 32)
                    with eng.While(df):
                        eng.reg_add(pc, pc, 16)
                        eng.dma_start(
                            out=self.ptl[:, :],
                            in_=self.sh_f[bass.ds(other, 1), :]
                        ).then_inc(self.sem_poll, 16)
                        eng.wait_ge(self.sem_poll, pc)
                        eng.reg_load(pf, self.ptl[0:1, 0:1])
                        eng.reg_sub(df, pf, nr)
                inst = eng.nop()
            self.inc(inst, "sCC", "ag", 16)
        if self.on("sync"):
            eng.wait_ge(self.sems["sCC"], self.M["ag"])

    def ph6_proj(self, eng):
        for og in range(2):
            if og == 1 and (self.on("gpsimd") or self.on("plan")):
                self.wait(eng, "sPE", f"pj0_{NT-1}_1")
                for q in range(4):
                    inst = None
                    if self.on("gpsimd"):
                        inst = eng.dma_start(
                            out=self.wp_sb[:, q * 8:(q + 1) * 8, :],
                            in_=self.wprojT_t[1, :, q * 8:(q + 1) * 8, :])
                    self.inc(inst, "sSP", f"wp1_{q}", 16)
            def yt_loads(tp2, og2):
                tb2 = tp2 % 2
                if self.on("sync") or self.on("plan"):
                    prev = None
                    if tp2 >= 2:
                        prev = (og2, 2 * tp2 - 3)
                    elif og2 == 1:
                        prev = (0, NT - 4 + 2 * tp2 + 1)
                    if prev:
                        self.wait(eng, "sPE", f"pj{prev[0]}_{prev[1]}_1")
                    for cc in range(NCC):
                        blk = 0 if cc < 16 else 1
                        coff = (cc % 16) * 128
                        inst = None
                        if self.on("sync"):
                            inst = eng.dma_start(
                                out=self.yT_sb[:, tb2, cc, :],
                                in_=self.sh_y[blk, tp2 * 256:(tp2 + 1) * 256,
                                              coff:coff + 128],
                                transpose=True)
                        self.inc(inst, "sLD", f"yT{og2}_{tp2}_{cc}", 16)

            if og == 0:
                yt_loads(0, 0)
            for tp in range(NT // 2):
                tb = tp % 2
                if tp + 1 < NT // 2:
                    yt_loads(tp + 1, og)
                elif og == 0:
                    yt_loads(0, 1)
                for ti in range(2):
                    tt = 2 * tp + ti
                    rows = slice(tt * 128, (tt + 1) * 128)
                    if self.on("scalar") or self.on("plan"):
                        for ow in range(2):
                            cols = slice(og * 1024 + ow * 512,
                                         og * 1024 + (ow + 1) * 512)
                            if tt >= 2 or og == 1:
                                po, pt = (og, tt - 2) if tt >= 2 else (0, NT - 2 + tt)
                                self.wait(eng, "sOUT", f"os{po}_{pt}_{ow}")
                            inst = None
                            if self.on("scalar"):
                                inst = eng.dma_start(out=self.gx_sb[:, tt % 2, 0, ow, :],
                                                     in_=self.g_dram[rows, cols])
                            self.inc(inst, "sXG", f"g{og}_{tt}_{ow}", 16)
                            inst = None
                            if self.on("scalar"):
                                inst = eng.dma_start(out=self.gx_sb[:, tt % 2, 1, ow, :],
                                                     in_=self.xc[rows, cols])
                            self.inc(inst, "sXG", f"xc{og}_{tt}_{ow}", 16)
                    for ow in range(2):
                        ps = self.PS[2 * (tt % 2) + ow]
                        if self.on("tensor") or self.on("plan"):
                            if ow == 0 and ti == 0:
                                self.wait(eng, "sLD", f"yT{og}_{tp}_{NCC-1}")
                                self.wait(eng, "sSP", f"wp{og}_3")
                            if tt >= 2 or og == 1:
                                po, pt = (og, tt - 2) if tt >= 2 else (0, NT - 2 + tt)
                                self.wait(eng, "sVE", f"cmb{po}_{pt}_{ow}")
                            inst = None
                            for cc in range(NCC):
                                if self.on("tensor"):
                                    inst = eng.matmul(
                                        ps[:, :], self.yT_sb[:, tb, cc,
                                                             ti * 128:(ti + 1) * 128],
                                        self.wp_sb[:, cc, ow * 512:(ow + 1) * 512],
                                        start=(cc == 0), stop=(cc == NCC - 1))
                            self.inc(inst, "sPE", f"pj{og}_{tt}_{ow}")
                        if self.on("vector") or self.on("plan"):
                            self.wait(eng, "sPE", f"pj{og}_{tt}_{ow}")
                            self.wait(eng, "sXG", f"g{og}_{tt}_{ow}")
                            self.wait(eng, "sXG", f"xc{og}_{tt}_{ow}")
                            inst = None
                            if self.on("vector"):
                                self.V(eng, eng.tensor_mul(
                                    self.t6scr[:, :], ps[:, :],
                                    self.gx_sb[:, tt % 2, 0, ow, :]))
                                inst = eng.tensor_add(self.gx_sb[:, tt % 2, 1, ow, :],
                                                      self.t6scr[:, :],
                                                      self.gx_sb[:, tt % 2, 1, ow, :])
                            self.inc(inst, "sVE", f"cmb{og}_{tt}_{ow}")
                            self.VM(eng, f"cmb{og}_{tt}_{ow}")
                        if self.on("sync") or self.on("plan"):
                            self.wait(eng, "sVE", f"cmb{og}_{tt}_{ow}")
                            cols = slice(og * 1024 + ow * 512,
                                         og * 1024 + (ow + 1) * 512)
                            inst = None
                            if self.on("sync"):
                                inst = eng.dma_start(out=self.out[rows, cols],
                                                     in_=self.gx_sb[:, tt % 2, 1, ow, :])
                            if self.mode == "plan":
                                self.cnt["sOUT"] += 16
                                self.M[f"os{og}_{tt}_{ow}"] = self.cnt["sOUT"]
                            elif inst is not None:
                                inst.then_inc(self.sems["sOUT"], 16)
        final = f"os1_{NT-1}_1"
        if self.on("sync"):
            eng.wait_ge(self.sems["sOUT"], self.M[final])
        if self.on("gpsimd"):
            eng.wait_ge(self.sems["sOUT"], self.M[final])


_NC_CACHE = None


def _build():
    global _NC_CACHE
    if _NC_CACHE is None:
        _NC_CACHE = Prog().build()
    return _NC_CACHE


def kernel(L0_attn_x, L0_attn_w_attn, L0_attn_w_proj, L0_attn_adapter_wte,
           L0_attn_gating, L0_ffn_W_res, L0_ffn_W_pre, L0_ffn_W_post,
           L0_ffn_H_res_base, L0_ffn_H_pre_base, L0_ffn_H_post_base,
           L0_ffn_gate, L0_ffn_W_branch):
    x = np.asarray(L0_attn_x, dtype=np.float32)
    w_attn = np.asarray(L0_attn_w_attn, dtype=np.float32)
    w_proj = np.asarray(L0_attn_w_proj, dtype=np.float32)
    wte = np.asarray(L0_attn_adapter_wte, dtype=np.float32)
    gating = np.ascontiguousarray(
        np.asarray(L0_attn_gating, dtype=np.float32).reshape(1, 1))
    W_res = np.asarray(L0_ffn_W_res, dtype=np.float32)
    W_pre = np.asarray(L0_ffn_W_pre, dtype=np.float32)
    W_post = np.asarray(L0_ffn_W_post, dtype=np.float32)
    Hres_b = np.asarray(L0_ffn_H_res_base, dtype=np.float32)
    Hpre_b = np.asarray(L0_ffn_H_pre_base, dtype=np.float32)
    Hpost_b = np.asarray(L0_ffn_H_post_base, dtype=np.float32)
    gate = np.ascontiguousarray(
        np.asarray(L0_ffn_gate, dtype=np.float32).reshape(1, 1))
    W_br = np.asarray(L0_ffn_W_branch, dtype=np.float32)

    nc = _build()
    nonce_arr = np.full((1, 64), np.random.randint(1, 2**31 - 1), np.int32)
    wbranch_t = np.ascontiguousarray(W_br.reshape(8, 128, 1024).transpose(1, 0, 2))
    wteT_t = np.ascontiguousarray(wte.reshape(10, 32, 128).transpose(2, 1, 0))
    in_maps = []
    for core in range(8):
        b, j = core >> 1, core & 1
        perm = np.concatenate([
            np.r_[(2 * i) * 128:(2 * i) * 128 + 64,
                  (2 * i + 1) * 128:(2 * i + 1) * 128 + 64,
                  (2 * i) * 128 + 64:(2 * i) * 128 + 128,
                  (2 * i + 1) * 128 + 64:(2 * i + 1) * 128 + 128]
            for i in range(8)])
        rows = [w_attn[g * C + 2048 * j: g * C + 2048 * (j + 1)] for g in range(3)]
        w_slice = np.concatenate([rows[0][perm], rows[1][perm], rows[2]], axis=0)
        wqkvT_t = np.ascontiguousarray(
            w_slice.reshape(48, 128, 32, 128).transpose(0, 3, 2, 1))
        wp_rows = w_proj[2048 * j: 2048 * (j + 1)]
        wprojT_t = np.ascontiguousarray(
            wp_rows.reshape(2, 1024, 32, 128).transpose(0, 3, 2, 1))
        iorder = [0, 1, 2, 3] if j == 0 else [2, 3, 0, 1]
        W_all = np.concatenate([
            W_res.reshape(4, 4, C)[iorder].reshape(16, C),
            W_pre, W_post[iorder]], axis=0)
        wffnT_t = np.ascontiguousarray(W_all.reshape(24, 32, 128).transpose(2, 1, 0))
        hbase = np.ascontiguousarray(np.concatenate(
            [Hres_b[iorder].reshape(16), Hpre_b, Hpost_b[iorder]]).reshape(1, 24))
        eyeP = np.ascontiguousarray(
            np.eye(4, dtype=np.float32)[iorder].reshape(1, 16))
        in_maps.append({
            "nonce": nonce_arr,
            "x_full": np.ascontiguousarray(x[b]),
            "xc": np.ascontiguousarray(x[b][:, 2048 * j: 2048 * (j + 1)]),
            "wqkvT_t": wqkvT_t,
            "wprojT_t": wprojT_t,
            "wbranch_t": wbranch_t,
            "wffnT_t": wffnT_t,
            "wteT_t": wteT_t,
            "hbase": hbase,
            "eyeP": eyeP,
            "gating": gating,
            "gate": gate,
        })
    res = run_bass_kernel_spmd(nc, in_maps, core_ids=list(range(8)))
    out = np.empty((B, T, C), dtype=np.float32)
    for core in range(8):
        b, j = core >> 1, core & 1
        out[b][:, 2048 * j: 2048 * (j + 1)] = res.results[core]["out"]
    return out


# version 11
# speedup vs baseline: 1.0427x; 1.0137x over previous
import numpy as np
import concourse.bass as bass
import concourse.mybir as mybir
from concourse.bass_utils import run_bass_kernel_spmd

F32 = mybir.dt.float32
BF16 = mybir.dt.bfloat16
AF = mybir.ActivationFunctionType
ALU = mybir.AluOpType
AX = mybir.AxisListType

B = 4
T = 2048
C = 4096
HC = 16
NT = T // 128
NCC = C // 128
LN_EPS = 1e-5
RMS_EPS = 1.1920929e-07
SINK_ITERS = 50
SC_MAIN = float(np.float32(1.0 / np.sqrt(np.float32(128.0))))
SC_ADPT = 1.0 / 64.0
CB = 98304


def _consts():
    theta = 1.0 / (10000.0 ** (np.arange(0, 128, 2, dtype=np.float64) / 128.0))
    idx = np.outer(np.arange(T, dtype=np.float64), theta)
    cos2 = np.cos(idx).astype(np.float32)
    sin2 = np.sin(idx).astype(np.float32)
    cosT = np.ascontiguousarray(np.concatenate([cos2, cos2], 1).T)
    sinT = np.concatenate([sin2, sin2], 1).T.copy()
    import ml_dtypes
    kl = np.arange(128)[:, None]
    ql = np.arange(256)[None, :]
    trapA = (kl <= ql).astype(ml_dtypes.bfloat16)
    trapB = ((kl + 128) <= ql).astype(ml_dtypes.bfloat16)
    trapAB = np.concatenate([trapA, trapB], axis=1)
    return (cosT, np.ascontiguousarray(sinT), trapAB,
            np.eye(128, dtype=ml_dtypes.bfloat16),
            np.eye(128, dtype=np.float32))


class Prog:

    def __init__(self, nphase=99, dump=None):
        self.nphase = nphase
        self.ph3_stop = 99
        self.dbg_ybr = False
        self.dbg_hfin = False
        self.dump = dump
        nc = bass.Bass("TRN2", target_bir_lowering=False,
                       detect_race_conditions=False)
        self.nc = nc
        d = nc.dram_tensor
        self.x_full = d("x_full", [T, C], F32, kind="ExternalInput")
        self.xc = d("xc", [T, 2048], F32, kind="ExternalInput")
        self.wqkvT_t = d("wqkvT_t", [48, 128, NCC, 128], F32, kind="ExternalInput")
        self.wprojT_t = d("wprojT_t", [2, 128, NCC, 1024], F32, kind="ExternalInput")
        self.wbranch_t = d("wbranch_t", [128, 8, 1024], F32, kind="ExternalInput")
        self.wffnT_t = d("wffnT_t", [128, NCC, 24], F32, kind="ExternalInput")
        self.wteT_t = d("wteT_t", [128, NCC, 10], F32, kind="ExternalInput")
        self.hbase = d("hbase", [1, 24], F32, kind="ExternalInput")
        self.eyeP = d("eyeP", [1, 16], F32, kind="ExternalInput")
        self.gating = d("gating", [1, 1], F32, kind="ExternalInput")
        self.gate = d("gate", [1, 1], F32, kind="ExternalInput")
        self.out = d("out", [T, 2048], F32, kind="ExternalOutput")
        self.dbg = None
        self.xn_dram = d("xn_dram", [T, C], BF16)
        self.qkT_dram = d("qkT_dram", [48, 128, T], BF16)
        self.xpre_dram = d("xpre_dram", [T, 1024], BF16)
        self.y_dram = d("y_dram", [T, 2048], BF16)
        self.sh_y = d("sh_y", [2, T, 2048], BF16, addr_space="Shared")
        self.sh_f = d("sh_f", [2, 64], mybir.dt.int32, addr_space="Shared")
        self.nonce = d("nonce", [1, 64], mybir.dt.int32, kind="ExternalInput")
        self.g_dram = d("g_dram", [T, 2048], F32)
        self.akav_dram = d("akav_dram", [2, HC, 128, 10], BF16)
        cosT, sinNT, trapAB, eye_bf, eye_f32 = _consts()
        self.cosT_d = nc.inline_tensor(cosT, "cosT_d")
        self.sinP_d = nc.inline_tensor(sinNT, "sinP_d")
        self.trapAB_d = nc.inline_tensor(trapAB, "trapAB_d")
        self.eyebf_d = nc.inline_tensor(eye_bf, "eyebf_d")
        self.eyef32_d = nc.inline_tensor(eye_f32, "eyef32_d")

        A = nc.alloc_sbuf_tensor
        self.trapAB = A("trapAB", [128, 512], BF16)
        self.eye_bf = A("eye_bf", [128, 128], BF16)
        self.eye_f32 = A("eye_f32", [128, 128], F32)
        self.hb_b = A("hb_b", [128, 24], F32)
        self.hb10 = A("hb10", [128, 16], F32)
        self.eyeP_b = A("eyeP_b", [128, 16], F32)
        self.gating_b = A("gating_b", [128, 1], F32)
        self.gate_b = A("gate_b", [128, 1], F32)
        self.gs = A("gs", [128, 1], F32)
        self.rinv_all = A("rinv_all", [128, NT], F32)
        self.rinv10 = A("rinv10", [128, NT], F32)
        self.logits_tm = A("logits_tm", [128, NT, 24], F32)
        self.akT = A("akT", [128, HC, 10], BF16)
        self.avT = A("avT", [128, HC, 10], BF16)
        self.av_sb = A("av_sb", [16, HC, 129], BF16)
        self.wffn_sb = A("wffn_sb", [128, NCC, 24], BF16)
        self.wte_sb = A("wte_sb", [128, NCC, 10], BF16)
        self.s_tiny = A("s_tiny", [128, 16], F32)
        self.ntl = A("ntl", [1, 64], mybir.dt.int32)
        self.ptl = A("ptl", [1, 64], mybir.dt.int32)

        self.arena = A("arena", [128, 159744], mybir.dt.uint8)
        base = nc.lookup_mloc(self.arena).addr
        self._b = base
        P = self._place
        self.sqscr = P("sqscr", 0, [128, 4096], BF16)
        self.xnT = P("xnT", 0, [128, NCC, 1024], BF16)
        self.y_br = P("y_br", 0, [128, NT, 1024], F32)
        self.wp_sb = P("wp_sb", 0, [128, NCC, 1024], BF16)
        self.w_col = P("w_col", 65536, [128, 4, NCC, 128], BF16)
        self.xpreT = P("xpreT", CB + 28672, [128, 8, T], BF16)
        self.yT_sb = P("yT_sb", 65536, [128, 2, NCC, 256], BF16)
        self.gx_sb = P("gx_sb", CB, [128, 2, 2, 2, 512], F32)
        self.t6scr = P("t6scr", CB + 16384, [128, 512], F32)
        self.x_sb = P("x_sb", CB, [128, 2, C], F32)
        self.xn1_sb = P("xn1_sb", CB + 32768, [128, 2, C], BF16)
        self.lgT = P("lgT", CB, [24, T], F32)
        self.cosT = P("cosT", CB + 8192, [128, T], F32)
        self.sinP = P("sinP", CB + 16384, [128, T], F32)
        self.cscr = P("cscr", CB + 24576, [128, 512], F32)
        self.s1scr = P("s1scr", CB + 26624, [128, 512], F32)
        self.ropeE = P("ropeE", CB + 28672, [128, 1024], BF16)
        self.ropeO = P("ropeO", CB + 30720, [128, 1024], BF16)
        self.akmix = P("akmix", CB + 32768 - 128, [128, 2, 10], BF16)
        self.Z = P("Z", CB, [128, NT * 16], F32)
        self.T0 = P("T0", CB + 1024, [128, NT * 16], F32)
        self.E = P("E", CB + 2048, [128, NT * 16], F32)
        self.Su = P("Su", CB + 3072, [128, NT * 4], F32)
        self.lnSu = P("lnSu", CB + 3328, [128, NT * 4], F32)
        self.Sv = P("Sv", CB + 3584, [128, NT * 4], F32)
        self.lnSv = P("lnSv", CB + 3840, [128, NT * 4], F32)
        self.LG = P("LG", CB + 4096, [128, NT * 8], F32)
        self.E2 = P("E2", CB + 4608, [128, NT * 8], F32)
        self.S2 = P("S2", CB + 5120, [128, NT * 2], F32)
        self.R2 = P("R2", CB + 5248, [128, NT * 2], F32)
        self.HP = P("HP", CB + 5376, [128, NT * 8], F32)
        self.Hfin = P("Hfin", CB + 5888, [128, NT * 16], F32)
        self.Mu = P("Mu", CB + 6912, [128, NT * 4], F32)
        self.Mv = P("Mv", CB + 7168, [128, NT * 4], F32)
        self.lnR = P("lnR", CB + 7424, [128, NT * 4], F32)
        self.M2 = P("M2", CB + 7680, [128, NT * 2], F32)
        self.xpre_sb = P("xpre_sb", CB + 8192, [128, 2, 1024], BF16)
        self.wb_sb = P("wb_sb", CB + 12288, [128, 8, 1024], BF16)
        self.xn3_sb = P("xn3_sb", CB + 28672, [128, 2, C], BF16)
        self.acc_sb = P("acc_sb", CB + 45056, [128, 2048], F32)
        self.g_sb = P("g_sb", CB + 53248, [128, 2048], F32)
        self.kT_sb = P("kT_sb", 65536, [128, 2, T], BF16)
        self.qT_sb = P("qT_sb", 73728, [128, 2, T], BF16)
        self.v_sb = P("v_sb", 81920, [128, 2, 16, 144], BF16)
        self.pT_sb = P("pT_sb", 91136, [128, 2, 2, 256], BF16)
        self.ea_sb = P("ea_sb", 95328, [128, 2, 160], F32)
        self.asum_sb = P("asum_sb", 96608, [128, 2, 16], F32)
        self.gra_sb = P("gra_sb", 96736, [128, 2, 16], F32)
        self.ap_sb = P("ap_sb", 93184, [128, 20], BF16)
        self.apT_sb = P("apT_sb", 93248, [16, 2, 256], BF16)
        self.ysb = P("ysb", 94272, [128, 2, 2, 128], BF16)
        self.ascr = P("ascr", 95296, [128, 8], F32)

        self.PSA = nc.alloc_psum_tensor("PSA", [128, 1024], F32)
        self.PSB = nc.alloc_psum_tensor("PSB", [128, 1024], F32)
        _ps4 = nc.alloc_psum_tensor("PS4", [128, 512], F32)
        _ps5 = nc.alloc_psum_tensor("PS5", [128, 512], F32)
        _ps6 = nc.alloc_psum_tensor("PS6", [128, 512], F32)
        self.PS7 = nc.alloc_psum_tensor("PS7", [128, 1024], BF16)
        self.PS = [self.PSA[:, 0:512], self.PSA[:, 512:1024],
                   self.PSB[:, 0:512], self.PSB[:, 512:1024],
                   _ps4[:, :], _ps5[:, :], _ps6[:, :]]

        self.sems = {}
        for name in ["sLD", "sSP", "sPE", "sAE", "sVE", "sOUT", "sCC", "sXG"]:
            self.sems[name] = nc.alloc_semaphore(name)
        self.sem_poll = nc.alloc_semaphore("sPOLL")
        self.sem_vc = nc.alloc_semaphore("sVC")
        self._vc = 0
        self.sem_ac = nc.alloc_semaphore("sAC")
        self._ac = 0
        self.sem_bar = nc.alloc_semaphore("sBAR")
        self._nbar = 0
        self.M = {}
        self.cnt = {k: 0 for k in self.sems}
        self.mode = "plan"

    def _place(self, name, off, shape, dtype):
        nbytes = int(np.prod(shape[1:])) * mybir.dt.size(dtype)
        assert off + nbytes <= 159744, (name, off, nbytes)
        return self.nc.alloc_sbuf_tensor_at(name, shape, dtype,
                                            offset=self._b + off)

    def inc(self, inst, sem, key, n=1):
        if self.mode == "plan":
            self.cnt[sem] += n
            assert key not in self.M, key
            self.M[key] = self.cnt[sem]
        else:
            assert inst is not None, f"milestone {key} has no instruction"
            inst.then_inc(self.sems[sem], n)
        return inst

    def V(self, eng, inst):
        if self.mode == "vector":
            self._vc += 1
            inst.then_inc(self.sem_vc, 1)
            eng.wait_ge(self.sem_vc, self._vc)
        return inst

    def SA(self, eng, inst):
        if self.mode == "scalar":
            self._ac += 1
            inst.then_inc(self.sem_ac, 1)
            eng.wait_ge(self.sem_ac, self._ac)
        return inst

    def VM(self, eng, key):
        if self.mode == "vector":
            eng.wait_ge(self.sems["sVE"], self.M[key])

    def wait(self, eng, sem, key):
        if self.mode == "plan":
            return
        eng.wait_ge(self.sems[sem], self.M[key])

    def on(self, which):
        return self.mode == which

    def build(self):
        with self.nc.Block() as block:
            self.mode = "plan"
            self._walk(None)
            for mode in ["sync", "tensor", "scalar", "vector", "gpsimd"]:
                self.mode = mode
                getattr(block, mode)(lambda eng: self._walk(eng))
        return self.nc

    def _walk(self, eng):
        phases = [self.ph0_consts, self.ph1_ln, self.ph2_qkv, self.ph34,
                  self.ph5_gather, self.ph6_proj]
        for i, fn in enumerate(phases):
            if i > self.nphase:
                break
            fn(eng)
            if i < len(phases) - 1:
                self.barrier(eng, i)
        if self.dump is not None:
            self.ph_dump(eng)

    def ph34(self, eng):
        self._c4 = {}
        self.ph3_front(eng)
        mix_sched = {11: [0, 1, 2], 12: [3, 4, 5], 13: [6, 7, 8],
                     14: [9, 10, 11], 15: [12, 13, 14, 15]}
        for h in range(HC):
            self.ph4_head(eng, h)
            if h < 10:
                for k in range(5):
                    it = (10 * h + 2 * k) // 2
                    self.ph3_sink(eng, it, 0)
                    self.ph3_sink(eng, it, 1)
            elif h == 10:
                self.ph3_fin(eng)
            else:
                for tt in mix_sched[h]:
                    self.ph3_mix(eng, tt)

    def ph_dump(self, eng):
        src = getattr(self, self.dump)
        n = int(np.prod(src.shape))
        if self.dbg is None:
            self.dbg = self.nc.dram_tensor("dbg", [1, n], src.dtype,
                                           kind="ExternalOutput")
        if self.on("gpsimd") or self.on("plan"):
            inst = None
            if self.on("gpsimd"):
                inst = eng.dma_start(out=self.dbg[0, :], in_=src.ap().flatten())
            self.inc(inst, "sSP", "dumpd", 16)
            self.wait(eng, "sSP", "dumpd")

    def barrier(self, eng, i):
        marks = {"sync": "sLD", "tensor": "sPE", "scalar": "sAE",
                 "vector": "sVE", "gpsimd": "sSP"}
        if self.mode == "plan":
            self._nbar += 1
            for e, sm in marks.items():
                self.M[f"bar{i}_{e}"] = self.cnt[sm]
            self.M[f"bar{i}"] = 5 * self._nbar
            return
        eng.wait_ge(self.sems[marks[self.mode]], self.M[f"bar{i}_{self.mode}"])
        eng.nop().then_inc(self.sem_bar, 1)
        eng.wait_ge(self.sem_bar, self.M[f"bar{i}"])

    def ph0_consts(self, eng):
        if self.on("sync") or self.on("plan"):
            loads = [
                (self.trapAB[:, :], self.trapAB_d[:, :]),
                (self.trapAB[:, :], self.trapAB_d[:, :]),
                (self.eye_bf[:, :], self.eyebf_d[:, :]),
                (self.eye_f32[:, :], self.eyef32_d[:, :]),
                (self.hb_b[:, :], bass.AP(self.hbase, 0, [[0, 128], [1, 24]])),
                (self.eyeP_b[:, :], bass.AP(self.eyeP, 0, [[0, 128], [1, 16]])),
                (self.gating_b[:, :], bass.AP(self.gating, 0, [[0, 128], [1, 1]])),
                (self.gate_b[:, :], bass.AP(self.gate, 0, [[0, 128], [1, 1]])),
            ]
            for i, (dst, src) in enumerate(loads):
                inst = eng.dma_start(out=dst, in_=src) if self.on("sync") else None
                self.inc(inst, "sLD", f"c{i}", 16)
        if self.on("gpsimd") or self.on("plan"):
            casts = [(self.wffn_sb[:, :, :], self.wffnT_t[:, :, :]),
                     (self.wte_sb[:, :, :], self.wteT_t[:, :, :]),
                     (self.ntl[:, :], self.nonce[:, :])]
            for i, (dst, src) in enumerate(casts):
                inst = eng.dma_start(out=dst, in_=src) if self.on("gpsimd") else None
                self.inc(inst, "sSP", f"cw{i}", 16)
        if self.on("scalar") or self.on("plan"):
            self.wait(eng, "sLD", "c7")
            inst = None
            if self.on("scalar"):
                inst = eng.activation(self.s_tiny[:, 0:1], self.gate_b[:, :],
                                      AF.Exp, scale=np.float32(-1.0))
            self.inc(inst, "sAE", "gsx")
        if self.on("vector") or self.on("plan"):
            self.wait(eng, "sAE", "gsx")
            self.wait(eng, "sLD", "c4")
            inst = None
            if self.on("vector"):
                self.V(eng, eng.tensor_scalar_add(self.s_tiny[:, 1:2],
                                                  self.s_tiny[:, 0:1], 1.0))
                eng.reciprocal(self.gs[:, :], self.s_tiny[:, 1:2])
                inst = eng.tensor_scalar_mul(self.hb10[:, :], self.hb_b[:, 0:16], 10.0)
            self.inc(inst, "sVE", "hb10")

    def ph1_ln(self, eng):
        inv_c = 1.0 / C
        for tp in range(NT // 2):
            uu = (2 * tp, 2 * tp + 1)
            if self.on("sync") or self.on("plan"):
                for tt in uu:
                    bi = tt % 2
                    if tt >= 2:
                        self.wait(eng, "sVE", f"xn{tt-2}")
                        self.wait(eng, "sAE", f"rsq{tt-2}")
                    inst = eng.dma_start(out=self.x_sb[:, bi, :],
                                         in_=self.x_full[tt * 128:(tt + 1) * 128, :]) if self.on("sync") else None
                    self.inc(inst, "sLD", f"x{tt}", 16)
            if self.on("vector") or self.on("plan"):
                st = self.s_tiny
                for tt in uu:
                    bi = tt % 2; o = 8 * bi
                    self.wait(eng, "sLD", f"x{tt}")
                    if self.on("vector"):
                        self.V(eng, eng.reduce_sum(st[:, o:o+1],
                                                   self.x_sb[:, bi, :], AX.X))
                for tt in uu:
                    bi = tt % 2; o = 8 * bi
                    self.wait(eng, "sAE", f"sq{tt}")
                    inst = None
                    if self.on("vector"):
                        self.V(eng, eng.tensor_scalar_mul(st[:, o+2:o+3],
                                                          st[:, o:o+1], inv_c))
                        self.V(eng, eng.tensor_mul(st[:, o+3:o+4], st[:, o+2:o+3],
                                                   st[:, o+2:o+3]))
                        self.V(eng, eng.tensor_scalar_mul(st[:, o+4:o+5],
                                                          st[:, o+1:o+2], inv_c))
                        inst = eng.tensor_sub(st[:, o+5:o+6], st[:, o+4:o+5],
                                              st[:, o+3:o+4])
                    self.inc(inst, "sVE", f"var{tt}")
                for tt in uu:
                    bi = tt % 2; o = 8 * bi
                    self.wait(eng, "sAE", f"rstd{tt}")
                    if tt >= 2:
                        self.wait(eng, "sSP", f"xnsp{tt-2}")
                    inst = None
                    if self.on("vector"):
                        inst = eng.tensor_scalar(self.xn1_sb[:, bi, :],
                                                 self.x_sb[:, bi, :],
                                                 st[:, o+2:o+3], st[:, o+7:o+8],
                                                 ALU.subtract, ALU.mult)
                    self.inc(inst, "sVE", f"xn{tt}")
            if self.on("scalar") or self.on("plan"):
                st = self.s_tiny
                for tt in uu:
                    bi = tt % 2; o = 8 * bi
                    self.wait(eng, "sLD", f"x{tt}")
                    inst = None
                    if self.on("scalar"):
                        inst = eng.activation(self.sqscr[:, :], self.x_sb[:, bi, :],
                                              AF.Square, accum_out=st[:, o+1:o+2])
                    self.inc(inst, "sAE", f"sq{tt}")
                for tt in uu:
                    bi = tt % 2; o = 8 * bi
                    self.wait(eng, "sVE", f"var{tt}")
                    inst = None
                    if self.on("scalar"):
                        self.SA(eng, eng.activation(st[:, o+6:o+7], st[:, o+5:o+6],
                                                    AF.Ln, bias=np.float32(LN_EPS)))
                        inst = eng.activation(st[:, o+7:o+8], st[:, o+6:o+7],
                                              AF.Exp, scale=np.float32(-0.5))
                    self.inc(inst, "sAE", f"rstd{tt}")
                for tt in uu:
                    bi = tt % 2; o = 8 * bi
                    self.wait(eng, "sVE", f"xn{tt}")
                    inst = None
                    if self.on("scalar"):
                        inst = eng.activation(self.sqscr[:, :], self.xn1_sb[:, bi, :],
                                              AF.Square, accum_out=st[:, o+1:o+2])
                    self.inc(inst, "sAE", f"rsq{tt}")
                for tt in uu:
                    bi = tt % 2; o = 8 * bi
                    if self.on("scalar"):
                        eng.wait_ge(self.sems["sAE"], self.M[f"rsq{tt}"])
                    inst = None
                    if self.on("scalar"):
                        self.SA(eng, eng.activation(st[:, o+6:o+7], st[:, o+1:o+2],
                                                    AF.Ln, scale=np.float32(inv_c),
                                                    bias=np.float32(RMS_EPS)))
                        inst = eng.activation(self.rinv_all[:, tt:tt + 1],
                                              st[:, o+6:o+7], AF.Exp,
                                              scale=np.float32(-0.5))
                    self.inc(inst, "sAE", f"rinv{tt}")
            if self.on("gpsimd") or self.on("plan"):
                for tt in uu:
                    bi = tt % 2
                    self.wait(eng, "sVE", f"xn{tt}")
                    inst = eng.dma_start(out=self.xn_dram[tt * 128:(tt + 1) * 128, :],
                                         in_=self.xn1_sb[:, bi, :]) if self.on("gpsimd") else None
                    self.inc(inst, "sSP", f"xnsp{tt}", 16)
        if self.on("vector") or self.on("plan"):
            self.wait(eng, "sAE", f"rinv{NT-1}")
            inst = None
            if self.on("vector"):
                inst = eng.tensor_scalar_mul(self.rinv10[:, :], self.rinv_all[:, :],
                                             10.0)
            self.inc(inst, "sVE", "rinvten")

    def ph2_qkv(self, eng):
        if self.on("sync") or self.on("plan"):
            for i, (dst, src) in enumerate([(self.cosT, self.cosT_d),
                                            (self.sinP, self.sinP_d)]):
                inst = eng.dma_start(out=dst[:, :], in_=src[:, :]) if self.on("sync") else None
                self.inc(inst, "sLD", f"rt{i}", 16)
        bprev = {}
        rprev = {}
        akprev = None
        for tp in range(2):
            if self.on("sync") or self.on("plan"):
                if tp == 1:
                    self.wait(eng, "sPE", "lg0")
                for cc in range(NCC):
                    src = self.xn_dram[tp * 1024:(tp + 1) * 1024,
                                       cc * 128:(cc + 1) * 128]
                    inst = None
                    if self.on("sync"):
                        inst = eng.dma_start(out=self.xnT[:, cc, :], in_=src,
                                             transpose=True)
                    self.inc(inst, "sLD", f"xnT{tp}_{cc}", 16)
            for p in range(16):
                is_k = p >= 8
                sa, sb = (0, 1) if p % 2 == 0 else (2, 3)
                if self.on("gpsimd") or self.on("plan"):
                    if p >= 2:
                        self.wait(eng, "sPE", f"qk{tp}_{p-2}_1")
                    elif tp == 1:
                        self.wait(eng, "sPE", "lg0")
                    for z, slot in ((0, sa), (1, sb)):
                        inst = None
                        if self.on("gpsimd"):
                            inst = eng.dma_start(out=self.w_col[:, slot, :, :],
                                                 in_=self.wqkvT_t[2 * p + z, :, :, :])
                        self.inc(inst, "sSP", f"wc{tp}_{p}_{z}", 16)
                for win in range(2):
                    pse = self.PS[2 * win]
                    pso = self.PS[2 * win + 1]
                    key = f"qk{tp}_{p}_{win}"
                    rider = is_k and tp == 0 and win == 0
                    if self.on("tensor") or self.on("plan"):
                        self.wait(eng, "sSP", f"wc{tp}_{p}_{1}")
                        self.wait(eng, "sLD", f"xnT{tp}_{NCC-1}")
                        tag = f"b{win}"
                        if tag in bprev:
                            self.wait(eng, bprev[tag][0], bprev[tag][1])
                        bprev[tag] = ("sVE", f"rp{key}")
                        if rider:
                            for tg in ("r4", "r5"):
                                if tg in bprev:
                                    self.wait(eng, bprev[tg][0], bprev[tg][1])
                                bprev[tg] = ("sAE", f"akm{tp}_{p}")
                        inst = None
                        for c in range(NCC):
                            if self.on("tensor"):
                                eng.matmul(pse[:, :], self.w_col[:, sa, c, :],
                                           self.xnT[:, c, win * 512:(win + 1) * 512],
                                           start=(c == 0), stop=(c == NCC - 1))
                                inst = eng.matmul(
                                    pso[:, :], self.w_col[:, sb, c, :],
                                    self.xnT[:, c, win * 512:(win + 1) * 512],
                                    start=(c == 0), stop=(c == NCC - 1))
                                if rider:
                                    eng.matmul(self.PS[4][:, 0:10],
                                               self.w_col[:, sa, c, :],
                                               self.wte_sb[:, c, :],
                                               start=(c == 0), stop=(c == NCC - 1))
                                    inst = eng.matmul(self.PS[5][:, 0:10],
                                                      self.w_col[:, sb, c, :],
                                                      self.wte_sb[:, c, :],
                                                      start=(c == 0),
                                                      stop=(c == NCC - 1))
                        self.inc(inst, "sPE", key)
                    if self.on("vector") or self.on("plan"):
                        self.wait(eng, "sPE", key)
                        if win == 0 and tp == 0 and p == 0:
                            self.wait(eng, "sLD", "rt1")
                        for tag in ("rE", "rO"):
                            if win == 0 and tag in rprev:
                                self.wait(eng, "sLD", rprev[tag])
                        half = slice(tp * 1024 + win * 512,
                                     tp * 1024 + (win + 1) * 512)
                        wsl = slice(win * 512, (win + 1) * 512)
                        inst = None
                        if self.on("vector"):
                            self.V(eng, eng.tensor_mul(self.cscr[:, :], pse[:, :],
                                                       self.cosT[:, half]))
                            self.V(eng, eng.tensor_mul(self.s1scr[:, :], pso[:, :],
                                                       self.sinP[:, half]))
                            self.V(eng, eng.tensor_sub(self.ropeE[:, wsl],
                                                       self.cscr[:, :],
                                                       self.s1scr[:, :]))
                            self.V(eng, eng.tensor_mul(self.cscr[:, :], pso[:, :],
                                                       self.cosT[:, half]))
                            self.V(eng, eng.tensor_mul(self.s1scr[:, :], pse[:, :],
                                                       self.sinP[:, half]))
                            inst = eng.tensor_add(self.ropeO[:, wsl],
                                                  self.cscr[:, :], self.s1scr[:, :])
                        self.inc(inst, "sVE", f"rp{key}")
                        self.VM(eng, f"rp{key}")
                if is_k and tp == 0:
                    if self.on("scalar") or self.on("plan"):
                        self.wait(eng, "sPE", f"qk{tp}_{p}_0")
                        if akprev is not None:
                            self.wait(eng, "sLD", akprev)
                        inst = None
                        if self.on("scalar"):
                            eng.activation(self.akmix[:, 0, :], self.PS[4][:, 0:10],
                                           AF.Copy)
                            inst = eng.activation(self.akmix[:, 1, :],
                                                  self.PS[5][:, 0:10], AF.Copy)
                        self.inc(inst, "sAE", f"akm{tp}_{p}")
                    if self.on("sync") or self.on("plan"):
                        self.wait(eng, "sAE", f"akm{tp}_{p}")
                        hA = 2 * (p - 8)
                        for z in range(2):
                            dst = bass.AP(
                                self.akav_dram,
                                (0 * HC + hA) * 1280 + z * 640,
                                [[1280, 2], [10, 64], [1, 10]])
                            inst = None
                            if self.on("sync"):
                                inst = eng.dma_start(out=dst,
                                                     in_=self.akmix[:, z, :])
                            self.inc(inst, "sLD", f"aks{tp}_{p}_{z}", 16)
                        akprev = f"aks{tp}_{p}_1"
                if self.on("sync") or self.on("plan"):
                    self.wait(eng, "sVE", f"rpqk{tp}_{p}_1")
                    hA = 2 * (p % 8) + (16 if is_k else 0)
                    half = slice(tp * 1024, (tp + 1) * 1024)
                    outs = [
                        (self.qkT_dram[hA, 0:64, half], self.ropeE[0:64, :]),
                        (self.qkT_dram[hA + 1, 0:64, half], self.ropeE[64:128, :]),
                        (self.qkT_dram[hA, 64:128, half], self.ropeO[0:64, :]),
                        (self.qkT_dram[hA + 1, 64:128, half], self.ropeO[64:128, :]),
                    ]
                    for z, (dst, srcb) in enumerate(outs):
                        inst = None
                        if self.on("sync"):
                            inst = eng.dma_start(out=dst, in_=srcb)
                        self.inc(inst, "sLD", f"spl{tp}_{p}_{z}", 16)
                    rprev["rE"] = f"spl{tp}_{p}_1"
                    rprev["rO"] = f"spl{tp}_{p}_3"
            for oc in range(32, 48):
                h = oc - 32
                sa, sb = (0, 1) if oc % 2 == 0 else (2, 3)
                rider = tp == 0
                key = f"v{tp}_{oc}"
                if self.on("gpsimd") or self.on("plan"):
                    pk = (f"v{tp}_{oc-2}" if oc >= 34 else f"qk{tp}_{14+(oc-32)}_1")
                    self.wait(eng, "sPE", pk)
                    inst = None
                    if self.on("gpsimd"):
                        inst = eng.dma_start(out=self.w_col[:, sa, :, :],
                                             in_=self.wqkvT_t[oc, :, :, :])
                    self.inc(inst, "sSP", f"wcv{tp}_{oc}", 16)
                if self.on("tensor") or self.on("plan"):
                    self.wait(eng, "sSP", f"wcv{tp}_{oc}")
                    psa, psb = self.PS[sa], self.PS[sb]
                    tag = "b0" if sa == 0 else "b1"
                    if tag in bprev:
                        self.wait(eng, bprev[tag][0], bprev[tag][1])
                    bprev[tag] = ("sAE", f"ev{key}")
                    if rider:
                        tg = "r4" if oc % 2 == 0 else "r5"
                        if tg in bprev:
                            self.wait(eng, bprev[tg][0], bprev[tg][1])
                        bprev[tg] = ("sAE", f"akmv{tp}_{oc}")
                    inst = None
                    for c in range(NCC):
                        if self.on("tensor"):
                            eng.matmul(psa[:, :], self.w_col[:, sa, c, :],
                                       self.xnT[:, c, 0:512],
                                       start=(c == 0), stop=(c == NCC - 1))
                            inst = eng.matmul(psb[:, :], self.w_col[:, sa, c, :],
                                              self.xnT[:, c, 512:1024],
                                              start=(c == 0), stop=(c == NCC - 1))
                            if rider:
                                inst = eng.matmul(
                                    self.PS[4 + (oc % 2)][:, 0:10],
                                    self.w_col[:, sa, c, :], self.wte_sb[:, c, :],
                                    start=(c == 0), stop=(c == NCC - 1))
                    self.inc(inst, "sPE", key)
                if self.on("scalar") or self.on("plan"):
                    self.wait(eng, "sPE", key)
                    if "rE" in rprev:
                        self.wait(eng, "sLD", rprev["rE"])
                    inst = None
                    if self.on("scalar"):
                        eng.activation(self.ropeE[:, 0:512], self.PS[sa][:, :],
                                       AF.Copy)
                        inst = eng.activation(self.ropeE[:, 512:1024],
                                              self.PS[sb][:, :], AF.Copy)
                    self.inc(inst, "sAE", f"ev{key}")
                    if rider:
                        if akprev is not None:
                            self.wait(eng, "sLD", akprev)
                        inst = None
                        if self.on("scalar"):
                            inst = eng.activation(self.akmix[:, 0, :],
                                                  self.PS[4 + (oc % 2)][:, 0:10],
                                                  AF.Copy)
                        self.inc(inst, "sAE", f"akmv{tp}_{oc}")
                if self.on("sync") or self.on("plan"):
                    self.wait(eng, "sAE", f"ev{key}")
                    inst = None
                    if self.on("sync"):
                        inst = eng.dma_start(
                            out=self.qkT_dram[oc, :, tp * 1024:(tp + 1) * 1024],
                            in_=self.ropeE[:, :])
                    self.inc(inst, "sLD", f"vsp{key}", 16)
                    rprev["rE"] = f"vsp{key}"
                    if rider:
                        self.wait(eng, "sAE", f"akmv{tp}_{oc}")
                        inst = None
                        if self.on("sync"):
                            inst = eng.dma_start(
                                out=self.akav_dram[1, h, :, :],
                                in_=self.akmix[:, 0, :])
                        self.inc(inst, "sLD", f"avs{tp}_{oc}", 16)
                        akprev = f"avs{tp}_{oc}"
            if self.on("tensor") or self.on("plan"):
                self.wait(eng, "sLD", f"xnT{tp}_{NCC-1}")
                for tg in ("r4", "r5"):
                    if tg in bprev:
                        self.wait(eng, bprev[tg][0], bprev[tg][1])
                    bprev[tg] = ("sAE", f"evlg{tp}")
                inst = None
                for c in range(NCC):
                    if self.on("tensor"):
                        lhsT = self.wffn_sb[:, c, :]
                        eng.matmul(self.PS[4][0:24, :], lhsT,
                                   self.xnT[:, c, 0:512],
                                   start=(c == 0), stop=(c == NCC - 1))
                        inst = eng.matmul(self.PS[5][0:24, :], lhsT,
                                          self.xnT[:, c, 512:1024],
                                          start=(c == 0), stop=(c == NCC - 1))
                self.inc(inst, "sPE", f"lg{tp}")
            if self.on("scalar") or self.on("plan"):
                self.wait(eng, "sPE", f"lg{tp}")
                inst = None
                if self.on("scalar"):
                    eng.activation(self.lgT[0:24, tp * 1024:tp * 1024 + 512],
                                   self.PS[4][0:24, :], AF.Copy)
                    inst = eng.activation(
                        self.lgT[0:24, tp * 1024 + 512:(tp + 1) * 1024],
                        self.PS[5][0:24, :], AF.Copy)
                self.inc(inst, "sAE", f"evlg{tp}")
        for piece in range(NT):
            if self.on("tensor") or self.on("plan"):
                self.wait(eng, "sAE", "evlg1" if piece == 0 else f"lgte{piece-1}")
                inst = None
                if self.on("tensor"):
                    inst = eng.transpose(
                        self.PS[6][:, 0:24],
                        self.lgT[0:24, piece * 128:(piece + 1) * 128],
                        self.eye_f32[0:24, 0:24])
                self.inc(inst, "sPE", f"lgtr{piece}")
            if self.on("scalar") or self.on("plan"):
                self.wait(eng, "sPE", f"lgtr{piece}")
                inst = None
                if self.on("scalar"):
                    inst = eng.activation(self.logits_tm[:, piece, :],
                                          self.PS[6][:, 0:24], AF.Copy)
                self.inc(inst, "sAE", f"lgte{piece}")
        if self.on("sync") or self.on("plan"):
            self.wait(eng, "sLD", "avs0_47")
            inst = None
            if self.on("sync"):
                inst = eng.dma_start(
                    out=self.akT[:, :, :],
                    in_=bass.AP(self.akav_dram, 0, [[10, 128], [1280, HC], [1, 10]]))
            self.inc(inst, "sLD", "akl", 16)
            inst = None
            if self.on("sync"):
                inst = eng.dma_start(
                    out=self.avT[:, :, :],
                    in_=bass.AP(self.akav_dram, HC * 1280,
                                [[10, 128], [1280, HC], [1, 10]]))
            self.inc(inst, "sLD", "avl", 16)
        if self.on("vector") or self.on("plan"):
            inst = eng.memset(self.av_sb[:, :, :], 0.0) if self.on("vector") else None
            self.inc(inst, "sVE", "avz")
        for h in range(HC):
            if self.on("tensor") or self.on("plan"):
                self.wait(eng, "sLD", "avl")
                if h > 0:
                    self.wait(eng, "sAE", f"avev{h-1}")
                inst = None
                if self.on("tensor"):
                    inst = eng.transpose(self.PS7[0:10, 0:128],
                                         self.avT[:, h, :], self.eye_bf[:, :])
                self.inc(inst, "sPE", f"avtr{h}")
            if self.on("scalar") or self.on("plan"):
                self.wait(eng, "sPE", f"avtr{h}")
                self.wait(eng, "sVE", "avz")
                inst = None
                if self.on("scalar"):
                    inst = eng.activation(self.av_sb[0:10, h, 0:128],
                                          self.PS7[0:10, 0:128], AF.Copy)
                self.inc(inst, "sAE", f"avev{h}")

    def ph3_front(self, eng):
        v4 = [[256, 128], [16, NT], [4, 4], [1, 4]]
        if self.on("vector") or self.on("plan"):
            inst = None
            for tt in range(NT):
                if self.on("vector"):
                    eng.scalar_tensor_tensor(
                        self.Z[:, tt * 16:(tt + 1) * 16],
                        self.logits_tm[:, tt, 0:16], self.rinv10[:, tt:tt + 1],
                        self.hb10[:, :], ALU.mult, ALU.add)
                    inst = eng.scalar_tensor_tensor(
                        self.LG[:, tt * 8:(tt + 1) * 8],
                        self.logits_tm[:, tt, 16:24], self.rinv_all[:, tt:tt + 1],
                        self.hb_b[:, 16:24], ALU.mult, ALU.add)
            self.inc(inst, "sVE", "zlg")
            inst = eng.memset(self.lnSv[:, :], 0.0) if self.on("vector") else None
            self.inc(inst, "sVE", "lnsv0")
        if self.on("vector") or self.on("plan"):
            inst = None
            if self.on("vector"):
                self.V(eng, eng.reduce_max(
                    bass.AP(self.M2, 0, [[32, 128], [1, 32]]),
                    bass.AP(self.LG, 0, [[128, 128], [4, 32], [1, 4]]), AX.X))
                inst = eng.tensor_tensor(
                    bass.AP(self.LG, 0, [[128, 128], [4, 32], [1, 4]]),
                    bass.AP(self.LG, 0, [[128, 128], [4, 32], [1, 4]]),
                    bass.AP(self.M2, 0, [[32, 128], [1, 32], [0, 4]]),
                    ALU.subtract)
            self.inc(inst, "sVE", "lgsh")
        if self.on("scalar") or self.on("plan"):
            self.wait(eng, "sVE", "lgsh")
            inst = None
            if self.on("scalar"):
                inst = eng.activation(self.E2[:, :], self.LG[:, :], AF.Exp)
            self.inc(inst, "sAE", "e2")
        if self.on("vector") or self.on("plan"):
            self.wait(eng, "sAE", "e2")
            inst = None
            if self.on("vector"):
                self.V(eng, eng.reduce_sum(
                    bass.AP(self.S2, 0, [[32, 128], [1, 32]]),
                    bass.AP(self.E2, 0, [[128, 128], [4, 32], [1, 4]]), AX.X))
                self.V(eng, eng.reciprocal(self.R2[:, :], self.S2[:, :]))
                inst = eng.tensor_tensor(
                    bass.AP(self.HP, 0, [[128, 128], [4, 32], [1, 4]]),
                    bass.AP(self.E2, 0, [[128, 128], [4, 32], [1, 4]]),
                    bass.AP(self.R2, 0, [[32, 128], [1, 32], [0, 4]]),
                    ALU.mult)
            self.inc(inst, "sVE", "hp")
            self.VM(eng, "hp")
        for tt in range(NT):
            bi = tt % 2
            rows = slice(tt * 128, (tt + 1) * 128)
            if self.on("sync") or self.on("plan"):
                if tt >= 2:
                    self.wait(eng, "sVE", f"xpre{tt-2}")
                inst = eng.dma_start(out=self.xn3_sb[:, bi, :],
                                     in_=self.xn_dram[rows, :]) if self.on("sync") else None
                self.inc(inst, "sLD", f"xn3a{tt}", 16)
            if self.on("vector") or self.on("plan"):
                self.wait(eng, "sLD", f"xn3a{tt}")
                if tt >= 2:
                    self.wait(eng, "sSP", f"xpsp{tt-2}")
                inst = None
                if self.on("vector"):
                    hp = self.HP
                    xn = self.xn3_sb[:, bi, :]
                    acc = self.acc_sb[:, 0:1024]
                    self.V(eng, eng.tensor_scalar_mul(acc, xn[:, 0:1024],
                                                      hp[:, tt * 8:tt * 8 + 1]))
                    for n in range(1, 4):
                        o = acc if n < 3 else self.xpre_sb[:, bi, :]
                        inst = eng.scalar_tensor_tensor(
                            o, xn[:, n * 1024:(n + 1) * 1024],
                            hp[:, tt * 8 + n:tt * 8 + n + 1],
                            acc, ALU.mult, ALU.add)
                        if n < 3:
                            self.V(eng, inst)
                self.inc(inst, "sVE", f"xpre{tt}")
                self.VM(eng, f"xpre{tt}")
            if self.on("gpsimd") or self.on("plan"):
                self.wait(eng, "sVE", f"xpre{tt}")
                inst = eng.dma_start(out=self.xpre_dram[rows, :],
                                     in_=self.xpre_sb[:, bi, :]) if self.on("gpsimd") else None
                self.inc(inst, "sSP", f"xpsp{tt}", 16)
        if self.on("gpsimd") or self.on("plan"):
            inst = eng.dma_start(out=self.wb_sb[:, :, :],
                                 in_=self.wbranch_t[:, :, :]) if self.on("gpsimd") else None
            self.inc(inst, "sSP", "wb", 16)
        if self.on("sync") or self.on("plan"):
            self.wait(eng, "sSP", f"xpsp{NT-1}")
            for cc in range(8):
                inst = None
                if self.on("sync"):
                    inst = eng.dma_start(
                        out=self.xpreT[:, cc, :],
                        in_=self.xpre_dram[:, cc * 128:(cc + 1) * 128],
                        transpose=True)
                self.inc(inst, "sLD", f"xpT{cc}", 16)
        for tt in range(NT):
            pa, pb = (0, 1) if tt % 2 == 0 else (2, 3)
            if self.on("tensor") or self.on("plan"):
                self.wait(eng, "sLD", "xpT7")
                self.wait(eng, "sSP", "wb")
                if tt >= 2:
                    self.wait(eng, "sAE", f"ybre{tt-2}")
                inst = None
                for ccb in range(8):
                    if self.on("tensor"):
                        lhsT = self.xpreT[:, ccb, tt * 128:(tt + 1) * 128]
                        eng.matmul(self.PS[pa][:, :], lhsT,
                                   self.wb_sb[:, ccb, 0:512],
                                   start=(ccb == 0), stop=(ccb == 7))
                        inst = eng.matmul(self.PS[pb][:, :], lhsT,
                                          self.wb_sb[:, ccb, 512:1024],
                                          start=(ccb == 0), stop=(ccb == 7))
                self.inc(inst, "sPE", f"ybr{tt}")
            if self.on("scalar") or self.on("plan"):
                self.wait(eng, "sPE", f"ybr{tt}")
                inst = None
                if self.on("scalar"):
                    eng.activation(self.y_br[:, tt, 0:512], self.PS[pa][:, :],
                                   AF.Copy)
                    inst = eng.activation(self.y_br[:, tt, 512:1024],
                                          self.PS[pb][:, :], AF.Copy)
                self.inc(inst, "sAE", f"ybre{tt}")

    def ph3_sink(self, eng, it, half):
        v4 = [[256, 128], [16, NT], [4, 4], [1, 4]]
        M = self.Mu if half == 0 else self.Mv
        if self.on("vector") or self.on("plan"):
            if it > 0 or half > 0:
                self.wait(eng, "sAE",
                  f"skl{it}_0" if half == 1 else f"skl{it-1}_1")
            src = self.lnSv if half == 0 else self.lnSu
            pat = ([[64, 128], [4, NT], [0, 4], [1, 4]] if half == 0
                   else [[64, 128], [4, NT], [1, 4], [0, 4]])
            epat = (v4 if half == 0
                    else [[256, 128], [16, NT], [1, 4], [4, 4]])
            gpat = [[64, 128], [4, NT], [1, 4], [0, 4]]
            gout = [[64, 128], [4, NT], [1, 4]]
            Madd = self.Mv if half == 0 else self.Mu
            inst = None
            if self.on("vector"):
                if it > 0 or half > 0:
                    self.V(eng, eng.tensor_add(src[:, :], src[:, :],
                               Madd[:, :]))
                self.V(eng, eng.tensor_tensor(bass.AP(self.T0, 0, v4),
                              bass.AP(self.Z, 0, v4),
                              bass.AP(src, 0, pat),
                              ALU.subtract))
                self.V(eng, eng.reduce_max(bass.AP(M, 0, gout),
                           bass.AP(self.T0, 0, epat),
                           AX.X))
                inst = eng.tensor_tensor(bass.AP(self.T0, 0, epat),
                         bass.AP(self.T0, 0, epat),
                         bass.AP(M, 0, gpat),
                         ALU.subtract)
            self.inc(inst, "sVE", f"skt{it}_{half}")
        if self.on("scalar") or self.on("plan"):
            self.wait(eng, "sVE", f"skt{it}_{half}")
            inst = None
            if self.on("scalar"):
                inst = eng.activation(self.E[:, :], self.T0[:, :], AF.Exp)
            self.inc(inst, "sAE", f"ske{it}_{half}")
        if self.on("vector") or self.on("plan"):
            self.wait(eng, "sAE", f"ske{it}_{half}")
            dst = self.Su if half == 0 else self.Sv
            pat = (v4 if half == 0
                   else [[256, 128], [16, NT], [1, 4], [4, 4]])
            inst = None
            if self.on("vector"):
                inst = eng.reduce_sum(
                    bass.AP(dst, 0, [[64, 128], [4, NT], [1, 4]]),
                    bass.AP(self.E, 0, pat), AX.X)
            self.inc(inst, "sVE", f"sks{it}_{half}")
        if self.on("scalar") or self.on("plan"):
            self.wait(eng, "sVE", f"sks{it}_{half}")
            src = self.Su if half == 0 else self.Sv
            dst = self.lnSu if half == 0 else self.lnSv
            inst = None
            if self.on("scalar"):
                inst = eng.activation(dst[:, :], src[:, :], AF.Ln)
            self.inc(inst, "sAE", f"skl{it}_{half}")

    def ph3_fin(self, eng):
        v4 = [[256, 128], [16, NT], [4, 4], [1, 4]]
        if self.on("vector") or self.on("plan"):
            self.wait(eng, "sAE", f"skl{SINK_ITERS-1}_1")
            inst = None
            if self.on("vector"):
                self.V(eng, eng.tensor_add(self.lnSv[:, :], self.lnSv[:, :],
                                           self.Mv[:, :]))
                self.V(eng, eng.tensor_tensor(
                    bass.AP(self.T0, 0, v4), bass.AP(self.Z, 0, v4),
                    bass.AP(self.lnSu, 0, [[64, 128], [4, NT], [1, 4], [0, 4]]),
                    ALU.subtract))
                inst = eng.tensor_tensor(bass.AP(self.T0, 0, v4),
                                         bass.AP(self.T0, 0, v4),
                                         bass.AP(self.lnSv, 0,
                                                 [[64, 128], [4, NT], [0, 4], [1, 4]]),
                                         ALU.subtract)
            self.inc(inst, "sVE", "hraw")
        if self.on("scalar") or self.on("plan"):
            self.wait(eng, "sVE", "hraw")
            inst = None
            if self.on("scalar"):
                inst = eng.activation(self.E[:, :], self.T0[:, :], AF.Exp)
            self.inc(inst, "sAE", "hexp")
        if self.on("vector") or self.on("plan"):
            self.wait(eng, "sAE", "hexp")
            self.wait(eng, "sVE", "hb10")
            v3 = [[256, 128], [16, NT], [1, 16]]
            eyepat = [[16, 128], [0, NT], [1, 16]]
            inst = None
            if self.on("vector"):
                self.V(eng, eng.tensor_tensor(
                    bass.AP(self.T0, 0, v3), bass.AP(self.E, 0, v3),
                    bass.AP(self.eyeP_b, 0, eyepat), ALU.subtract))
                inst = eng.scalar_tensor_tensor(
                    bass.AP(self.Hfin, 0, v3), bass.AP(self.T0, 0, v3),
                    self.gs[:, 0:1], bass.AP(self.eyeP_b, 0, eyepat),
                    ALU.mult, ALU.add)
            self.inc(inst, "sVE", "hfin")

    def ph3_mix(self, eng, tt):
        bi = tt % 2
        rows = slice(tt * 128, (tt + 1) * 128)
        if self.on("sync") or self.on("plan"):
            if tt >= 2:
                self.wait(eng, "sVE", f"mix{tt-2}")
            else:
                self.wait(eng, "sAE", f"ybre{NT-1}")
            inst = eng.dma_start(out=self.xn3_sb[:, bi, :],
                                 in_=self.xn_dram[rows, :]) if self.on("sync") else None
            self.inc(inst, "sLD", f"xn3b{tt}", 16)
        if self.on("vector") or self.on("plan"):
            self.wait(eng, "sLD", f"xn3b{tt}")
            self.wait(eng, "sVE", "hfin")
            self.wait(eng, "sAE", f"ybre{tt}")
            if tt >= 1:
                self.wait(eng, "sAE", f"silu{tt-1}")
            inst = None
            if self.on("vector"):
                xn = self.xn3_sb[:, bi, :]
                hf = self.Hfin
                for r in range(2):
                    acc = self.acc_sb[:, r * 1024:(r + 1) * 1024]
                    self.V(eng, eng.tensor_scalar_mul(
                        acc, xn[:, 0:1024],
                        hf[:, tt * 16 + 4 * r:tt * 16 + 4 * r + 1]))
                    for jj in range(1, 4):
                        self.V(eng, eng.scalar_tensor_tensor(
                            acc, xn[:, jj * 1024:(jj + 1) * 1024],
                            hf[:, tt * 16 + 4 * r + jj:tt * 16 + 4 * r + jj + 1],
                            acc, ALU.mult, ALU.add))
                    inst = eng.scalar_tensor_tensor(
                        acc, self.y_br[:, tt, :],
                        self.HP[:, tt * 8 + 4 + r:tt * 8 + 4 + r + 1],
                        acc, ALU.mult, ALU.add)
                    if r == 0:
                        self.V(eng, inst)
            self.inc(inst, "sVE", f"mix{tt}")
        if self.on("scalar") or self.on("plan"):
            self.wait(eng, "sVE", f"mix{tt}")
            if tt >= 2:
                self.wait(eng, "sSP", f"gsp{tt-2}")
            inst = None
            if self.on("scalar"):
                inst = eng.activation(self.g_sb[:, :], self.acc_sb[:, :], AF.Silu)
            self.inc(inst, "sAE", f"silu{tt}")
        if self.on("gpsimd") or self.on("plan"):
            self.wait(eng, "sAE", f"silu{tt}")
            inst = eng.dma_start(out=self.g_dram[rows, :],
                                 in_=self.g_sb[:, :]) if self.on("gpsimd") else None
            self.inc(inst, "sSP", f"gsp{tt}", 16)

    def ph4_head(self, eng, h):
        if True:
            hb = h % 2
            if self.on("sync") or self.on("plan"):
                if h >= 2:
                    self.wait(eng, "sPE", f"yadj{h-2}_7")
                inst = eng.dma_start(out=self.kT_sb[:, hb, :],
                                     in_=self.qkT_dram[16 + h, :, :]) if self.on("sync") else None
                self.inc(inst, "sLD", f"kT{h}", 16)
                inst = eng.dma_start(out=self.qT_sb[:, hb, :],
                                     in_=self.qkT_dram[h, :, :]) if self.on("sync") else None
                self.inc(inst, "sLD", f"qT{h}", 16)
                for kc in range(16):
                    inst = None
                    if self.on("sync"):
                        inst = eng.dma_start(
                            out=self.v_sb[:, hb, kc, 0:128],
                            in_=self.qkT_dram[32 + h, :, kc * 128:(kc + 1) * 128],
                            transpose=True)
                    self.inc(inst, "sLD", f"v{h}_{kc}", 16)
            if h < 2 and (self.on("vector") or self.on("plan")):
                inst = None
                if self.on("vector"):
                    inst = eng.memset(self.v_sb[:, hb, :, 128:129], 1.0)
                self.inc(inst, "sVE", f"vones{h}")
            if self.on("tensor") or self.on("plan"):
                if h > 0:
                    self.wait(eng, "sAE", f"ae{h-1}")
                if h == 0:
                    self.wait(eng, "sLD", "akl")
                self.wait(eng, "sLD", f"qT{h}")
                inst = None
                for u in range(16):
                    if self.on("tensor"):
                        inst = eng.matmul(self.PS[4][:, 10 * u:10 * u + 10],
                                          self.qT_sb[:, hb, u * 128:(u + 1) * 128],
                                          self.akT[:, h, :], start=True, stop=True,
                                          skip_group_check=True)
                self.inc(inst, "sPE", f"as{h}")
            if self.on("scalar") or self.on("plan"):
                self.wait(eng, "sPE", f"as{h}")
                if h >= 2:
                    self.wait(eng, "sVE", f"apv{h-2}_7")
                inst = None
                for u in range(16):
                    if self.on("scalar"):
                        inst = eng.activation(self.ea_sb[:, hb, 10 * u:10 * u + 10],
                                              self.PS[4][:, 10 * u:10 * u + 10],
                                              AF.Exp, scale=np.float32(SC_ADPT),
                                              accum_out=self.asum_sb[:, hb, u:u + 1])
                self.inc(inst, "sAE", f"ae{h}")
            if self.on("vector") or self.on("plan"):
                self.wait(eng, "sAE", f"ae{h}")
                inst = None
                if self.on("vector"):
                    self.V(eng, eng.reciprocal(self.gra_sb[:, hb, :],
                                               self.asum_sb[:, hb, :]))
                    inst = eng.tensor_scalar_mul(self.gra_sb[:, hb, :],
                                                 self.gra_sb[:, hb, :],
                                                 self.gating_b[:, 0:1])
                self.inc(inst, "sVE", f"gra{h}")
                self.VM(eng, f"gra{h}")
            for s in range(8):
                nkc = 2 * s + 2
                npk = s + 1
                ye, yo = 5, 6
                skey = f"{h}_{s}"
                for pk in range(npk + 1):
                    if pk < npk:
                        PSP = self.PSA if pk % 2 == 0 else self.PSB
                        key = f"{h}_{s}_{pk}"
                        if self.on("tensor") or self.on("plan"):
                            if h == 0 and s == 0 and pk == 0:
                                self.wait(eng, "sAE", f"ybre{NT-1}")
                                self.wait(eng, "sLD", f"v{h}_15")
                                self.wait(eng, "sVE", f"vones{min(h, 1)}")
                            if s == 0 and pk == 0:
                                self.wait(eng, "sLD", f"v{h}_15")
                            tag = f"stb{pk % 2}"
                            if tag in self._c4:
                                self.wait(eng, "sAE", self._c4[tag])
                            self._c4[tag] = f"pT{key}"
                            inst = None
                            if self.on("tensor"):
                                eng.matmul(PSP[:, 0:256],
                                           self.kT_sb[:, hb, 2 * pk * 128:(2 * pk + 1) * 128],
                                           self.qT_sb[:, hb, s * 256:(s + 1) * 256],
                                           start=True, stop=True)
                                inst = eng.matmul(
                                    PSP[:, 512:768],
                                    self.kT_sb[:, hb, (2 * pk + 1) * 128:(2 * pk + 2) * 128],
                                    self.qT_sb[:, hb, s * 256:(s + 1) * 256],
                                    start=True, stop=True)
                            self.inc(inst, "sPE", f"sT{key}")
                        if self.on("scalar") or self.on("plan"):
                            self.wait(eng, "sPE", f"sT{key}")
                            tag = f"ptb{pk % 2}"
                            if tag in self._c4:
                                self.wait(eng, "sPE", self._c4[tag])
                            self._c4[tag] = f"ym{key}"
                            inst = None
                            if self.on("scalar"):
                                inst = eng.activation(
                                    self.pT_sb[:, pk % 2, :, :],
                                    bass.AP(PSP, 0, [[1024, 128], [512, 2], [1, 256]]),
                                    AF.Exp, scale=np.float32(SC_MAIN),
                                    bias=np.float32(-20.0))
                            self.inc(inst, "sAE", f"pT{key}")
                        if pk == npk - 1 and (self.on("vector") or self.on("plan")):
                            self.wait(eng, "sAE", f"pT{key}")
                            inst = None
                            if self.on("vector"):
                                inst = eng.tensor_mul(
                                    self.pT_sb[:, pk % 2, :, :],
                                    self.pT_sb[:, pk % 2, :, :],
                                    bass.AP(self.trapAB, 0,
                                            [[512, 128], [256, 2], [1, 256]]))
                            self.inc(inst, "sVE", f"mk{key}")
                    if pk >= 1:
                        pky = pk - 1
                        pby = pky % 2
                        ykey = f"{h}_{s}_{pky}"
                        if self.on("tensor") or self.on("plan"):
                            if pky == npk - 1:
                                self.wait(eng, "sVE", f"mk{ykey}")
                            else:
                                self.wait(eng, "sAE", f"pT{ykey}")
                            if pky == 0 and "yb" in self._c4:
                                self.wait(eng, "sVE", self._c4["yb"])
                            inst = None
                            for ci in range(2):
                                kc = 2 * pky + ci
                                for sub in range(2):
                                    if self.on("tensor"):
                                        inst = eng.matmul(
                                            self.PS[(ye, yo)[sub]][:, 0:129],
                                            self.pT_sb[:, pby, ci,
                                                       sub * 128:(sub + 1) * 128],
                                            self.v_sb[:, hb, kc, 0:129],
                                            start=(kc == 0 and True), stop=False,
                                            skip_group_check=True)
                            self.inc(inst, "sPE", f"ym{ykey}")
                if self.on("plan") or self.on("tensor") or True:
                    self._c4["yb"] = f"yev{skey}"
                p7 = slice(0, 256) if s % 2 == 0 else slice(256, 512)
                if self.on("vector") or self.on("plan"):
                    self.wait(eng, "sPE", f"ym{h}_{s}_{npk-1}")
                    prevtr = (f"aptr{h}_{s-1}" if s > 0 else
                              (f"aptr{h-1}_7" if h > 0 else None))
                    if prevtr:
                        self.wait(eng, "sPE", prevtr)
                    inst = None
                    if self.on("vector"):
                        a = self.ascr
                        self.V(eng, eng.tensor_mul(a[:, 2:3],
                                                   self.gra_sb[:, hb, 2 * s:2 * s + 1],
                                                   self.PS[ye][:, 128:129]))
                        self.V(eng, eng.tensor_mul(a[:, 3:4],
                                                   self.gra_sb[:, hb, 2 * s + 1:2 * s + 2],
                                                   self.PS[yo][:, 128:129]))
                        self.V(eng, eng.tensor_scalar_mul(
                            self.ap_sb[:, 0:10],
                            self.ea_sb[:, hb, 20 * s:20 * s + 10], a[:, 2:3]))
                        inst = eng.tensor_scalar_mul(
                            self.ap_sb[:, 10:20],
                            self.ea_sb[:, hb, 20 * s + 10:20 * s + 20], a[:, 3:4])
                    self.inc(inst, "sVE", f"apv{skey}")
                    self.VM(eng, f"apv{skey}")
                if self.on("tensor") or self.on("plan"):
                    self.wait(eng, "sVE", f"apv{skey}")
                    prev = (f"apev{h}_{s-2}" if s >= 2 else
                            (f"apev{h-1}_{s+6}" if h > 0 else None))
                    if prev:
                        self.wait(eng, "sAE", prev)
                    inst = None
                    if self.on("tensor"):
                        eng.transpose(self.PS7[0:10, p7.start:p7.start + 128],
                                      self.ap_sb[:, 0:10], self.eye_bf[:, :])
                        inst = eng.transpose(self.PS7[0:10, p7.start + 128:p7.stop],
                                             self.ap_sb[:, 10:20], self.eye_bf[:, :])
                    self.inc(inst, "sPE", f"aptr{skey}")
                if self.on("scalar") or self.on("plan"):
                    self.wait(eng, "sPE", f"aptr{skey}")
                    prev = (f"yadj{h}_{s-2}" if s >= 2 else
                            (f"yadj{h-1}_{s+6}" if h > 0 else None))
                    if prev:
                        self.wait(eng, "sPE", prev)
                    inst = None
                    if self.on("scalar"):
                        inst = eng.activation(self.apT_sb[0:10, s % 2, :],
                                              self.PS7[0:10, p7], AF.Copy)
                    self.inc(inst, "sAE", f"apev{skey}")
                if self.on("tensor") or self.on("plan"):
                    self.wait(eng, "sAE", f"apev{skey}")
                    inst = None
                    if self.on("tensor"):
                        eng.matmul(self.PS[ye][:, 0:129],
                                   self.apT_sb[0:10, s % 2, 0:128],
                                   self.av_sb[0:10, h, :], start=False, stop=True,
                                   skip_group_check=True)
                        inst = eng.matmul(self.PS[yo][:, 0:129],
                                          self.apT_sb[0:10, s % 2, 128:256],
                                          self.av_sb[0:10, h, :], start=False,
                                          stop=True, skip_group_check=True)
                    self.inc(inst, "sPE", f"yadj{skey}")
                if self.on("vector") or self.on("plan"):
                    self.wait(eng, "sPE", f"yadj{skey}")
                    prev = (f"ysp{h}_{s-2}" if s >= 2 else
                            (f"ysp{h-1}_{s+6}" if h > 0 else None))
                    if prev:
                        self.wait(eng, "sSP", prev)
                    inst = None
                    if self.on("vector"):
                        a = self.ascr
                        self.V(eng, eng.reciprocal(a[:, 4:5], self.PS[ye][:, 128:129]))
                        self.V(eng, eng.reciprocal(a[:, 5:6], self.PS[yo][:, 128:129]))
                        self.V(eng, eng.tensor_scalar_mul(self.ysb[:, s % 2, 0, :],
                                                          self.PS[ye][:, 0:128],
                                                          a[:, 4:5]))
                        inst = eng.tensor_scalar_mul(self.ysb[:, s % 2, 1, :],
                                                     self.PS[yo][:, 0:128], a[:, 5:6])
                    self.inc(inst, "sVE", f"yev{skey}")
                    self.VM(eng, f"yev{skey}")
                if self.on("gpsimd") or self.on("plan"):
                    self.wait(eng, "sVE", f"yev{skey}")
                    inst = None
                    if self.on("gpsimd"):
                        eng.dma_start(
                            out=self.y_dram[s * 256:s * 256 + 128,
                                            h * 128:(h + 1) * 128],
                            in_=self.ysb[:, s % 2, 0, :]).then_inc(self.sems["sSP"], 16)
                        inst = eng.dma_start(
                            out=self.y_dram[s * 256 + 128:(s + 1) * 256,
                                            h * 128:(h + 1) * 128],
                            in_=self.ysb[:, s % 2, 1, :])
                    if self.mode == "plan":
                        self.cnt["sSP"] += 16
                    self.inc(inst, "sSP", f"ysp{skey}", 16)

    def ph5_gather(self, eng):
        if self.on("gpsimd") or self.on("plan"):
            self.wait(eng, "sSP", f"ysp{HC-1}_7")
            inst = None
            if self.on("gpsimd"):
                pid = self.nc.partition_id(engines=[mybir.EngineType.Pool])
                slot = pid % 2
                other = 1 - slot
                eng.dma_start(out=self.sh_y[bass.ds(slot, 1), :, :],
                              in_=self.y_dram[:, :]).then_inc(self.sem_poll, 16)
                eng.wait_ge(self.sem_poll, 16)
                eng.dma_start(out=self.sh_f[bass.ds(slot, 1), :],
                              in_=self.ntl[:, :]).then_inc(self.sem_poll, 16)
                eng.wait_ge(self.sem_poll, 32)
            if self.on("gpsimd"):
                for q in range(4):
                    eng.dma_start(
                        out=self.wp_sb[:, q * 8:(q + 1) * 8, :],
                        in_=self.wprojT_t[0, :, q * 8:(q + 1) * 8, :]
                    ).then_inc(self.sems["sSP"], 16)
                with (eng.register("pf") as pf, eng.register("nr") as nr,
                      eng.register("df") as df, eng.register("pc") as pc):
                    eng.reg_load(nr, self.ntl[0:1, 0:1])
                    eng.reg_mov(df, 1)
                    eng.reg_mov(pc, 32)
                    with eng.While(df):
                        eng.reg_add(pc, pc, 16)
                        eng.dma_start(
                            out=self.ptl[:, :],
                            in_=self.sh_f[bass.ds(other, 1), :]
                        ).then_inc(self.sem_poll, 16)
                        eng.wait_ge(self.sem_poll, pc)
                        eng.reg_load(pf, self.ptl[0:1, 0:1])
                        eng.reg_sub(df, pf, nr)
                inst = eng.nop()
            if self.on("plan"):
                for q in range(4):
                    self.cnt["sSP"] += 16
                    self.M[f"wp0_{q}"] = self.cnt["sSP"]
            self.inc(inst, "sCC", "ag", 16)
        if self.on("sync"):
            eng.wait_ge(self.sems["sCC"], self.M["ag"])

    def ph6_proj(self, eng):
        for og in range(2):
            if og == 1 and (self.on("gpsimd") or self.on("plan")):
                self.wait(eng, "sPE", f"pj0_{NT-1}_1")
                for q in range(4):
                    inst = None
                    if self.on("gpsimd"):
                        inst = eng.dma_start(
                            out=self.wp_sb[:, q * 8:(q + 1) * 8, :],
                            in_=self.wprojT_t[1, :, q * 8:(q + 1) * 8, :])
                    self.inc(inst, "sSP", f"wp1_{q}", 16)
            def yt_loads(tp2, og2):
                tb2 = tp2 % 2
                if self.on("sync") or self.on("plan"):
                    prev = None
                    if tp2 >= 2:
                        prev = (og2, 2 * tp2 - 3)
                    elif og2 == 1:
                        prev = (0, NT - 4 + 2 * tp2 + 1)
                    if prev:
                        self.wait(eng, "sPE", f"pj{prev[0]}_{prev[1]}_1")
                    for cc in range(NCC):
                        blk = 0 if cc < 16 else 1
                        coff = (cc % 16) * 128
                        inst = None
                        if self.on("sync"):
                            inst = eng.dma_start(
                                out=self.yT_sb[:, tb2, cc, :],
                                in_=self.sh_y[blk, tp2 * 256:(tp2 + 1) * 256,
                                              coff:coff + 128],
                                transpose=True)
                        self.inc(inst, "sLD", f"yT{og2}_{tp2}_{cc}", 16)

            if og == 0:
                yt_loads(0, 0)
            for tp in range(NT // 2):
                tb = tp % 2
                if tp + 1 < NT // 2:
                    yt_loads(tp + 1, og)
                elif og == 0:
                    yt_loads(0, 1)
                for ti in range(2):
                    tt = 2 * tp + ti
                    rows = slice(tt * 128, (tt + 1) * 128)
                    if self.on("scalar") or self.on("plan"):
                        for ow in range(2):
                            cols = slice(og * 1024 + ow * 512,
                                         og * 1024 + (ow + 1) * 512)
                            if tt >= 2 or og == 1:
                                po, pt = (og, tt - 2) if tt >= 2 else (0, NT - 2 + tt)
                                self.wait(eng, "sOUT", f"os{po}_{pt}_{ow}")
                            inst = None
                            if self.on("scalar"):
                                inst = eng.dma_start(out=self.gx_sb[:, tt % 2, 0, ow, :],
                                                     in_=self.g_dram[rows, cols])
                            self.inc(inst, "sXG", f"g{og}_{tt}_{ow}", 16)
                            inst = None
                            if self.on("scalar"):
                                inst = eng.dma_start(out=self.gx_sb[:, tt % 2, 1, ow, :],
                                                     in_=self.xc[rows, cols])
                            self.inc(inst, "sXG", f"xc{og}_{tt}_{ow}", 16)
                    for ow in range(2):
                        ps = self.PS[2 * (tt % 2) + ow]
                        if self.on("tensor") or self.on("plan"):
                            if ow == 0 and ti == 0:
                                self.wait(eng, "sLD", f"yT{og}_{tp}_{NCC-1}")
                                self.wait(eng, "sSP", f"wp{og}_3")
                            if tt >= 2 or og == 1:
                                po, pt = (og, tt - 2) if tt >= 2 else (0, NT - 2 + tt)
                                self.wait(eng, "sVE", f"cmb{po}_{pt}_{ow}")
                            inst = None
                            for cc in range(NCC):
                                if self.on("tensor"):
                                    inst = eng.matmul(
                                        ps[:, :], self.yT_sb[:, tb, cc,
                                                             ti * 128:(ti + 1) * 128],
                                        self.wp_sb[:, cc, ow * 512:(ow + 1) * 512],
                                        start=(cc == 0), stop=(cc == NCC - 1))
                            self.inc(inst, "sPE", f"pj{og}_{tt}_{ow}")
                        if self.on("vector") or self.on("plan"):
                            self.wait(eng, "sPE", f"pj{og}_{tt}_{ow}")
                            self.wait(eng, "sXG", f"g{og}_{tt}_{ow}")
                            self.wait(eng, "sXG", f"xc{og}_{tt}_{ow}")
                            inst = None
                            if self.on("vector"):
                                self.V(eng, eng.tensor_mul(
                                    self.t6scr[:, :], ps[:, :],
                                    self.gx_sb[:, tt % 2, 0, ow, :]))
                                inst = eng.tensor_add(self.gx_sb[:, tt % 2, 1, ow, :],
                                                      self.t6scr[:, :],
                                                      self.gx_sb[:, tt % 2, 1, ow, :])
                            self.inc(inst, "sVE", f"cmb{og}_{tt}_{ow}")
                            self.VM(eng, f"cmb{og}_{tt}_{ow}")
                        if self.on("sync") or self.on("plan"):
                            self.wait(eng, "sVE", f"cmb{og}_{tt}_{ow}")
                            cols = slice(og * 1024 + ow * 512,
                                         og * 1024 + (ow + 1) * 512)
                            inst = None
                            if self.on("sync"):
                                inst = eng.dma_start(out=self.out[rows, cols],
                                                     in_=self.gx_sb[:, tt % 2, 1, ow, :])
                            if self.mode == "plan":
                                self.cnt["sOUT"] += 16
                                self.M[f"os{og}_{tt}_{ow}"] = self.cnt["sOUT"]
                            elif inst is not None:
                                inst.then_inc(self.sems["sOUT"], 16)
        final = f"os1_{NT-1}_1"
        if self.on("sync"):
            eng.wait_ge(self.sems["sOUT"], self.M[final])
        if self.on("gpsimd"):
            eng.wait_ge(self.sems["sOUT"], self.M[final])


_NC_CACHE = None


def _build():
    global _NC_CACHE
    if _NC_CACHE is None:
        _NC_CACHE = Prog().build()
    return _NC_CACHE


def kernel(L0_attn_x, L0_attn_w_attn, L0_attn_w_proj, L0_attn_adapter_wte,
           L0_attn_gating, L0_ffn_W_res, L0_ffn_W_pre, L0_ffn_W_post,
           L0_ffn_H_res_base, L0_ffn_H_pre_base, L0_ffn_H_post_base,
           L0_ffn_gate, L0_ffn_W_branch):
    x = np.asarray(L0_attn_x, dtype=np.float32)
    w_attn = np.asarray(L0_attn_w_attn, dtype=np.float32)
    w_proj = np.asarray(L0_attn_w_proj, dtype=np.float32)
    wte = np.asarray(L0_attn_adapter_wte, dtype=np.float32)
    gating = np.ascontiguousarray(
        np.asarray(L0_attn_gating, dtype=np.float32).reshape(1, 1))
    W_res = np.asarray(L0_ffn_W_res, dtype=np.float32)
    W_pre = np.asarray(L0_ffn_W_pre, dtype=np.float32)
    W_post = np.asarray(L0_ffn_W_post, dtype=np.float32)
    Hres_b = np.asarray(L0_ffn_H_res_base, dtype=np.float32)
    Hpre_b = np.asarray(L0_ffn_H_pre_base, dtype=np.float32)
    Hpost_b = np.asarray(L0_ffn_H_post_base, dtype=np.float32)
    gate = np.ascontiguousarray(
        np.asarray(L0_ffn_gate, dtype=np.float32).reshape(1, 1))
    W_br = np.asarray(L0_ffn_W_branch, dtype=np.float32)

    nc = _build()
    nonce_arr = np.full((1, 64), np.random.randint(1, 2**31 - 1), np.int32)
    wbranch_t = np.ascontiguousarray(W_br.reshape(8, 128, 1024).transpose(1, 0, 2))
    wteT_t = np.ascontiguousarray(wte.reshape(10, 32, 128).transpose(2, 1, 0))
    in_maps = []
    for core in range(8):
        b, j = core >> 1, core & 1
        perm = np.concatenate([
            np.r_[(2 * i) * 128:(2 * i) * 128 + 64,
                  (2 * i + 1) * 128:(2 * i + 1) * 128 + 64,
                  (2 * i) * 128 + 64:(2 * i) * 128 + 128,
                  (2 * i + 1) * 128 + 64:(2 * i + 1) * 128 + 128]
            for i in range(8)])
        rows = [w_attn[g * C + 2048 * j: g * C + 2048 * (j + 1)] for g in range(3)]
        w_slice = np.concatenate([rows[0][perm], rows[1][perm], rows[2]], axis=0)
        wqkvT_t = np.ascontiguousarray(
            w_slice.reshape(48, 128, 32, 128).transpose(0, 3, 2, 1))
        wp_rows = w_proj[2048 * j: 2048 * (j + 1)]
        wprojT_t = np.ascontiguousarray(
            wp_rows.reshape(2, 1024, 32, 128).transpose(0, 3, 2, 1))
        iorder = [0, 1, 2, 3] if j == 0 else [2, 3, 0, 1]
        W_all = np.concatenate([
            W_res.reshape(4, 4, C)[iorder].reshape(16, C),
            W_pre, W_post[iorder]], axis=0)
        wffnT_t = np.ascontiguousarray(W_all.reshape(24, 32, 128).transpose(2, 1, 0))
        hbase = np.ascontiguousarray(np.concatenate(
            [Hres_b[iorder].reshape(16), Hpre_b, Hpost_b[iorder]]).reshape(1, 24))
        eyeP = np.ascontiguousarray(
            np.eye(4, dtype=np.float32)[iorder].reshape(1, 16))
        in_maps.append({
            "nonce": nonce_arr,
            "x_full": np.ascontiguousarray(x[b]),
            "xc": np.ascontiguousarray(x[b][:, 2048 * j: 2048 * (j + 1)]),
            "wqkvT_t": wqkvT_t,
            "wprojT_t": wprojT_t,
            "wbranch_t": wbranch_t,
            "wffnT_t": wffnT_t,
            "wteT_t": wteT_t,
            "hbase": hbase,
            "eyeP": eyeP,
            "gating": gating,
            "gate": gate,
        })
    res = run_bass_kernel_spmd(nc, in_maps, core_ids=list(range(8)))
    out = np.empty((B, T, C), dtype=np.float32)
    for core in range(8):
        b, j = core >> 1, core & 1
        out[b][:, 2048 * j: 2048 * (j + 1)] = res.results[core]["out"]
    return out


# version 14 (final)
# speedup vs baseline: 1.0602x; 1.0167x over previous
import numpy as np
import concourse.bass as bass
import concourse.mybir as mybir
from concourse.bass_utils import run_bass_kernel_spmd

F32 = mybir.dt.float32
BF16 = mybir.dt.bfloat16
AF = mybir.ActivationFunctionType
ALU = mybir.AluOpType
AX = mybir.AxisListType

B = 4
T = 2048
C = 4096
HC = 16
NT = T // 128
NCC = C // 128
LN_EPS = 1e-5
RMS_EPS = 1.1920929e-07
SINK_ITERS = 50
SC_MAIN = float(np.float32(1.0 / np.sqrt(np.float32(128.0))))
SC_ADPT = 1.0 / 64.0
CB = 98304


def _consts():
    theta = 1.0 / (10000.0 ** (np.arange(0, 128, 2, dtype=np.float64) / 128.0))
    idx = np.outer(np.arange(T, dtype=np.float64), theta)
    cos2 = np.cos(idx).astype(np.float32)
    sin2 = np.sin(idx).astype(np.float32)
    cosT = np.ascontiguousarray(np.concatenate([cos2, cos2], 1).T)
    sinT = np.concatenate([sin2, sin2], 1).T.copy()
    import ml_dtypes
    kl = np.arange(128)[:, None]
    ql = np.arange(256)[None, :]
    trapA = (kl <= ql).astype(ml_dtypes.bfloat16)
    trapB = ((kl + 128) <= ql).astype(ml_dtypes.bfloat16)
    trapAB = np.concatenate([trapA, trapB], axis=1)
    return (cosT, np.ascontiguousarray(sinT), trapAB,
            np.eye(128, dtype=ml_dtypes.bfloat16),
            np.eye(128, dtype=np.float32))


class Prog:

    def __init__(self, nphase=99, dump=None):
        self.nphase = nphase
        self.ph3_stop = 99
        self.dbg_ybr = False
        self.dbg_hfin = False
        self.dump = dump
        nc = bass.Bass("TRN2", target_bir_lowering=False,
                       detect_race_conditions=False)
        self.nc = nc
        d = nc.dram_tensor
        self.x_full = d("x_full", [T, C], F32, kind="ExternalInput")
        self.xc = d("xc", [T, 2048], F32, kind="ExternalInput")
        self.wqkvT_t = d("wqkvT_t", [48, 128, NCC, 128], F32, kind="ExternalInput")
        self.wprojT_t = d("wprojT_t", [2, 128, NCC, 1024], F32, kind="ExternalInput")
        self.wbranch_t = d("wbranch_t", [128, 8, 1024], F32, kind="ExternalInput")
        self.wffnT_t = d("wffnT_t", [128, NCC, 24], F32, kind="ExternalInput")
        self.wteT_t = d("wteT_t", [128, NCC, 10], F32, kind="ExternalInput")
        self.hbase = d("hbase", [1, 24], F32, kind="ExternalInput")
        self.eyeP = d("eyeP", [1, 16], F32, kind="ExternalInput")
        self.gating = d("gating", [1, 1], F32, kind="ExternalInput")
        self.gate = d("gate", [1, 1], F32, kind="ExternalInput")
        self.out = d("out", [T, 2048], F32, kind="ExternalOutput")
        self.dbg = None
        self._slot_rv = None
        self.xn_dram = d("xn_dram", [T, C], BF16)
        self.qkT_dram = d("qkT_dram", [48, 128, T], BF16)
        self.xpre_dram = d("xpre_dram", [T, 1024], BF16)
        self.y_dram = d("y_dram", [T, 2048], BF16)
        self.sh_y = d("sh_y", [2, T, 2048], BF16, addr_space="Shared")
        self.sh_f = d("sh_f", [2, 64], mybir.dt.int32, addr_space="Shared")
        self.nonce = d("nonce", [1, 64], mybir.dt.int32, kind="ExternalInput")
        self.g_dram = d("g_dram", [T, 2048], F32)
        self.akav_dram = d("akav_dram", [2, HC, 128, 10], BF16)
        cosT, sinNT, trapAB, eye_bf, eye_f32 = _consts()
        self.cosT_d = nc.inline_tensor(cosT, "cosT_d")
        self.sinP_d = nc.inline_tensor(sinNT, "sinP_d")
        self.trapAB_d = nc.inline_tensor(trapAB, "trapAB_d")
        self.eyebf_d = nc.inline_tensor(eye_bf, "eyebf_d")
        self.eyef32_d = nc.inline_tensor(eye_f32, "eyef32_d")

        A = nc.alloc_sbuf_tensor
        self.trapAB = A("trapAB", [128, 512], BF16)
        self.eye_bf = A("eye_bf", [128, 128], BF16)
        self.eye_f32 = A("eye_f32", [128, 128], F32)
        self.hb_b = A("hb_b", [128, 24], F32)
        self.hb10 = A("hb10", [128, 16], F32)
        self.eyeP_b = A("eyeP_b", [128, 16], F32)
        self.gating_b = A("gating_b", [128, 1], F32)
        self.gate_b = A("gate_b", [128, 1], F32)
        self.gs = A("gs", [128, 1], F32)
        self.rinv_all = A("rinv_all", [128, NT], F32)
        self.rinv10 = A("rinv10", [128, NT], F32)
        self.logits_tm = A("logits_tm", [128, NT, 24], F32)
        self.akT = A("akT", [128, HC, 10], BF16)
        self.avT = A("avT", [128, HC, 10], BF16)
        self.av_sb = A("av_sb", [16, HC, 129], BF16)
        self.wffn_sb = A("wffn_sb", [128, NCC, 24], BF16)
        self.wte_sb = A("wte_sb", [128, NCC, 10], BF16)
        self.s_tiny = A("s_tiny", [128, 16], F32)
        self.ntl = A("ntl", [1, 64], mybir.dt.int32)
        self.ptl = A("ptl", [1, 64], mybir.dt.int32)

        self.arena = A("arena", [128, 159744], mybir.dt.uint8)
        base = nc.lookup_mloc(self.arena).addr
        self._b = base
        P = self._place
        self.sqscr = P("sqscr", 0, [128, 4096], BF16)
        self.xnT = P("xnT", 0, [128, NCC, 1024], BF16)
        self.y_br = P("y_br", 0, [128, NT, 1024], F32)
        self.wp_sb = P("wp_sb", 0, [128, NCC, 1024], BF16)
        self.w_col = P("w_col", 65536, [128, 4, NCC, 128], BF16)
        self.xpreT = P("xpreT", CB + 28672, [128, 8, T], BF16)
        self.yT_sb = P("yT_sb", 65536, [128, 2, NCC, 256], BF16)
        self.gx_sb = P("gx_sb", CB, [128, 2, 2, 2, 512], F32)
        self.t6scr = P("t6scr", CB + 16384, [128, 512], F32)
        self.x_sb = P("x_sb", CB, [128, 2, C], F32)
        self.xn1_sb = P("xn1_sb", CB + 32768, [128, 2, C], BF16)
        self.lgT = P("lgT", CB, [24, T], F32)
        self.cosT = P("cosT", CB + 8192, [128, T], F32)
        self.sinP = P("sinP", CB + 16384, [128, T], F32)
        self.cscr = P("cscr", CB + 24576, [128, 512], F32)
        self.s1scr = P("s1scr", CB + 26624, [128, 512], F32)
        self.ropeE = P("ropeE", CB + 28672, [128, 1024], BF16)
        self.ropeO = P("ropeO", CB + 30720, [128, 1024], BF16)
        self.akmix = P("akmix", CB + 32768 - 128, [128, 2, 10], BF16)
        self.Z = P("Z", CB, [128, NT * 16], F32)
        self.T0 = P("T0", CB + 1024, [128, NT * 16], F32)
        self.E = P("E", CB + 2048, [128, NT * 16], F32)
        self.Su = P("Su", CB + 3072, [128, NT * 4], F32)
        self.lnSu = P("lnSu", CB + 3328, [128, NT * 4], F32)
        self.Sv = P("Sv", CB + 3584, [128, NT * 4], F32)
        self.lnSv = P("lnSv", CB + 3840, [128, NT * 4], F32)
        self.LG = P("LG", CB + 4096, [128, NT * 8], F32)
        self.E2 = P("E2", CB + 4608, [128, NT * 8], F32)
        self.S2 = P("S2", CB + 5120, [128, NT * 2], F32)
        self.R2 = P("R2", CB + 5248, [128, NT * 2], F32)
        self.HP = P("HP", CB + 5376, [128, NT * 8], F32)
        self.Hfin = P("Hfin", CB + 5888, [128, NT * 16], F32)
        self.Mu = P("Mu", CB + 6912, [128, NT * 4], F32)
        self.Mv = P("Mv", CB + 7168, [128, NT * 4], F32)
        self.lnR = P("lnR", CB + 7424, [128, NT * 4], F32)
        self.M2 = P("M2", CB + 7680, [128, NT * 2], F32)
        self.xpre_sb = P("xpre_sb", CB + 8192, [128, 2, 1024], BF16)
        self.wb_sb = P("wb_sb", CB + 12288, [128, 8, 1024], BF16)
        self.xn3_sb = P("xn3_sb", CB + 28672, [128, 2, C], BF16)
        self.acc_sb = P("acc_sb", CB + 45056, [128, 2048], F32)
        self.g_sb = P("g_sb", CB + 53248, [128, 2048], F32)
        self.kT_sb = P("kT_sb", 65536, [128, 2, T], BF16)
        self.qT_sb = P("qT_sb", 73728, [128, 2, T], BF16)
        self.v_sb = P("v_sb", 81920, [128, 2, 16, 144], BF16)
        self.pT_sb = P("pT_sb", 91136, [128, 2, 2, 256], BF16)
        self.ea_sb = P("ea_sb", 95328, [128, 2, 160], F32)
        self.asum_sb = P("asum_sb", 96608, [128, 2, 16], F32)
        self.gra_sb = P("gra_sb", 96736, [128, 2, 16], F32)
        self.ap_sb = P("ap_sb", 93184, [128, 20], BF16)
        self.apT_sb = P("apT_sb", 93248, [16, 2, 256], BF16)
        self.ysb = P("ysb", 94272, [128, 2, 2, 128], BF16)
        self.ascr = P("ascr", 95296, [128, 8], F32)

        self.PSA = nc.alloc_psum_tensor("PSA", [128, 1024], F32)
        self.PSB = nc.alloc_psum_tensor("PSB", [128, 1024], F32)
        _ps4 = nc.alloc_psum_tensor("PS4", [128, 512], F32)
        _ps5 = nc.alloc_psum_tensor("PS5", [128, 512], F32)
        _ps6 = nc.alloc_psum_tensor("PS6", [128, 512], F32)
        self.PS7 = nc.alloc_psum_tensor("PS7", [128, 1024], BF16)
        self.PS = [self.PSA[:, 0:512], self.PSA[:, 512:1024],
                   self.PSB[:, 0:512], self.PSB[:, 512:1024],
                   _ps4[:, :], _ps5[:, :], _ps6[:, :]]

        self.sems = {}
        for name in ["sLD", "sSP", "sPE", "sAE", "sVE", "sOUT", "sCC", "sXG"]:
            self.sems[name] = nc.alloc_semaphore(name)
        self.sem_poll = nc.alloc_semaphore("sPOLL")
        self.sem_vc = nc.alloc_semaphore("sVC")
        self._vc = 0
        self.sem_ac = nc.alloc_semaphore("sAC")
        self._ac = 0
        self.sem_bar = nc.alloc_semaphore("sBAR")
        self._nbar = 0
        self.M = {}
        self.cnt = {k: 0 for k in self.sems}
        self.mode = "plan"

    def _place(self, name, off, shape, dtype):
        nbytes = int(np.prod(shape[1:])) * mybir.dt.size(dtype)
        assert off + nbytes <= 159744, (name, off, nbytes)
        return self.nc.alloc_sbuf_tensor_at(name, shape, dtype,
                                            offset=self._b + off)

    def inc(self, inst, sem, key, n=1):
        if self.mode == "plan":
            self.cnt[sem] += n
            assert key not in self.M, key
            self.M[key] = self.cnt[sem]
        else:
            assert inst is not None, f"milestone {key} has no instruction"
            inst.then_inc(self.sems[sem], n)
        return inst

    def V(self, eng, inst):
        if self.mode == "vector":
            self._vc += 1
            inst.then_inc(self.sem_vc, 1)
            eng.wait_ge(self.sem_vc, self._vc)
        return inst

    def SA(self, eng, inst):
        if self.mode == "scalar":
            self._ac += 1
            inst.then_inc(self.sem_ac, 1)
            eng.wait_ge(self.sem_ac, self._ac)
        return inst

    def VM(self, eng, key):
        if self.mode == "vector":
            eng.wait_ge(self.sems["sVE"], self.M[key])

    def wait(self, eng, sem, key):
        if self.mode == "plan":
            return
        eng.wait_ge(self.sems[sem], self.M[key])

    def on(self, which):
        return self.mode == which

    def build(self):
        with self.nc.Block() as block:
            self.mode = "plan"
            self._walk(None)
            for mode in ["sync", "tensor", "scalar", "vector", "gpsimd"]:
                self.mode = mode
                getattr(block, mode)(lambda eng: self._walk(eng))
        return self.nc

    def _walk(self, eng):
        phases = [self.ph0_consts, self.ph1_ln, self.ph2_qkv, self.ph34,
                  self.ph5_gather, self.ph6_proj]
        for i, fn in enumerate(phases):
            if i > self.nphase:
                break
            fn(eng)
            if i < len(phases) - 1:
                self.barrier(eng, i)
        if self.dump is not None:
            self.ph_dump(eng)

    def ph34(self, eng):
        self._c4 = {}
        self.ph3_front(eng)
        mix_sched = {11: [0, 1, 2], 12: [3, 4, 5], 13: [6, 7, 8],
                     14: [9, 10, 11], 15: [12, 13, 14, 15]}
        for h in range(HC):
            self.ph4_head(eng, h)
            if h < 10:
                for k in range(5):
                    it = (10 * h + 2 * k) // 2
                    self.ph3_sink(eng, it, 0)
                    self.ph3_sink(eng, it, 1)
            elif h == 10:
                self.ph3_fin(eng)
            else:
                for tt in mix_sched[h]:
                    self.ph3_mix(eng, tt)

    def ph_dump(self, eng):
        src = getattr(self, self.dump)
        n = int(np.prod(src.shape))
        if self.dbg is None:
            self.dbg = self.nc.dram_tensor("dbg", [1, n], src.dtype,
                                           kind="ExternalOutput")
        if self.on("gpsimd") or self.on("plan"):
            inst = None
            if self.on("gpsimd"):
                inst = eng.dma_start(out=self.dbg[0, :], in_=src.ap().flatten())
            self.inc(inst, "sSP", "dumpd", 16)
            self.wait(eng, "sSP", "dumpd")

    def barrier(self, eng, i):
        marks = {"sync": "sLD", "tensor": "sPE", "scalar": "sAE",
                 "vector": "sVE", "gpsimd": "sSP"}
        if self.mode == "plan":
            self._nbar += 1
            for e, sm in marks.items():
                self.M[f"bar{i}_{e}"] = self.cnt[sm]
            self.M[f"bar{i}"] = 5 * self._nbar
            return
        eng.wait_ge(self.sems[marks[self.mode]], self.M[f"bar{i}_{self.mode}"])
        eng.nop().then_inc(self.sem_bar, 1)
        eng.wait_ge(self.sem_bar, self.M[f"bar{i}"])

    def ph0_consts(self, eng):
        if self.on("sync") or self.on("plan"):
            loads = [
                (self.trapAB[:, :], self.trapAB_d[:, :]),
                (self.trapAB[:, :], self.trapAB_d[:, :]),
                (self.eye_bf[:, :], self.eyebf_d[:, :]),
                (self.eye_f32[:, :], self.eyef32_d[:, :]),
                (self.hb_b[:, :], bass.AP(self.hbase, 0, [[0, 128], [1, 24]])),
                (self.eyeP_b[:, :], bass.AP(self.eyeP, 0, [[0, 128], [1, 16]])),
                (self.gating_b[:, :], bass.AP(self.gating, 0, [[0, 128], [1, 1]])),
                (self.gate_b[:, :], bass.AP(self.gate, 0, [[0, 128], [1, 1]])),
            ]
            for i, (dst, src) in enumerate(loads):
                inst = eng.dma_start(out=dst, in_=src) if self.on("sync") else None
                self.inc(inst, "sLD", f"c{i}", 16)
        if self.on("gpsimd") or self.on("plan"):
            casts = [(self.wffn_sb[:, :, :], self.wffnT_t[:, :, :]),
                     (self.wte_sb[:, :, :], self.wteT_t[:, :, :]),
                     (self.ntl[:, :], self.nonce[:, :])]
            for i, (dst, src) in enumerate(casts):
                inst = eng.dma_start(out=dst, in_=src) if self.on("gpsimd") else None
                self.inc(inst, "sSP", f"cw{i}", 16)
        if self.on("scalar") or self.on("plan"):
            self.wait(eng, "sLD", "c7")
            inst = None
            if self.on("scalar"):
                inst = eng.activation(self.s_tiny[:, 0:1], self.gate_b[:, :],
                                      AF.Exp, scale=np.float32(-1.0))
            self.inc(inst, "sAE", "gsx")
        if self.on("vector") or self.on("plan"):
            self.wait(eng, "sAE", "gsx")
            self.wait(eng, "sLD", "c4")
            inst = None
            if self.on("vector"):
                self.V(eng, eng.tensor_scalar_add(self.s_tiny[:, 1:2],
                                                  self.s_tiny[:, 0:1], 1.0))
                eng.reciprocal(self.gs[:, :], self.s_tiny[:, 1:2])
                inst = eng.tensor_scalar_mul(self.hb10[:, :], self.hb_b[:, 0:16], 10.0)
            self.inc(inst, "sVE", "hb10")

    def ph1_ln(self, eng):
        inv_c = 1.0 / C
        for tp in range(NT // 2):
            uu = (2 * tp, 2 * tp + 1)
            if self.on("sync") or self.on("plan"):
                for tt in uu:
                    bi = tt % 2
                    if tt >= 2:
                        self.wait(eng, "sVE", f"xn{tt-2}")
                        self.wait(eng, "sAE", f"rsq{tt-2}")
                    inst = eng.dma_start(out=self.x_sb[:, bi, :],
                                         in_=self.x_full[tt * 128:(tt + 1) * 128, :]) if self.on("sync") else None
                    self.inc(inst, "sLD", f"x{tt}", 16)
            if self.on("vector") or self.on("plan"):
                st = self.s_tiny
                for tt in uu:
                    bi = tt % 2; o = 8 * bi
                    self.wait(eng, "sLD", f"x{tt}")
                    if self.on("vector"):
                        self.V(eng, eng.reduce_sum(st[:, o:o+1],
                                                   self.x_sb[:, bi, :], AX.X))
                for tt in uu:
                    bi = tt % 2; o = 8 * bi
                    self.wait(eng, "sAE", f"sq{tt}")
                    inst = None
                    if self.on("vector"):
                        self.V(eng, eng.tensor_scalar_mul(st[:, o+2:o+3],
                                                          st[:, o:o+1], inv_c))
                        self.V(eng, eng.tensor_mul(st[:, o+3:o+4], st[:, o+2:o+3],
                                                   st[:, o+2:o+3]))
                        self.V(eng, eng.tensor_scalar_mul(st[:, o+4:o+5],
                                                          st[:, o+1:o+2], inv_c))
                        inst = eng.tensor_sub(st[:, o+5:o+6], st[:, o+4:o+5],
                                              st[:, o+3:o+4])
                    self.inc(inst, "sVE", f"var{tt}")
                for tt in uu:
                    bi = tt % 2; o = 8 * bi
                    self.wait(eng, "sAE", f"rstd{tt}")
                    if tt >= 2:
                        self.wait(eng, "sSP", f"xnsp{tt-2}")
                    inst = None
                    if self.on("vector"):
                        inst = eng.tensor_scalar(self.xn1_sb[:, bi, :],
                                                 self.x_sb[:, bi, :],
                                                 st[:, o+2:o+3], st[:, o+7:o+8],
                                                 ALU.subtract, ALU.mult)
                    self.inc(inst, "sVE", f"xn{tt}")
            if self.on("scalar") or self.on("plan"):
                st = self.s_tiny
                for tt in uu:
                    bi = tt % 2; o = 8 * bi
                    self.wait(eng, "sLD", f"x{tt}")
                    inst = None
                    if self.on("scalar"):
                        inst = eng.activation(self.sqscr[:, :], self.x_sb[:, bi, :],
                                              AF.Square, accum_out=st[:, o+1:o+2])
                    self.inc(inst, "sAE", f"sq{tt}")
                for tt in uu:
                    bi = tt % 2; o = 8 * bi
                    self.wait(eng, "sVE", f"var{tt}")
                    inst = None
                    if self.on("scalar"):
                        self.SA(eng, eng.activation(st[:, o+6:o+7], st[:, o+5:o+6],
                                                    AF.Ln, bias=np.float32(LN_EPS)))
                        inst = eng.activation(st[:, o+7:o+8], st[:, o+6:o+7],
                                              AF.Exp, scale=np.float32(-0.5))
                    self.inc(inst, "sAE", f"rstd{tt}")
                for tt in uu:
                    bi = tt % 2; o = 8 * bi
                    self.wait(eng, "sVE", f"xn{tt}")
                    inst = None
                    if self.on("scalar"):
                        inst = eng.activation(self.sqscr[:, :], self.xn1_sb[:, bi, :],
                                              AF.Square, accum_out=st[:, o+1:o+2])
                    self.inc(inst, "sAE", f"rsq{tt}")
                for tt in uu:
                    bi = tt % 2; o = 8 * bi
                    if self.on("scalar"):
                        eng.wait_ge(self.sems["sAE"], self.M[f"rsq{tt}"])
                    inst = None
                    if self.on("scalar"):
                        self.SA(eng, eng.activation(st[:, o+6:o+7], st[:, o+1:o+2],
                                                    AF.Ln, scale=np.float32(inv_c),
                                                    bias=np.float32(RMS_EPS)))
                        inst = eng.activation(self.rinv_all[:, tt:tt + 1],
                                              st[:, o+6:o+7], AF.Exp,
                                              scale=np.float32(-0.5))
                    self.inc(inst, "sAE", f"rinv{tt}")
            if self.on("gpsimd") or self.on("plan"):
                for tt in uu:
                    bi = tt % 2
                    self.wait(eng, "sVE", f"xn{tt}")
                    inst = eng.dma_start(out=self.xn_dram[tt * 128:(tt + 1) * 128, :],
                                         in_=self.xn1_sb[:, bi, :]) if self.on("gpsimd") else None
                    self.inc(inst, "sSP", f"xnsp{tt}", 16)
        if self.on("vector") or self.on("plan"):
            self.wait(eng, "sAE", f"rinv{NT-1}")
            inst = None
            if self.on("vector"):
                inst = eng.tensor_scalar_mul(self.rinv10[:, :], self.rinv_all[:, :],
                                             10.0)
            self.inc(inst, "sVE", "rinvten")

    def ph2_qkv(self, eng):
        if self.on("sync") or self.on("plan"):
            for i, (dst, src) in enumerate([(self.cosT, self.cosT_d),
                                            (self.sinP, self.sinP_d)]):
                inst = eng.dma_start(out=dst[:, :], in_=src[:, :]) if self.on("sync") else None
                self.inc(inst, "sLD", f"rt{i}", 16)
        bprev = {}
        rprev = {}
        akprev = None
        for tp in range(2):
            if self.on("sync") or self.on("plan"):
                if tp == 1:
                    self.wait(eng, "sPE", "lg0")
                for cc in range(NCC):
                    src = self.xn_dram[tp * 1024:(tp + 1) * 1024,
                                       cc * 128:(cc + 1) * 128]
                    inst = None
                    if self.on("sync"):
                        inst = eng.dma_start(out=self.xnT[:, cc, :], in_=src,
                                             transpose=True)
                    self.inc(inst, "sLD", f"xnT{tp}_{cc}", 16)
            for p in range(16):
                is_k = p >= 8
                sa, sb = (0, 1) if p % 2 == 0 else (2, 3)
                if self.on("gpsimd") or self.on("plan"):
                    if p >= 2:
                        self.wait(eng, "sPE", f"qk{tp}_{p-2}_1")
                    elif tp == 1:
                        self.wait(eng, "sPE", "lg0")
                    for z, slot in ((0, sa), (1, sb)):
                        inst = None
                        if self.on("gpsimd"):
                            inst = eng.dma_start(out=self.w_col[:, slot, :, :],
                                                 in_=self.wqkvT_t[2 * p + z, :, :, :])
                        self.inc(inst, "sSP", f"wc{tp}_{p}_{z}", 16)
                for win in range(2):
                    pse = self.PS[2 * win]
                    pso = self.PS[2 * win + 1]
                    key = f"qk{tp}_{p}_{win}"
                    rider = is_k and tp == 0 and win == 0
                    if self.on("tensor") or self.on("plan"):
                        self.wait(eng, "sSP", f"wc{tp}_{p}_{1}")
                        self.wait(eng, "sLD", f"xnT{tp}_{NCC-1}")
                        tag = f"b{win}"
                        if tag in bprev:
                            self.wait(eng, bprev[tag][0], bprev[tag][1])
                        bprev[tag] = ("sVE", f"rp{key}")
                        if rider:
                            for tg in ("r4", "r5"):
                                if tg in bprev:
                                    self.wait(eng, bprev[tg][0], bprev[tg][1])
                                bprev[tg] = ("sAE", f"akm{tp}_{p}")
                        inst = None
                        for c in range(NCC):
                            if self.on("tensor"):
                                eng.matmul(pse[:, :], self.w_col[:, sa, c, :],
                                           self.xnT[:, c, win * 512:(win + 1) * 512],
                                           start=(c == 0), stop=(c == NCC - 1))
                                inst = eng.matmul(
                                    pso[:, :], self.w_col[:, sb, c, :],
                                    self.xnT[:, c, win * 512:(win + 1) * 512],
                                    start=(c == 0), stop=(c == NCC - 1))
                                if rider:
                                    eng.matmul(self.PS[4][:, 0:10],
                                               self.w_col[:, sa, c, :],
                                               self.wte_sb[:, c, :],
                                               start=(c == 0), stop=(c == NCC - 1))
                                    inst = eng.matmul(self.PS[5][:, 0:10],
                                                      self.w_col[:, sb, c, :],
                                                      self.wte_sb[:, c, :],
                                                      start=(c == 0),
                                                      stop=(c == NCC - 1))
                        self.inc(inst, "sPE", key)
                    if self.on("vector") or self.on("plan"):
                        self.wait(eng, "sPE", key)
                        if win == 0 and tp == 0 and p == 0:
                            self.wait(eng, "sLD", "rt1")
                        for tag in ("rE", "rO"):
                            if win == 0 and tag in rprev:
                                self.wait(eng, "sLD", rprev[tag])
                        half = slice(tp * 1024 + win * 512,
                                     tp * 1024 + (win + 1) * 512)
                        wsl = slice(win * 512, (win + 1) * 512)
                        inst = None
                        if self.on("vector"):
                            self.V(eng, eng.tensor_mul(self.cscr[:, :], pse[:, :],
                                                       self.cosT[:, half]))
                            self.V(eng, eng.tensor_mul(self.s1scr[:, :], pso[:, :],
                                                       self.sinP[:, half]))
                            self.V(eng, eng.tensor_sub(self.ropeE[:, wsl],
                                                       self.cscr[:, :],
                                                       self.s1scr[:, :]))
                            self.V(eng, eng.tensor_mul(self.cscr[:, :], pso[:, :],
                                                       self.cosT[:, half]))
                            self.V(eng, eng.tensor_mul(self.s1scr[:, :], pse[:, :],
                                                       self.sinP[:, half]))
                            inst = eng.tensor_add(self.ropeO[:, wsl],
                                                  self.cscr[:, :], self.s1scr[:, :])
                        self.inc(inst, "sVE", f"rp{key}")
                        self.VM(eng, f"rp{key}")
                if is_k and tp == 0:
                    if self.on("scalar") or self.on("plan"):
                        self.wait(eng, "sPE", f"qk{tp}_{p}_0")
                        if akprev is not None:
                            self.wait(eng, "sLD", akprev)
                        inst = None
                        if self.on("scalar"):
                            eng.activation(self.akmix[:, 0, :], self.PS[4][:, 0:10],
                                           AF.Copy)
                            inst = eng.activation(self.akmix[:, 1, :],
                                                  self.PS[5][:, 0:10], AF.Copy)
                        self.inc(inst, "sAE", f"akm{tp}_{p}")
                    if self.on("sync") or self.on("plan"):
                        self.wait(eng, "sAE", f"akm{tp}_{p}")
                        hA = 2 * (p - 8)
                        for z in range(2):
                            dst = bass.AP(
                                self.akav_dram,
                                (0 * HC + hA) * 1280 + z * 640,
                                [[1280, 2], [10, 64], [1, 10]])
                            inst = None
                            if self.on("sync"):
                                inst = eng.dma_start(out=dst,
                                                     in_=self.akmix[:, z, :])
                            self.inc(inst, "sLD", f"aks{tp}_{p}_{z}", 16)
                        akprev = f"aks{tp}_{p}_1"
                if self.on("sync") or self.on("plan"):
                    self.wait(eng, "sVE", f"rpqk{tp}_{p}_1")
                    hA = 2 * (p % 8) + (16 if is_k else 0)
                    half = slice(tp * 1024, (tp + 1) * 1024)
                    outs = [
                        (self.qkT_dram[hA, 0:64, half], self.ropeE[0:64, :]),
                        (self.qkT_dram[hA + 1, 0:64, half], self.ropeE[64:128, :]),
                        (self.qkT_dram[hA, 64:128, half], self.ropeO[0:64, :]),
                        (self.qkT_dram[hA + 1, 64:128, half], self.ropeO[64:128, :]),
                    ]
                    for z, (dst, srcb) in enumerate(outs):
                        inst = None
                        if self.on("sync"):
                            inst = eng.dma_start(out=dst, in_=srcb)
                        self.inc(inst, "sLD", f"spl{tp}_{p}_{z}", 16)
                    rprev["rE"] = f"spl{tp}_{p}_1"
                    rprev["rO"] = f"spl{tp}_{p}_3"
            for oc in range(32, 48):
                h = oc - 32
                sa, sb = (0, 1) if oc % 2 == 0 else (2, 3)
                rider = tp == 0
                key = f"v{tp}_{oc}"
                if self.on("gpsimd") or self.on("plan"):
                    pk = (f"v{tp}_{oc-2}" if oc >= 34 else f"qk{tp}_{14+(oc-32)}_1")
                    self.wait(eng, "sPE", pk)
                    inst = None
                    if self.on("gpsimd"):
                        inst = eng.dma_start(out=self.w_col[:, sa, :, :],
                                             in_=self.wqkvT_t[oc, :, :, :])
                    self.inc(inst, "sSP", f"wcv{tp}_{oc}", 16)
                if self.on("tensor") or self.on("plan"):
                    self.wait(eng, "sSP", f"wcv{tp}_{oc}")
                    psa, psb = self.PS[sa], self.PS[sb]
                    tag = "b0" if sa == 0 else "b1"
                    if tag in bprev:
                        self.wait(eng, bprev[tag][0], bprev[tag][1])
                    bprev[tag] = ("sAE", f"ev{key}")
                    if rider:
                        tg = "r4" if oc % 2 == 0 else "r5"
                        if tg in bprev:
                            self.wait(eng, bprev[tg][0], bprev[tg][1])
                        bprev[tg] = ("sAE", f"akmv{tp}_{oc}")
                    inst = None
                    for c in range(NCC):
                        if self.on("tensor"):
                            eng.matmul(psa[:, :], self.w_col[:, sa, c, :],
                                       self.xnT[:, c, 0:512],
                                       start=(c == 0), stop=(c == NCC - 1))
                            inst = eng.matmul(psb[:, :], self.w_col[:, sa, c, :],
                                              self.xnT[:, c, 512:1024],
                                              start=(c == 0), stop=(c == NCC - 1))
                            if rider:
                                inst = eng.matmul(
                                    self.PS[4 + (oc % 2)][:, 0:10],
                                    self.w_col[:, sa, c, :], self.wte_sb[:, c, :],
                                    start=(c == 0), stop=(c == NCC - 1))
                    self.inc(inst, "sPE", key)
                if self.on("scalar") or self.on("plan"):
                    self.wait(eng, "sPE", key)
                    if "rE" in rprev:
                        self.wait(eng, "sLD", rprev["rE"])
                    inst = None
                    if self.on("scalar"):
                        eng.activation(self.ropeE[:, 0:512], self.PS[sa][:, :],
                                       AF.Copy)
                        inst = eng.activation(self.ropeE[:, 512:1024],
                                              self.PS[sb][:, :], AF.Copy)
                    self.inc(inst, "sAE", f"ev{key}")
                    if rider:
                        if akprev is not None:
                            self.wait(eng, "sLD", akprev)
                        inst = None
                        if self.on("scalar"):
                            inst = eng.activation(self.akmix[:, 0, :],
                                                  self.PS[4 + (oc % 2)][:, 0:10],
                                                  AF.Copy)
                        self.inc(inst, "sAE", f"akmv{tp}_{oc}")
                if self.on("sync") or self.on("plan"):
                    self.wait(eng, "sAE", f"ev{key}")
                    inst = None
                    if self.on("sync"):
                        inst = eng.dma_start(
                            out=self.qkT_dram[oc, :, tp * 1024:(tp + 1) * 1024],
                            in_=self.ropeE[:, :])
                    self.inc(inst, "sLD", f"vsp{key}", 16)
                    rprev["rE"] = f"vsp{key}"
                    if rider:
                        self.wait(eng, "sAE", f"akmv{tp}_{oc}")
                        inst = None
                        if self.on("sync"):
                            inst = eng.dma_start(
                                out=self.akav_dram[1, h, :, :],
                                in_=self.akmix[:, 0, :])
                        self.inc(inst, "sLD", f"avs{tp}_{oc}", 16)
                        akprev = f"avs{tp}_{oc}"
            if self.on("tensor") or self.on("plan"):
                self.wait(eng, "sLD", f"xnT{tp}_{NCC-1}")
                for tg in ("r4", "r5"):
                    if tg in bprev:
                        self.wait(eng, bprev[tg][0], bprev[tg][1])
                    bprev[tg] = ("sAE", f"evlg{tp}")
                inst = None
                for c in range(NCC):
                    if self.on("tensor"):
                        lhsT = self.wffn_sb[:, c, :]
                        eng.matmul(self.PS[4][0:24, :], lhsT,
                                   self.xnT[:, c, 0:512],
                                   start=(c == 0), stop=(c == NCC - 1))
                        inst = eng.matmul(self.PS[5][0:24, :], lhsT,
                                          self.xnT[:, c, 512:1024],
                                          start=(c == 0), stop=(c == NCC - 1))
                self.inc(inst, "sPE", f"lg{tp}")
            if self.on("scalar") or self.on("plan"):
                self.wait(eng, "sPE", f"lg{tp}")
                inst = None
                if self.on("scalar"):
                    eng.activation(self.lgT[0:24, tp * 1024:tp * 1024 + 512],
                                   self.PS[4][0:24, :], AF.Copy)
                    inst = eng.activation(
                        self.lgT[0:24, tp * 1024 + 512:(tp + 1) * 1024],
                        self.PS[5][0:24, :], AF.Copy)
                self.inc(inst, "sAE", f"evlg{tp}")
        for piece in range(NT):
            if self.on("tensor") or self.on("plan"):
                self.wait(eng, "sAE", "evlg1" if piece == 0 else f"lgte{piece-1}")
                inst = None
                if self.on("tensor"):
                    inst = eng.transpose(
                        self.PS[6][:, 0:24],
                        self.lgT[0:24, piece * 128:(piece + 1) * 128],
                        self.eye_f32[0:24, 0:24])
                self.inc(inst, "sPE", f"lgtr{piece}")
            if self.on("scalar") or self.on("plan"):
                self.wait(eng, "sPE", f"lgtr{piece}")
                inst = None
                if self.on("scalar"):
                    inst = eng.activation(self.logits_tm[:, piece, :],
                                          self.PS[6][:, 0:24], AF.Copy)
                self.inc(inst, "sAE", f"lgte{piece}")
        if self.on("sync") or self.on("plan"):
            self.wait(eng, "sLD", "avs0_47")
            inst = None
            if self.on("sync"):
                inst = eng.dma_start(
                    out=self.akT[:, :, :],
                    in_=bass.AP(self.akav_dram, 0, [[10, 128], [1280, HC], [1, 10]]))
            self.inc(inst, "sLD", "akl", 16)
            inst = None
            if self.on("sync"):
                inst = eng.dma_start(
                    out=self.avT[:, :, :],
                    in_=bass.AP(self.akav_dram, HC * 1280,
                                [[10, 128], [1280, HC], [1, 10]]))
            self.inc(inst, "sLD", "avl", 16)
        if self.on("vector") or self.on("plan"):
            inst = eng.memset(self.av_sb[:, :, :], 0.0) if self.on("vector") else None
            self.inc(inst, "sVE", "avz")
        for h in range(HC):
            if self.on("tensor") or self.on("plan"):
                self.wait(eng, "sLD", "avl")
                if h > 0:
                    self.wait(eng, "sAE", f"avev{h-1}")
                inst = None
                if self.on("tensor"):
                    inst = eng.transpose(self.PS7[0:10, 0:128],
                                         self.avT[:, h, :], self.eye_bf[:, :])
                self.inc(inst, "sPE", f"avtr{h}")
            if self.on("scalar") or self.on("plan"):
                self.wait(eng, "sPE", f"avtr{h}")
                self.wait(eng, "sVE", "avz")
                inst = None
                if self.on("scalar"):
                    inst = eng.activation(self.av_sb[0:10, h, 0:128],
                                          self.PS7[0:10, 0:128], AF.Copy)
                self.inc(inst, "sAE", f"avev{h}")

    def ph3_front(self, eng):
        v4 = [[256, 128], [16, NT], [4, 4], [1, 4]]
        if self.on("vector") or self.on("plan"):
            inst = None
            for tt in range(NT):
                if self.on("vector"):
                    eng.scalar_tensor_tensor(
                        self.Z[:, tt * 16:(tt + 1) * 16],
                        self.logits_tm[:, tt, 0:16], self.rinv10[:, tt:tt + 1],
                        self.hb10[:, :], ALU.mult, ALU.add)
                    inst = eng.scalar_tensor_tensor(
                        self.LG[:, tt * 8:(tt + 1) * 8],
                        self.logits_tm[:, tt, 16:24], self.rinv_all[:, tt:tt + 1],
                        self.hb_b[:, 16:24], ALU.mult, ALU.add)
            self.inc(inst, "sVE", "zlg")
            inst = eng.memset(self.lnSv[:, :], 0.0) if self.on("vector") else None
            self.inc(inst, "sVE", "lnsv0")
        if self.on("vector") or self.on("plan"):
            inst = None
            if self.on("vector"):
                self.V(eng, eng.reduce_max(
                    bass.AP(self.M2, 0, [[32, 128], [1, 32]]),
                    bass.AP(self.LG, 0, [[128, 128], [4, 32], [1, 4]]), AX.X))
                inst = eng.tensor_tensor(
                    bass.AP(self.LG, 0, [[128, 128], [4, 32], [1, 4]]),
                    bass.AP(self.LG, 0, [[128, 128], [4, 32], [1, 4]]),
                    bass.AP(self.M2, 0, [[32, 128], [1, 32], [0, 4]]),
                    ALU.subtract)
            self.inc(inst, "sVE", "lgsh")
        if self.on("scalar") or self.on("plan"):
            self.wait(eng, "sVE", "lgsh")
            inst = None
            if self.on("scalar"):
                inst = eng.activation(self.E2[:, :], self.LG[:, :], AF.Exp)
            self.inc(inst, "sAE", "e2")
        if self.on("vector") or self.on("plan"):
            self.wait(eng, "sAE", "e2")
            inst = None
            if self.on("vector"):
                self.V(eng, eng.reduce_sum(
                    bass.AP(self.S2, 0, [[32, 128], [1, 32]]),
                    bass.AP(self.E2, 0, [[128, 128], [4, 32], [1, 4]]), AX.X))
                self.V(eng, eng.reciprocal(self.R2[:, :], self.S2[:, :]))
                inst = eng.tensor_tensor(
                    bass.AP(self.HP, 0, [[128, 128], [4, 32], [1, 4]]),
                    bass.AP(self.E2, 0, [[128, 128], [4, 32], [1, 4]]),
                    bass.AP(self.R2, 0, [[32, 128], [1, 32], [0, 4]]),
                    ALU.mult)
            self.inc(inst, "sVE", "hp")
            self.VM(eng, "hp")
        for tt in range(NT):
            bi = tt % 2
            rows = slice(tt * 128, (tt + 1) * 128)
            if self.on("sync") or self.on("plan"):
                if tt >= 2:
                    self.wait(eng, "sVE", f"xpre{tt-2}")
                inst = eng.dma_start(out=self.xn3_sb[:, bi, :],
                                     in_=self.xn_dram[rows, :]) if self.on("sync") else None
                self.inc(inst, "sLD", f"xn3a{tt}", 16)
            if self.on("vector") or self.on("plan"):
                self.wait(eng, "sLD", f"xn3a{tt}")
                if tt >= 2:
                    self.wait(eng, "sSP", f"xpsp{tt-2}")
                inst = None
                if self.on("vector"):
                    hp = self.HP
                    xn = self.xn3_sb[:, bi, :]
                    acc = self.acc_sb[:, 0:1024]
                    self.V(eng, eng.tensor_scalar_mul(acc, xn[:, 0:1024],
                                                      hp[:, tt * 8:tt * 8 + 1]))
                    for n in range(1, 4):
                        o = acc if n < 3 else self.xpre_sb[:, bi, :]
                        inst = eng.scalar_tensor_tensor(
                            o, xn[:, n * 1024:(n + 1) * 1024],
                            hp[:, tt * 8 + n:tt * 8 + n + 1],
                            acc, ALU.mult, ALU.add)
                        if n < 3:
                            self.V(eng, inst)
                self.inc(inst, "sVE", f"xpre{tt}")
                self.VM(eng, f"xpre{tt}")
            if self.on("gpsimd") or self.on("plan"):
                self.wait(eng, "sVE", f"xpre{tt}")
                inst = eng.dma_start(out=self.xpre_dram[rows, :],
                                     in_=self.xpre_sb[:, bi, :]) if self.on("gpsimd") else None
                self.inc(inst, "sSP", f"xpsp{tt}", 16)
        if self.on("gpsimd") or self.on("plan"):
            inst = eng.dma_start(out=self.wb_sb[:, :, :],
                                 in_=self.wbranch_t[:, :, :]) if self.on("gpsimd") else None
            self.inc(inst, "sSP", "wb", 16)
        if self.on("sync") or self.on("plan"):
            self.wait(eng, "sSP", f"xpsp{NT-1}")
            for cc in range(8):
                inst = None
                if self.on("sync"):
                    inst = eng.dma_start(
                        out=self.xpreT[:, cc, :],
                        in_=self.xpre_dram[:, cc * 128:(cc + 1) * 128],
                        transpose=True)
                self.inc(inst, "sLD", f"xpT{cc}", 16)
        for tt in range(NT):
            pa, pb = (0, 1) if tt % 2 == 0 else (2, 3)
            if self.on("tensor") or self.on("plan"):
                self.wait(eng, "sLD", "xpT7")
                self.wait(eng, "sSP", "wb")
                if tt >= 2:
                    self.wait(eng, "sAE", f"ybre{tt-2}")
                inst = None
                for ccb in range(8):
                    if self.on("tensor"):
                        lhsT = self.xpreT[:, ccb, tt * 128:(tt + 1) * 128]
                        eng.matmul(self.PS[pa][:, :], lhsT,
                                   self.wb_sb[:, ccb, 0:512],
                                   start=(ccb == 0), stop=(ccb == 7))
                        inst = eng.matmul(self.PS[pb][:, :], lhsT,
                                          self.wb_sb[:, ccb, 512:1024],
                                          start=(ccb == 0), stop=(ccb == 7))
                self.inc(inst, "sPE", f"ybr{tt}")
            if self.on("scalar") or self.on("plan"):
                self.wait(eng, "sPE", f"ybr{tt}")
                inst = None
                if self.on("scalar"):
                    eng.activation(self.y_br[:, tt, 0:512], self.PS[pa][:, :],
                                   AF.Copy)
                    inst = eng.activation(self.y_br[:, tt, 512:1024],
                                          self.PS[pb][:, :], AF.Copy)
                self.inc(inst, "sAE", f"ybre{tt}")

    def ph3_sink(self, eng, it, half):
        v4 = [[256, 128], [16, NT], [4, 4], [1, 4]]
        M = self.Mu if half == 0 else self.Mv
        if self.on("vector") or self.on("plan"):
            if it > 0 or half > 0:
                self.wait(eng, "sAE",
                  f"skl{it}_0" if half == 1 else f"skl{it-1}_1")
            src = self.lnSv if half == 0 else self.lnSu
            pat = ([[64, 128], [4, NT], [0, 4], [1, 4]] if half == 0
                   else [[64, 128], [4, NT], [1, 4], [0, 4]])
            epat = (v4 if half == 0
                    else [[256, 128], [16, NT], [1, 4], [4, 4]])
            gpat = [[64, 128], [4, NT], [1, 4], [0, 4]]
            gout = [[64, 128], [4, NT], [1, 4]]
            Madd = self.Mv if half == 0 else self.Mu
            inst = None
            if self.on("vector"):
                if it > 0 or half > 0:
                    self.V(eng, eng.tensor_add(src[:, :], src[:, :],
                               Madd[:, :]))
                self.V(eng, eng.tensor_tensor(bass.AP(self.T0, 0, v4),
                              bass.AP(self.Z, 0, v4),
                              bass.AP(src, 0, pat),
                              ALU.subtract))
                self.V(eng, eng.reduce_max(bass.AP(M, 0, gout),
                           bass.AP(self.T0, 0, epat),
                           AX.X))
                inst = eng.tensor_tensor(bass.AP(self.T0, 0, epat),
                         bass.AP(self.T0, 0, epat),
                         bass.AP(M, 0, gpat),
                         ALU.subtract)
            self.inc(inst, "sVE", f"skt{it}_{half}")
        if self.on("scalar") or self.on("plan"):
            self.wait(eng, "sVE", f"skt{it}_{half}")
            inst = None
            if self.on("scalar"):
                inst = eng.activation(self.E[:, :], self.T0[:, :], AF.Exp)
            self.inc(inst, "sAE", f"ske{it}_{half}")
        if self.on("vector") or self.on("plan"):
            self.wait(eng, "sAE", f"ske{it}_{half}")
            dst = self.Su if half == 0 else self.Sv
            pat = (v4 if half == 0
                   else [[256, 128], [16, NT], [1, 4], [4, 4]])
            inst = None
            if self.on("vector"):
                inst = eng.reduce_sum(
                    bass.AP(dst, 0, [[64, 128], [4, NT], [1, 4]]),
                    bass.AP(self.E, 0, pat), AX.X)
            self.inc(inst, "sVE", f"sks{it}_{half}")
        if self.on("scalar") or self.on("plan"):
            self.wait(eng, "sVE", f"sks{it}_{half}")
            src = self.Su if half == 0 else self.Sv
            dst = self.lnSu if half == 0 else self.lnSv
            inst = None
            if self.on("scalar"):
                inst = eng.activation(dst[:, :], src[:, :], AF.Ln)
            self.inc(inst, "sAE", f"skl{it}_{half}")

    def ph3_fin(self, eng):
        v4 = [[256, 128], [16, NT], [4, 4], [1, 4]]
        if self.on("vector") or self.on("plan"):
            self.wait(eng, "sAE", f"skl{SINK_ITERS-1}_1")
            inst = None
            if self.on("vector"):
                self.V(eng, eng.tensor_add(self.lnSv[:, :], self.lnSv[:, :],
                                           self.Mv[:, :]))
                self.V(eng, eng.tensor_tensor(
                    bass.AP(self.T0, 0, v4), bass.AP(self.Z, 0, v4),
                    bass.AP(self.lnSu, 0, [[64, 128], [4, NT], [1, 4], [0, 4]]),
                    ALU.subtract))
                inst = eng.tensor_tensor(bass.AP(self.T0, 0, v4),
                                         bass.AP(self.T0, 0, v4),
                                         bass.AP(self.lnSv, 0,
                                                 [[64, 128], [4, NT], [0, 4], [1, 4]]),
                                         ALU.subtract)
            self.inc(inst, "sVE", "hraw")
        if self.on("scalar") or self.on("plan"):
            self.wait(eng, "sVE", "hraw")
            inst = None
            if self.on("scalar"):
                inst = eng.activation(self.E[:, :], self.T0[:, :], AF.Exp)
            self.inc(inst, "sAE", "hexp")
        if self.on("vector") or self.on("plan"):
            self.wait(eng, "sAE", "hexp")
            self.wait(eng, "sVE", "hb10")
            v3 = [[256, 128], [16, NT], [1, 16]]
            eyepat = [[16, 128], [0, NT], [1, 16]]
            inst = None
            if self.on("vector"):
                self.V(eng, eng.tensor_tensor(
                    bass.AP(self.T0, 0, v3), bass.AP(self.E, 0, v3),
                    bass.AP(self.eyeP_b, 0, eyepat), ALU.subtract))
                inst = eng.scalar_tensor_tensor(
                    bass.AP(self.Hfin, 0, v3), bass.AP(self.T0, 0, v3),
                    self.gs[:, 0:1], bass.AP(self.eyeP_b, 0, eyepat),
                    ALU.mult, ALU.add)
            self.inc(inst, "sVE", "hfin")

    def ph3_mix(self, eng, tt):
        bi = tt % 2
        rows = slice(tt * 128, (tt + 1) * 128)
        if self.on("sync") or self.on("plan"):
            if tt >= 2:
                self.wait(eng, "sVE", f"mix{tt-2}")
            else:
                self.wait(eng, "sAE", f"ybre{NT-1}")
            inst = eng.dma_start(out=self.xn3_sb[:, bi, :],
                                 in_=self.xn_dram[rows, :]) if self.on("sync") else None
            self.inc(inst, "sLD", f"xn3b{tt}", 16)
        if self.on("vector") or self.on("plan"):
            self.wait(eng, "sLD", f"xn3b{tt}")
            self.wait(eng, "sVE", "hfin")
            self.wait(eng, "sAE", f"ybre{tt}")
            if tt >= 1:
                self.wait(eng, "sAE", f"silu{tt-1}")
            inst = None
            if self.on("vector"):
                xn = self.xn3_sb[:, bi, :]
                hf = self.Hfin
                for r in range(2):
                    acc = self.acc_sb[:, r * 1024:(r + 1) * 1024]
                    self.V(eng, eng.tensor_scalar_mul(
                        acc, xn[:, 0:1024],
                        hf[:, tt * 16 + 4 * r:tt * 16 + 4 * r + 1]))
                    for jj in range(1, 4):
                        self.V(eng, eng.scalar_tensor_tensor(
                            acc, xn[:, jj * 1024:(jj + 1) * 1024],
                            hf[:, tt * 16 + 4 * r + jj:tt * 16 + 4 * r + jj + 1],
                            acc, ALU.mult, ALU.add))
                    inst = eng.scalar_tensor_tensor(
                        acc, self.y_br[:, tt, :],
                        self.HP[:, tt * 8 + 4 + r:tt * 8 + 4 + r + 1],
                        acc, ALU.mult, ALU.add)
                    if r == 0:
                        self.V(eng, inst)
            self.inc(inst, "sVE", f"mix{tt}")
        if self.on("scalar") or self.on("plan"):
            self.wait(eng, "sVE", f"mix{tt}")
            if tt >= 2:
                self.wait(eng, "sSP", f"gsp{tt-2}")
            inst = None
            if self.on("scalar"):
                inst = eng.activation(self.g_sb[:, :], self.acc_sb[:, :], AF.Silu)
            self.inc(inst, "sAE", f"silu{tt}")
        if self.on("gpsimd") or self.on("plan"):
            self.wait(eng, "sAE", f"silu{tt}")
            inst = eng.dma_start(out=self.g_dram[rows, :],
                                 in_=self.g_sb[:, :]) if self.on("gpsimd") else None
            self.inc(inst, "sSP", f"gsp{tt}", 16)

    def ph4_head(self, eng, h):
        if True:
            hb = h % 2
            if self.on("sync") or self.on("plan"):
                if h >= 2:
                    self.wait(eng, "sPE", f"yadj{h-2}_7")
                inst = eng.dma_start(out=self.kT_sb[:, hb, :],
                                     in_=self.qkT_dram[16 + h, :, :]) if self.on("sync") else None
                self.inc(inst, "sLD", f"kT{h}", 16)
                inst = eng.dma_start(out=self.qT_sb[:, hb, :],
                                     in_=self.qkT_dram[h, :, :]) if self.on("sync") else None
                self.inc(inst, "sLD", f"qT{h}", 16)
                for kc in range(16):
                    inst = None
                    if self.on("sync"):
                        inst = eng.dma_start(
                            out=self.v_sb[:, hb, kc, 0:128],
                            in_=self.qkT_dram[32 + h, :, kc * 128:(kc + 1) * 128],
                            transpose=True)
                    self.inc(inst, "sLD", f"v{h}_{kc}", 16)
            if h < 2 and (self.on("vector") or self.on("plan")):
                inst = None
                if self.on("vector"):
                    inst = eng.memset(self.v_sb[:, hb, :, 128:129], 1.0)
                self.inc(inst, "sVE", f"vones{h}")
            if self.on("tensor") or self.on("plan"):
                if h > 0:
                    self.wait(eng, "sAE", f"ae{h-1}")
                if h == 0:
                    self.wait(eng, "sLD", "akl")
                self.wait(eng, "sLD", f"qT{h}")
                inst = None
                for u in range(16):
                    if self.on("tensor"):
                        inst = eng.matmul(self.PS[4][:, 10 * u:10 * u + 10],
                                          self.qT_sb[:, hb, u * 128:(u + 1) * 128],
                                          self.akT[:, h, :], start=True, stop=True,
                                          skip_group_check=True)
                self.inc(inst, "sPE", f"as{h}")
            if self.on("scalar") or self.on("plan"):
                self.wait(eng, "sPE", f"as{h}")
                if h >= 2:
                    self.wait(eng, "sVE", f"apv{h-2}_7")
                inst = None
                for u in range(16):
                    if self.on("scalar"):
                        inst = eng.activation(self.ea_sb[:, hb, 10 * u:10 * u + 10],
                                              self.PS[4][:, 10 * u:10 * u + 10],
                                              AF.Exp, scale=np.float32(SC_ADPT),
                                              accum_out=self.asum_sb[:, hb, u:u + 1])
                self.inc(inst, "sAE", f"ae{h}")
            if self.on("vector") or self.on("plan"):
                self.wait(eng, "sAE", f"ae{h}")
                inst = None
                if self.on("vector"):
                    self.V(eng, eng.reciprocal(self.gra_sb[:, hb, :],
                                               self.asum_sb[:, hb, :]))
                    inst = eng.tensor_scalar_mul(self.gra_sb[:, hb, :],
                                                 self.gra_sb[:, hb, :],
                                                 self.gating_b[:, 0:1])
                self.inc(inst, "sVE", f"gra{h}")
                self.VM(eng, f"gra{h}")
            for s in range(8):
                nkc = 2 * s + 2
                npk = s + 1
                ye, yo = 5, 6
                skey = f"{h}_{s}"
                for pk in range(npk + 1):
                    if pk < npk:
                        PSP = self.PSA if pk % 2 == 0 else self.PSB
                        key = f"{h}_{s}_{pk}"
                        if self.on("tensor") or self.on("plan"):
                            if h == 0 and s == 0 and pk == 0:
                                self.wait(eng, "sAE", f"ybre{NT-1}")
                                self.wait(eng, "sLD", f"v{h}_15")
                                self.wait(eng, "sVE", f"vones{min(h, 1)}")
                            if s == 0 and pk == 0:
                                self.wait(eng, "sLD", f"v{h}_15")
                            tag = f"stb{pk % 2}"
                            if tag in self._c4:
                                self.wait(eng, "sAE", self._c4[tag])
                            self._c4[tag] = f"pT{key}"
                            inst = None
                            if self.on("tensor"):
                                eng.matmul(PSP[:, 0:256],
                                           self.kT_sb[:, hb, 2 * pk * 128:(2 * pk + 1) * 128],
                                           self.qT_sb[:, hb, s * 256:(s + 1) * 256],
                                           start=True, stop=True)
                                inst = eng.matmul(
                                    PSP[:, 512:768],
                                    self.kT_sb[:, hb, (2 * pk + 1) * 128:(2 * pk + 2) * 128],
                                    self.qT_sb[:, hb, s * 256:(s + 1) * 256],
                                    start=True, stop=True)
                            self.inc(inst, "sPE", f"sT{key}")
                        if self.on("scalar") or self.on("plan"):
                            self.wait(eng, "sPE", f"sT{key}")
                            tag = f"ptb{pk % 2}"
                            if tag in self._c4:
                                self.wait(eng, "sPE", self._c4[tag])
                            self._c4[tag] = f"ym{key}"
                            inst = None
                            if self.on("scalar"):
                                inst = eng.activation(
                                    self.pT_sb[:, pk % 2, :, :],
                                    bass.AP(PSP, 0, [[1024, 128], [512, 2], [1, 256]]),
                                    AF.Exp, scale=np.float32(SC_MAIN),
                                    bias=np.float32(-20.0))
                            self.inc(inst, "sAE", f"pT{key}")
                        if pk == npk - 1 and (self.on("vector") or self.on("plan")):
                            self.wait(eng, "sAE", f"pT{key}")
                            inst = None
                            if self.on("vector"):
                                inst = eng.tensor_mul(
                                    self.pT_sb[:, pk % 2, :, :],
                                    self.pT_sb[:, pk % 2, :, :],
                                    bass.AP(self.trapAB, 0,
                                            [[512, 128], [256, 2], [1, 256]]))
                            self.inc(inst, "sVE", f"mk{key}")
                    if pk >= 1:
                        pky = pk - 1
                        pby = pky % 2
                        ykey = f"{h}_{s}_{pky}"
                        if self.on("tensor") or self.on("plan"):
                            if pky == npk - 1:
                                self.wait(eng, "sVE", f"mk{ykey}")
                            else:
                                self.wait(eng, "sAE", f"pT{ykey}")
                            if pky == 0 and "yb" in self._c4:
                                self.wait(eng, "sVE", self._c4["yb"])
                            inst = None
                            for ci in range(2):
                                kc = 2 * pky + ci
                                for sub in range(2):
                                    if self.on("tensor"):
                                        inst = eng.matmul(
                                            self.PS[(ye, yo)[sub]][:, 0:129],
                                            self.pT_sb[:, pby, ci,
                                                       sub * 128:(sub + 1) * 128],
                                            self.v_sb[:, hb, kc, 0:129],
                                            start=(kc == 0 and True), stop=False,
                                            skip_group_check=True)
                            self.inc(inst, "sPE", f"ym{ykey}")
                if self.on("plan") or self.on("tensor") or True:
                    self._c4["yb"] = f"yev{skey}"
                p7 = slice(0, 256) if s % 2 == 0 else slice(256, 512)
                if self.on("vector") or self.on("plan"):
                    self.wait(eng, "sPE", f"ym{h}_{s}_{npk-1}")
                    prevtr = (f"aptr{h}_{s-1}" if s > 0 else
                              (f"aptr{h-1}_7" if h > 0 else None))
                    if prevtr:
                        self.wait(eng, "sPE", prevtr)
                    inst = None
                    if self.on("vector"):
                        a = self.ascr
                        self.V(eng, eng.tensor_mul(a[:, 2:3],
                                                   self.gra_sb[:, hb, 2 * s:2 * s + 1],
                                                   self.PS[ye][:, 128:129]))
                        self.V(eng, eng.tensor_mul(a[:, 3:4],
                                                   self.gra_sb[:, hb, 2 * s + 1:2 * s + 2],
                                                   self.PS[yo][:, 128:129]))
                        self.V(eng, eng.tensor_scalar_mul(
                            self.ap_sb[:, 0:10],
                            self.ea_sb[:, hb, 20 * s:20 * s + 10], a[:, 2:3]))
                        inst = eng.tensor_scalar_mul(
                            self.ap_sb[:, 10:20],
                            self.ea_sb[:, hb, 20 * s + 10:20 * s + 20], a[:, 3:4])
                    self.inc(inst, "sVE", f"apv{skey}")
                    self.VM(eng, f"apv{skey}")
                if self.on("tensor") or self.on("plan"):
                    self.wait(eng, "sVE", f"apv{skey}")
                    prev = (f"apev{h}_{s-2}" if s >= 2 else
                            (f"apev{h-1}_{s+6}" if h > 0 else None))
                    if prev:
                        self.wait(eng, "sAE", prev)
                    inst = None
                    if self.on("tensor"):
                        eng.transpose(self.PS7[0:10, p7.start:p7.start + 128],
                                      self.ap_sb[:, 0:10], self.eye_bf[:, :])
                        inst = eng.transpose(self.PS7[0:10, p7.start + 128:p7.stop],
                                             self.ap_sb[:, 10:20], self.eye_bf[:, :])
                    self.inc(inst, "sPE", f"aptr{skey}")
                if self.on("scalar") or self.on("plan"):
                    self.wait(eng, "sPE", f"aptr{skey}")
                    prev = (f"yadj{h}_{s-2}" if s >= 2 else
                            (f"yadj{h-1}_{s+6}" if h > 0 else None))
                    if prev:
                        self.wait(eng, "sPE", prev)
                    inst = None
                    if self.on("scalar"):
                        inst = eng.activation(self.apT_sb[0:10, s % 2, :],
                                              self.PS7[0:10, p7], AF.Copy)
                    self.inc(inst, "sAE", f"apev{skey}")
                if self.on("tensor") or self.on("plan"):
                    self.wait(eng, "sAE", f"apev{skey}")
                    inst = None
                    if self.on("tensor"):
                        eng.matmul(self.PS[ye][:, 0:129],
                                   self.apT_sb[0:10, s % 2, 0:128],
                                   self.av_sb[0:10, h, :], start=False, stop=True,
                                   skip_group_check=True)
                        inst = eng.matmul(self.PS[yo][:, 0:129],
                                          self.apT_sb[0:10, s % 2, 128:256],
                                          self.av_sb[0:10, h, :], start=False,
                                          stop=True, skip_group_check=True)
                    self.inc(inst, "sPE", f"yadj{skey}")
                if self.on("vector") or self.on("plan"):
                    self.wait(eng, "sPE", f"yadj{skey}")
                    prev = (f"ysp{h}_{s-2}" if s >= 2 else
                            (f"ysp{h-1}_{s+6}" if h > 0 else None))
                    if prev:
                        self.wait(eng, "sSP", prev)
                    inst = None
                    if self.on("vector"):
                        a = self.ascr
                        self.V(eng, eng.reciprocal(a[:, 4:5], self.PS[ye][:, 128:129]))
                        self.V(eng, eng.reciprocal(a[:, 5:6], self.PS[yo][:, 128:129]))
                        self.V(eng, eng.tensor_scalar_mul(self.ysb[:, s % 2, 0, :],
                                                          self.PS[ye][:, 0:128],
                                                          a[:, 4:5]))
                        inst = eng.tensor_scalar_mul(self.ysb[:, s % 2, 1, :],
                                                     self.PS[yo][:, 0:128], a[:, 5:6])
                    self.inc(inst, "sVE", f"yev{skey}")
                    self.VM(eng, f"yev{skey}")
                if self.on("gpsimd") or self.on("plan"):
                    self.wait(eng, "sVE", f"yev{skey}")
                    inst = None
                    if self.on("gpsimd"):
                        eng.dma_start(
                            out=self.y_dram[s * 256:s * 256 + 128,
                                            h * 128:(h + 1) * 128],
                            in_=self.ysb[:, s % 2, 0, :]).then_inc(self.sems["sSP"], 16)
                        inst = eng.dma_start(
                            out=self.y_dram[s * 256 + 128:(s + 1) * 256,
                                            h * 128:(h + 1) * 128],
                            in_=self.ysb[:, s % 2, 1, :])
                    if self.mode == "plan":
                        self.cnt["sSP"] += 16
                    self.inc(inst, "sSP", f"ysp{skey}", 16)
                if s == 7 and h % 4 == 3 and (self.on("gpsimd")
                                              or self.on("plan")):
                    inst = None
                    if self.on("gpsimd"):
                        if self._slot_rv is None:
                            pid = self.nc.partition_id(
                                engines=[mybir.EngineType.Pool])
                            self._slot_rv = pid % 2
                        inst = eng.dma_start(
                            out=self.sh_y[bass.ds(self._slot_rv, 1), :,
                                          (h - 3) * 128:(h + 1) * 128],
                            in_=self.y_dram[:, (h - 3) * 128:(h + 1) * 128])
                    self.inc(inst, "sSP", f"shy{h}", 16)

    def ph5_gather(self, eng):
        if self.on("gpsimd") or self.on("plan"):
            self.wait(eng, "sSP", f"ysp{HC-1}_7")
            inst = None
            if self.on("gpsimd"):
                slot = self._slot_rv
                other = 1 - slot
                eng.dma_start(out=self.sh_f[bass.ds(slot, 1), :],
                              in_=self.ntl[:, :]).then_inc(self.sem_poll, 16)
                eng.wait_ge(self.sem_poll, 16)
            if self.on("gpsimd"):
                for q in range(4):
                    eng.dma_start(
                        out=self.wp_sb[:, q * 8:(q + 1) * 8, :],
                        in_=self.wprojT_t[0, :, q * 8:(q + 1) * 8, :]
                    ).then_inc(self.sems["sSP"], 16)
                with (eng.register("pf") as pf, eng.register("nr") as nr,
                      eng.register("df") as df, eng.register("pc") as pc):
                    eng.reg_load(nr, self.ntl[0:1, 0:1])
                    eng.reg_mov(df, 1)
                    eng.reg_mov(pc, 16)
                    with eng.While(df):
                        eng.reg_add(pc, pc, 16)
                        eng.dma_start(
                            out=self.ptl[:, :],
                            in_=self.sh_f[bass.ds(other, 1), :]
                        ).then_inc(self.sem_poll, 16)
                        eng.wait_ge(self.sem_poll, pc)
                        eng.reg_load(pf, self.ptl[0:1, 0:1])
                        eng.reg_sub(df, pf, nr)
                inst = eng.nop()
            if self.on("plan"):
                for q in range(4):
                    self.cnt["sSP"] += 16
                    self.M[f"wp0_{q}"] = self.cnt["sSP"]
            self.inc(inst, "sCC", "ag", 16)
        if self.on("sync"):
            eng.wait_ge(self.sems["sCC"], self.M["ag"])

    def ph6_proj(self, eng):
        for og in range(2):
            if og == 1 and (self.on("gpsimd") or self.on("plan")):
                self.wait(eng, "sPE", f"pj0_{NT-1}_1")
                for q in range(4):
                    inst = None
                    if self.on("gpsimd"):
                        inst = eng.dma_start(
                            out=self.wp_sb[:, q * 8:(q + 1) * 8, :],
                            in_=self.wprojT_t[1, :, q * 8:(q + 1) * 8, :])
                    self.inc(inst, "sSP", f"wp1_{q}", 16)
            def yt_loads(tp2, og2):
                tb2 = tp2 % 2
                prev = None
                if tp2 >= 2:
                    prev = (og2, 2 * tp2 - 3)
                elif og2 == 1:
                    prev = (0, NT - 4 + 2 * tp2 + 1)
                for qn, sem in (("sync", "sLD"),):
                    if not (self.on(qn) or self.on("plan")):
                        continue
                    if prev and self.on(qn):
                        self.wait(eng, "sPE", f"pj{prev[0]}_{prev[1]}_1")
                    for cc in range(NCC):
                        blk = 0 if cc < 16 else 1
                        coff = (cc % 16) * 128
                        inst = None
                        if self.on(qn):
                            inst = eng.dma_start(
                                out=self.yT_sb[:, tb2, cc, :],
                                in_=self.sh_y[blk, tp2 * 256:(tp2 + 1) * 256,
                                              coff:coff + 128],
                                transpose=True)
                        self.inc(inst, sem, f"yT{og2}_{tp2}_{cc}", 16)

            if og == 0:
                yt_loads(0, 0)
            for tp in range(NT // 2):
                tb = tp % 2
                if tp + 1 < NT // 2:
                    yt_loads(tp + 1, og)
                elif og == 0:
                    yt_loads(0, 1)
                for ti in range(2):
                    tt = 2 * tp + ti
                    rows = slice(tt * 128, (tt + 1) * 128)
                    if self.on("scalar") or self.on("plan"):
                        for ow in range(2):
                            cols = slice(og * 1024 + ow * 512,
                                         og * 1024 + (ow + 1) * 512)
                            if tt >= 2 or og == 1:
                                po, pt = (og, tt - 2) if tt >= 2 else (0, NT - 2 + tt)
                                self.wait(eng, "sOUT", f"os{po}_{pt}_{ow}")
                            inst = None
                            if self.on("scalar"):
                                inst = eng.dma_start(out=self.gx_sb[:, tt % 2, 0, ow, :],
                                                     in_=self.g_dram[rows, cols])
                            self.inc(inst, "sXG", f"g{og}_{tt}_{ow}", 16)
                            inst = None
                            if self.on("scalar"):
                                inst = eng.dma_start(out=self.gx_sb[:, tt % 2, 1, ow, :],
                                                     in_=self.xc[rows, cols])
                            self.inc(inst, "sXG", f"xc{og}_{tt}_{ow}", 16)
                    for ow in range(2):
                        ps = self.PS[2 * (tt % 2) + ow]
                        if self.on("tensor") or self.on("plan"):
                            if ow == 0 and ti == 0:
                                self.wait(eng, "sLD", f"yT{og}_{tp}_{NCC-1}")
                                self.wait(eng, "sSP", f"wp{og}_3")
                            if tt >= 2 or og == 1:
                                po, pt = (og, tt - 2) if tt >= 2 else (0, NT - 2 + tt)
                                self.wait(eng, "sVE", f"cmb{po}_{pt}_{ow}")
                            inst = None
                            for cc in range(NCC):
                                if self.on("tensor"):
                                    inst = eng.matmul(
                                        ps[:, :], self.yT_sb[:, tb, cc,
                                                             ti * 128:(ti + 1) * 128],
                                        self.wp_sb[:, cc, ow * 512:(ow + 1) * 512],
                                        start=(cc == 0), stop=(cc == NCC - 1))
                            self.inc(inst, "sPE", f"pj{og}_{tt}_{ow}")
                        if self.on("vector") or self.on("plan"):
                            self.wait(eng, "sPE", f"pj{og}_{tt}_{ow}")
                            self.wait(eng, "sXG", f"g{og}_{tt}_{ow}")
                            self.wait(eng, "sXG", f"xc{og}_{tt}_{ow}")
                            inst = None
                            if self.on("vector"):
                                self.V(eng, eng.tensor_mul(
                                    self.t6scr[:, :], ps[:, :],
                                    self.gx_sb[:, tt % 2, 0, ow, :]))
                                inst = eng.tensor_add(self.gx_sb[:, tt % 2, 1, ow, :],
                                                      self.t6scr[:, :],
                                                      self.gx_sb[:, tt % 2, 1, ow, :])
                            self.inc(inst, "sVE", f"cmb{og}_{tt}_{ow}")
                            self.VM(eng, f"cmb{og}_{tt}_{ow}")
                        if self.on("gpsimd") or self.on("plan"):
                            self.wait(eng, "sVE", f"cmb{og}_{tt}_{ow}")
                            cols = slice(og * 1024 + ow * 512,
                                         og * 1024 + (ow + 1) * 512)
                            inst = None
                            if self.on("gpsimd"):
                                inst = eng.dma_start(out=self.out[rows, cols],
                                                     in_=self.gx_sb[:, tt % 2, 1, ow, :])
                            if self.mode == "plan":
                                self.cnt["sOUT"] += 16
                                self.M[f"os{og}_{tt}_{ow}"] = self.cnt["sOUT"]
                            elif inst is not None:
                                inst.then_inc(self.sems["sOUT"], 16)
        final = f"os1_{NT-1}_1"
        if self.on("sync"):
            eng.wait_ge(self.sems["sOUT"], self.M[final])
        if self.on("gpsimd"):
            eng.wait_ge(self.sems["sOUT"], self.M[final])


_NC_CACHE = None


def _build():
    global _NC_CACHE
    if _NC_CACHE is None:
        _NC_CACHE = Prog().build()
    return _NC_CACHE


def kernel(L0_attn_x, L0_attn_w_attn, L0_attn_w_proj, L0_attn_adapter_wte,
           L0_attn_gating, L0_ffn_W_res, L0_ffn_W_pre, L0_ffn_W_post,
           L0_ffn_H_res_base, L0_ffn_H_pre_base, L0_ffn_H_post_base,
           L0_ffn_gate, L0_ffn_W_branch):
    x = np.asarray(L0_attn_x, dtype=np.float32)
    w_attn = np.asarray(L0_attn_w_attn, dtype=np.float32)
    w_proj = np.asarray(L0_attn_w_proj, dtype=np.float32)
    wte = np.asarray(L0_attn_adapter_wte, dtype=np.float32)
    gating = np.ascontiguousarray(
        np.asarray(L0_attn_gating, dtype=np.float32).reshape(1, 1))
    W_res = np.asarray(L0_ffn_W_res, dtype=np.float32)
    W_pre = np.asarray(L0_ffn_W_pre, dtype=np.float32)
    W_post = np.asarray(L0_ffn_W_post, dtype=np.float32)
    Hres_b = np.asarray(L0_ffn_H_res_base, dtype=np.float32)
    Hpre_b = np.asarray(L0_ffn_H_pre_base, dtype=np.float32)
    Hpost_b = np.asarray(L0_ffn_H_post_base, dtype=np.float32)
    gate = np.ascontiguousarray(
        np.asarray(L0_ffn_gate, dtype=np.float32).reshape(1, 1))
    W_br = np.asarray(L0_ffn_W_branch, dtype=np.float32)

    nc = _build()
    nonce_arr = np.full((1, 64), np.random.randint(1, 2**31 - 1), np.int32)
    wbranch_t = np.ascontiguousarray(W_br.reshape(8, 128, 1024).transpose(1, 0, 2))
    wteT_t = np.ascontiguousarray(wte.reshape(10, 32, 128).transpose(2, 1, 0))
    in_maps = []
    for core in range(8):
        b, j = core >> 1, core & 1
        perm = np.concatenate([
            np.r_[(2 * i) * 128:(2 * i) * 128 + 64,
                  (2 * i + 1) * 128:(2 * i + 1) * 128 + 64,
                  (2 * i) * 128 + 64:(2 * i) * 128 + 128,
                  (2 * i + 1) * 128 + 64:(2 * i + 1) * 128 + 128]
            for i in range(8)])
        rows = [w_attn[g * C + 2048 * j: g * C + 2048 * (j + 1)] for g in range(3)]
        w_slice = np.concatenate([rows[0][perm], rows[1][perm], rows[2]], axis=0)
        wqkvT_t = np.ascontiguousarray(
            w_slice.reshape(48, 128, 32, 128).transpose(0, 3, 2, 1))
        wp_rows = w_proj[2048 * j: 2048 * (j + 1)]
        wprojT_t = np.ascontiguousarray(
            wp_rows.reshape(2, 1024, 32, 128).transpose(0, 3, 2, 1))
        iorder = [0, 1, 2, 3] if j == 0 else [2, 3, 0, 1]
        W_all = np.concatenate([
            W_res.reshape(4, 4, C)[iorder].reshape(16, C),
            W_pre, W_post[iorder]], axis=0)
        wffnT_t = np.ascontiguousarray(W_all.reshape(24, 32, 128).transpose(2, 1, 0))
        hbase = np.ascontiguousarray(np.concatenate(
            [Hres_b[iorder].reshape(16), Hpre_b, Hpost_b[iorder]]).reshape(1, 24))
        eyeP = np.ascontiguousarray(
            np.eye(4, dtype=np.float32)[iorder].reshape(1, 16))
        in_maps.append({
            "nonce": nonce_arr,
            "x_full": np.ascontiguousarray(x[b]),
            "xc": np.ascontiguousarray(x[b][:, 2048 * j: 2048 * (j + 1)]),
            "wqkvT_t": wqkvT_t,
            "wprojT_t": wprojT_t,
            "wbranch_t": wbranch_t,
            "wffnT_t": wffnT_t,
            "wteT_t": wteT_t,
            "hbase": hbase,
            "eyeP": eyeP,
            "gating": gating,
            "gate": gate,
        })
    res = run_bass_kernel_spmd(nc, in_maps, core_ids=list(range(8)))
    out = np.empty((B, T, C), dtype=np.float32)
    for core in range(8):
        b, j = core >> 1, core & 1
        out[b][:, 2048 * j: 2048 * (j + 1)] = res.results[core]["out"]
    return out
